# version 03
# baseline (speedup 1.0000x reference)
import contextlib
import numpy as np
import ml_dtypes
import concourse.bass as bass
import concourse.mybir as mybir
from concourse.bass_utils import run_bass_kernel_spmd

F32 = mybir.dt.float32
BF16 = mybir.dt.bfloat16
AF = mybir.ActivationFunctionType
ALU = mybir.AluOpType
AX = mybir.AxisListType

NCORE = 8
BATCH = 2
LTOT = 4096
D = 4096
TS = LTOT // 4
DQ = D // 4
NDT = D // 128
NTT = TS // 128
DTR = 128
NST = 16
NEXP = 8
RK = 64
NH = 32
HD = 128
QL = 12
KVL = 4


def build_L1(upto=99, dbg=False):
    nc = bass.Bass("TRN2", target_bir_lowering=False)
    x_d = nc.dram_tensor("x", [TS, D], F32, kind="ExternalInput")
    xpw_d = nc.dram_tensor("xpw", [128, NDT * 160], F32, kind="ExternalInput")
    dtpw_d = nc.dram_tensor("dtpw", [DTR, D], F32, kind="ExternalInput")
    dtpb_d = nc.dram_tensor("dtpb", [NDT, 128], F32, kind="ExternalInput")
    dvec_d = nc.dram_tensor("dvec", [NDT, 128], F32, kind="ExternalInput")
    gw_d = nc.dram_tensor("gw", [128, NDT * NEXP], F32, kind="ExternalInput")
    lA_d = nc.dram_tensor("lA", [128, NEXP * NDT * RK], F32, kind="ExternalInput")
    lB_d = nc.dram_tensor("lB", [128, NEXP * D // 2], F32, kind="ExternalInput")
    idm_d = nc.dram_tensor("idm", [128, 128], F32, kind="ExternalInput")

    dt_o = nc.dram_tensor("dt_o", [D, TS], F32, kind="ExternalOutput")
    u_o = nc.dram_tensor("u_o", [D, TS], BF16, kind="ExternalOutput")
    bc_o = nc.dram_tensor("bc_o", [32, TS], F32, kind="ExternalOutput")
    md_o = nc.dram_tensor("md_o", [D, TS], BF16, kind="ExternalOutput")
    w_dram = nc.dram_tensor("w_scr", [NEXP, TS], F32, kind="Internal")
    if dbg:
        xn_o = nc.dram_tensor("xn_o", [128, NDT * TS], BF16, kind="ExternalOutput")
        gp_o = nc.dram_tensor("gp_o", [NEXP, TS], F32, kind="ExternalOutput")
        w_o = nc.dram_tensor("w_o", [NEXP, TS], F32, kind="ExternalOutput")
        t1_o = nc.dram_tensor("t1_o", [128, NEXP * TS // 2], BF16, kind="ExternalOutput")
        cols_o = nc.dram_tensor("cols_o", [128, 8 * NTT], F32, kind="ExternalOutput")

    xnT = nc.alloc_sbuf_tensor("xnT", [128, NDT * TS], BF16)
    xpw_s = nc.alloc_sbuf_tensor("xpw_s", [128, NDT * 160], BF16)
    dtpw_s = nc.alloc_sbuf_tensor("dtpw_s", [128, D], BF16)
    gw_s = nc.alloc_sbuf_tensor("gw_s", [128, NDT * NEXP], BF16)
    id_b = nc.alloc_sbuf_tensor("id_b", [128, 128], BF16)
    id_f = nc.alloc_sbuf_tensor("id_f", [128, 128], F32)
    cols = nc.alloc_sbuf_tensor("cols", [128, 8], F32)
    colsnap = nc.alloc_sbuf_tensor("colsnap", [128, 8 * NTT], F32)
    dtpb_r = nc.alloc_sbuf_tensor("dtpb_r", [NDT, 128], F32)
    dvec_r = nc.alloc_sbuf_tensor("dvec_r", [NDT, 128], F32)
    bcol_s = nc.alloc_sbuf_tensor("bcol_s", [128, NDT], F32)
    dcol_s = nc.alloc_sbuf_tensor("dcol_s", [128, NDT], F32)
    dt_in_s = nc.alloc_sbuf_tensor("dt_in_s", [128, TS], BF16)
    bcT_s = nc.alloc_sbuf_tensor("bcT_s", [32, TS], F32)
    g_sb = nc.alloc_sbuf_tensor("g_sb", [NEXP, TS], F32)
    gt_sb = nc.alloc_sbuf_tensor("gt_sb", [128, NEXP], F32)
    ge_sb = nc.alloc_sbuf_tensor("ge_sb", [128, NEXP], F32)
    ge2_sb = nc.alloc_sbuf_tensor("ge2_sb", [128, NEXP], F32)
    gw_tm = nc.alloc_sbuf_tensor("gw_tm", [128, NEXP], F32)
    gcol = nc.alloc_sbuf_tensor("gcol", [128, 8], F32)
    wT_s = nc.alloc_sbuf_tensor("wT_s", [NEXP, TS], F32)
    md_t = [nc.alloc_sbuf_tensor(f"md_t{i}", [128, TS], BF16) for i in range(2)]

    P0 = nc.alloc_psum_tensor("P0", [128, 1024], F32)
    P1 = nc.alloc_psum_tensor("P1", [128, 1024], F32)
    P2 = nc.alloc_psum_tensor("P2", [40, 1024], F32)
    P3 = nc.alloc_psum_tensor("P3", [128, 1024], BF16)
    P4 = nc.alloc_psum_tensor("P4", [128, 1024], BF16)
    p_tr = [P3, P4]

    s = {}
    for name in ["wld", "wld2", "xld", "c1", "c2", "xn", "tr", "trd", "g",
                 "gtr", "gtrd", "wrt", "mm1", "ev1", "mm3", "ev3", "mm4",
                 "ev4", "st4", "mm5", "ev5a", "ev5b", "act5", "st5", "stout"]:
        s[name] = nc.alloc_semaphore("s_" + name)

    with nc.Block() as blk:
        @blk.sync
        def _(sync):
            sync.dma_start(dtpb_r[:, :], dtpb_d[:, :]).then_inc(s["wld"], 16)
            sync.dma_start(dvec_r[:, :], dvec_d[:, :]).then_inc(s["wld"], 16)

        @blk.gpsimd
        def _(g):
            g.dma_start(id_b[:, :], idm_d[:, :]).then_inc(s["wld"], 16)
            g.dma_start(id_f[:, :], idm_d[:, :]).then_inc(s["wld"], 16)
            g.dma_start(xpw_s[:, :], xpw_d[:, :]).then_inc(s["wld"], 16)
            g.dma_start(dtpw_s[:, :], dtpw_d[:, :]).then_inc(s["wld"], 16)
            g.dma_start(gw_s[:, :], gw_d[:, :]).then_inc(s["wld"], 16)

    with nc.Block() as blk:
        @blk.tensor
        def _(t):
            t.wait_ge(s["wld"], 16 * 7)
            t.transpose(P0[:, 0:32], dtpb_r[:, :], id_f[0:32, 0:32])
            t.transpose(P0[:, 32:64], dvec_r[:, :], id_f[0:32, 0:32]).then_inc(s["tr"], 1)

        @blk.vector
        def _(v):
            v.wait_ge(s["tr"], 1)
            v.tensor_copy(bcol_s[:, :], P0[:, 0:32])
            v.tensor_copy(dcol_s[:, :], P0[:, 32:64]).then_inc(s["trd"], 1)

    if upto <= 0:
        return nc
    with contextlib.ExitStack() as stk:
        x_t = [stk.enter_context(nc.sbuf_tensor(f"x_t{i}", [128, D], F32)) for i in range(2)]
        xsq = stk.enter_context(nc.sbuf_tensor("xsq", [128, D], BF16))
        xn_t = [stk.enter_context(nc.sbuf_tensor(f"xn_t{i}", [128, D], BF16)) for i in range(2)]

        with nc.Block() as blk:
            @blk.sync
            def _(sync):
                for i in range(NTT):
                    if i >= 2:
                        sync.wait_ge(s["xn"], i - 1)
                    sync.dma_start(x_t[i % 2][:, :], x_d[i * 128:(i + 1) * 128, :]
                                   ).then_inc(s["xld"], 16)

            @blk.vector
            def _(v):
                for i in range(NTT):
                    v.wait_ge(s["c2"], 2 * i + 1)
                    v.tensor_scalar(cols[:, 1:2], cols[:, 0:1], 1.0 / D, None, ALU.mult)
                    v.tensor_scalar(cols[:, 3:4], cols[:, 2:3], 1.0 / D, None, ALU.mult)
                    v.drain()
                    v.tensor_mul(cols[:, 4:5], cols[:, 1:2], cols[:, 1:2])
                    v.drain()
                    v.tensor_sub(cols[:, 5:6], cols[:, 3:4], cols[:, 4:5])
                    v.drain()
                    v.tensor_scalar(cols[:, 5:6], cols[:, 5:6], 1e-5, None, ALU.add
                                    ).then_inc(s["c1"], 1)
                    xt = x_t[i % 2]
                    v.wait_ge(s["c2"], 2 * i + 2)
                    if i >= 1:
                        v.wait_ge(s["gtr"], i * NDT)
                    v.tensor_scalar(xn_t[i % 2][:, :], xt[:, :], cols[:, 1:2],
                                    cols[:, 7:8], ALU.subtract, ALU.mult
                                    ).then_inc(s["xn"], 1)
                    if dbg:
                        v.tensor_copy(colsnap[:, i * 8:(i + 1) * 8], cols[:, :])
                    for c in range(1, NDT, 2):
                        v.wait_ge(s["gtr"], i * NDT + c + 1)
                        v.tensor_copy(xnT[:, c * TS + i * 128: c * TS + (i + 1) * 128],
                                      p_tr[c % 2][:, 0:128]).then_inc(s["gtrd"], 1)

            @blk.scalar
            def _(sc):
                for i in range(NTT):
                    sc.wait_ge(s["xld"], 16 * (i + 1))
                    sc.activation(xsq[:, :], x_t[i % 2][:, :], AF.Copy,
                                  accum_out=cols[:, 0:1])
                    sc.activation(xsq[:, :], x_t[i % 2][:, :], AF.Square,
                                  accum_out=cols[:, 2:3]).then_inc(s["c2"], 1)
                    sc.wait_ge(s["c1"], i + 1)
                    sc.activation(cols[:, 6:7], cols[:, 5:6], AF.Ln)
                    sc.drain()
                    sc.activation(cols[:, 7:8], cols[:, 6:7], AF.Exp, scale=-0.5
                                  ).then_inc(s["c2"], 1)
                    for c in range(0, NDT, 2):
                        sc.wait_ge(s["gtr"], i * NDT + c + 1)
                        sc.copy(xnT[:, c * TS + i * 128: c * TS + (i + 1) * 128],
                                p_tr[c % 2][:, 0:128]).then_inc(s["gtrd"], 1)

            @blk.tensor
            def _(t):
                for i in range(NTT):
                    t.wait_ge(s["xn"], i + 1)
                    for c in range(NDT):
                        idx = i * NDT + c
                        if idx >= 2:
                            t.wait_ge(s["gtrd"], idx - 1)
                        t.transpose(p_tr[c % 2][:, 0:128],
                                    xn_t[i % 2][:, c * 128:(c + 1) * 128],
                                    id_b[:, :]).then_inc(s["gtr"], 1)

    if upto <= 1:
        return nc
    with nc.Block() as blk:
        @blk.tensor
        def _(t):
            for c in range(NDT):
                st_ = (c == 0)
                sp = (c == NDT - 1)
                rhs = xnT[:, c * TS:(c + 1) * TS]
                for li, (lhs, pp) in enumerate((
                        (xpw_s[:, c * 160:c * 160 + 128], P0),
                        (xpw_s[:, c * 160 + 128:(c + 1) * 160], P2[0:32, :]),
                        (gw_s[:, c * NEXP:(c + 1) * NEXP], P1[0:8, :]))):
                    for hf in range(2):
                        mm = t.matmul(pp[:, hf * 512:(hf + 1) * 512],
                                      lhs, rhs[:, hf * 512:(hf + 1) * 512],
                                      start=st_, stop=sp)
                        if c == NDT - 1 and li == 2 and hf == 1:
                            mm.then_inc(s["mm1"], 1)

        @blk.vector
        def _(v):
            v.wait_ge(s["mm1"], 1)
            v.tensor_copy(dt_in_s[:, :], P0[:, :])
            v.tensor_copy(bcT_s[:, :], P2[0:32, :])
            v.tensor_copy(g_sb[:, :], P1[0:8, :]).then_inc(s["ev1"], 1)

        @blk.sync
        def _(sync):
            sync.wait_ge(s["ev1"], 1)
            sync.dma_start(bc_o[:, :], bcT_s[:, :]).then_inc(s["stout"], 16)
            if dbg:
                sync.dma_start(xn_o[:, :], xnT[:, :]).then_inc(s["stout"], 16)
                sync.dma_start(cols_o[:, :], colsnap[:, :]).then_inc(s["stout"], 16)
                sync.dma_start(gp_o[:, :], g_sb[:, :]).then_inc(s["stout"], 16)

    if upto <= 2:
        return nc
    with contextlib.ExitStack() as stk:
        lA_s = stk.enter_context(nc.sbuf_tensor("lA_s", [128, NEXP * NDT * RK], BF16))
        lB_s = stk.enter_context(nc.sbuf_tensor("lB_s", [128, NEXP * D // 2], BF16))
        w_bc = stk.enter_context(nc.sbuf_tensor("w_bc", [128, NEXP * TS // 2], BF16))
        t1w_s = stk.enter_context(nc.sbuf_tensor("t1w_s", [128, NEXP * TS // 2], BF16))

        def lBsl(e, c):
            return lB_s[64 * (e % 2):64 * (e % 2) + 64,
                        (e // 2) * D + c * 128:(e // 2) * D + (c + 1) * 128]

        def wbsl(e):
            return w_bc[64 * (e % 2):64 * (e % 2) + 64, (e // 2) * TS:(e // 2 + 1) * TS]

        def t1sl(e):
            return t1w_s[64 * (e % 2):64 * (e % 2) + 64, (e // 2) * TS:(e // 2 + 1) * TS]

        with nc.Block() as blk:
            @blk.gpsimd
            def _(g):
                g.dma_start(lA_s[:, :], lA_d[:, :]).then_inc(s["wld2"], 16)
                g.dma_start(lB_s[:, :], lB_d[:, :]).then_inc(s["wld2"], 16)

            @blk.tensor
            def _(t):
                for j in range(NTT):
                    if j >= 2:
                        t.wait_ge(s["g"], 7 * (j - 1))
                    bank = P1[:, 0:512] if j % 2 == 0 else P1[:, 512:1024]
                    t.transpose(bank[:, 0:8], g_sb[:, j * 128:(j + 1) * 128],
                                id_f[0:8, 0:8]).then_inc(s["tr"], 1)
                    t.wait_ge(s["g"], 7 * j + 6)
                    t.transpose(bank[0:8, 256:384], gw_tm[:, :], id_f[:, :]
                                ).then_inc(s["tr"], 1)

            @blk.vector
            def _(v):
                tr0 = 1
                for j in range(NTT):
                    bank = P1[:, 0:512] if j % 2 == 0 else P1[:, 512:1024]
                    v.wait_ge(s["tr"], tr0 + 2 * j + 1)
                    v.tensor_copy(gt_sb[:, :], bank[:, 0:8])
                    v.drain()
                    v.sem_inc(s["g"], 1)
                    v.reduce_max(gcol[:, 0:1], gt_sb[:, :], axis=AX.X, negate=True
                                 ).then_inc(s["g"], 1)
                    v.wait_ge(s["g"], 7 * j + 3)
                    v.max(ge2_sb[:, :], ge_sb[:, :]).then_inc(s["g"], 1)
                    v.wait_ge(s["g"], 7 * j + 4)
                    v.tensor_scalar(gw_tm[:, :], ge_sb[:, :], ge2_sb[:, 1:2], None,
                                    ALU.is_ge)
                    v.drain()
                    v.tensor_mul(gw_tm[:, :], gw_tm[:, :], ge_sb[:, :])
                    v.tensor_add(gcol[:, 3:4], ge2_sb[:, 0:1], ge2_sb[:, 1:2])
                    v.drain()
                    v.reciprocal(gcol[:, 4:5], gcol[:, 3:4]).then_inc(s["g"], 1)
                    v.wait_ge(s["g"], 7 * j + 5)
                    v.tensor_scalar(gw_tm[:, :], gw_tm[:, :], gcol[:, 4:5], None,
                                    ALU.mult).then_inc(s["g"], 1)
                    v.wait_ge(s["tr"], tr0 + 2 * j + 2)
                    v.tensor_copy(wT_s[:, j * 128:(j + 1) * 128], bank[0:8, 256:384]
                                  ).then_inc(s["g"], 1)

            @blk.scalar
            def _(sc):
                for j in range(NTT):
                    sc.wait_ge(s["g"], 7 * j + 2)
                    sc.activation(ge_sb[:, :], gt_sb[:, :], AF.Exp, bias=gcol[:, 0:1]
                                  ).then_inc(s["g"], 1)

        if upto <= 3:
            return nc
        with nc.Block() as blk:
            @blk.sync
            def _(sync):
                sync.dma_start(w_dram[:, :], wT_s[:, :]).then_inc(s["wrt"], 16)
                if dbg:
                    sync.dma_start(w_o[:, :], wT_s[:, :]).then_inc(s["stout"], 16)

            @blk.gpsimd
            def _(g):
                g.wait_ge(s["wrt"], 16)
                for e in range(NEXP):
                    src = bass.AP(w_dram, e * TS, [[0, 64], [1, TS]])
                    g.dma_start(wbsl(e), src).then_inc(s["wrt"], 16)

        if upto <= 4:
            return nc
        with nc.Block() as blk:
            @blk.tensor
            def _(t):
                t.wait_ge(s["wrt"], 16 * (1 + NEXP))
                t.wait_ge(s["wld2"], 16)
                for e in range(NEXP):
                    pt = P0 if e % 2 == 0 else P1
                    if e >= 2:
                        t.wait_ge(s["ev3"], e - 1)
                    for c in range(NDT):
                        lhs = lA_s[:, (e * NDT + c) * RK:(e * NDT + c + 1) * RK]
                        for hf in range(2):
                            mm = t.matmul(pt[0:64, hf * 512:(hf + 1) * 512], lhs,
                                          xnT[:, c * TS + hf * 512:c * TS + (hf + 1) * 512],
                                          start=(c == 0), stop=(c == NDT - 1))
                            if c == NDT - 1 and hf == 1:
                                mm.then_inc(s["mm3"], 1)

            @blk.vector
            def _(v):
                for e in range(NEXP):
                    v.wait_ge(s["mm3"], e + 1)
                    pt = P0 if e % 2 == 0 else P1
                    v.tensor_mul(t1sl(e), pt[0:64, :], wbsl(e)).then_inc(s["ev3"], 1)

        if upto <= 5:
            return nc
        if dbg:
            with nc.Block() as blk:
                @blk.sync
                def _(sync):
                    sync.dma_start(t1_o[:, :], t1w_s[:, :]).then_inc(s["stout"], 16)

        with nc.Block() as blk:
            @blk.tensor
            def _(t):
                t.wait_ge(s["wld2"], 32)
                for c in range(NDT):
                    pt = P0 if c % 2 == 0 else P1
                    if c >= 2:
                        t.wait_ge(s["ev4"], c - 1)
                    for p in range(NEXP // 2):
                        lhs = lB_s[:, p * D + c * 128:p * D + (c + 1) * 128]
                        rhs_p = t1w_s[:, p * TS:(p + 1) * TS]
                        for hf in range(2):
                            mm = t.matmul(pt[:, hf * 512:(hf + 1) * 512], lhs,
                                          rhs_p[:, hf * 512:(hf + 1) * 512],
                                          start=(p == 0), stop=(p == NEXP // 2 - 1))
                            if p == NEXP // 2 - 1 and hf == 1:
                                mm.then_inc(s["mm4"], 1)

            @blk.vector
            def _(v):
                for c in range(NDT):
                    v.wait_ge(s["mm4"], c + 1)
                    if c >= 2:
                        v.wait_ge(s["st4"], 16 * (c - 1))
                    pt = P0 if c % 2 == 0 else P1
                    v.scalar_tensor_tensor(md_t[c % 2][:, :],
                                           xnT[:, c * TS:(c + 1) * TS],
                                           dcol_s[:, c:c + 1], pt[:, :],
                                           ALU.mult, ALU.add).then_inc(s["ev4"], 1)

            @blk.sync
            def _(sync):
                for c in range(NDT):
                    sync.wait_ge(s["ev4"], c + 1)
                    sync.dma_start(md_o[c * 128:(c + 1) * 128, :], md_t[c % 2][:, :]
                                   ).then_inc(s["st4"], 16)

    if upto <= 6:
        return nc
    with contextlib.ExitStack() as stk5:
        zb_abs = [stk5.enter_context(nc.sbuf_tensor(f"zb_abs{i}", [128, TS], F32)) for i in range(2)]
        zb_rel = [stk5.enter_context(nc.sbuf_tensor(f"zb_rel{i}", [128, TS], F32)) for i in range(2)]
        e_t = stk5.enter_context(nc.sbuf_tensor("e_t", [128, TS], F32))
        ee_t = stk5.enter_context(nc.sbuf_tensor("ee_t", [128, TS], F32))
        l_t = [stk5.enter_context(nc.sbuf_tensor(f"l_t{i}", [128, TS], F32)) for i in range(2)]
        dt_t = [stk5.enter_context(nc.sbuf_tensor(f"dt_t{i}", [128, TS], F32)) for i in range(2)]
        u_t = [stk5.enter_context(nc.sbuf_tensor(f"u_t{i}", [128, TS], BF16)) for i in range(2)]

        with nc.Block() as blk:
            @blk.tensor
            def _(t):
                for c in range(NDT):
                    pt = P0 if c % 2 == 0 else P1
                    if c >= 2:
                        t.wait_ge(s["ev5a"], c - 1)
                    lhs = dtpw_s[:, c * 128:(c + 1) * 128]
                    for hf in range(2):
                        mm = t.matmul(pt[:, hf * 512:(hf + 1) * 512], lhs,
                                      dt_in_s[:, hf * 512:(hf + 1) * 512],
                                      start=True, stop=True)
                        if hf == 1:
                            mm.then_inc(s["mm5"], 1)

            @blk.vector
            def _(v):
                for c in range(NDT):
                    v.wait_ge(s["mm5"], c + 1)
                    if c >= 2:
                        v.wait_ge(s["st5"], 32 * (c - 1))
                    pt = P0 if c % 2 == 0 else P1
                    v.tensor_scalar(zb_abs[c % 2][:, :], pt[:, :],
                                    bcol_s[:, c:c + 1], None, ALU.add)
                    v.tensor_scalar_max(zb_rel[c % 2][:, :], zb_abs[c % 2][:, :], 0.0
                                        ).then_inc(s["ev5a"], 1)
                    v.wait_ge(s["act5"], c + 1)
                    v.tensor_add(dt_t[c % 2][:, :], zb_rel[c % 2][:, :],
                                 l_t[c % 2][:, :])
                    v.tensor_mul(u_t[c % 2][:, :], dt_t[c % 2][:, :],
                                 xnT[:, c * TS:(c + 1) * TS]).then_inc(s["ev5b"], 1)

            @blk.scalar
            def _(sc):
                for c in range(NDT):
                    sc.wait_ge(s["ev5a"], c + 1)
                    sc.activation(e_t[:, :], zb_abs[c % 2][:, :], AF.Abs)
                    sc.activation(ee_t[:, :], e_t[:, :], AF.Exp, scale=-1.0)
                    sc.activation(l_t[c % 2][:, :], ee_t[:, :], AF.Ln, bias=1.0
                                  ).then_inc(s["act5"], 1)

            @blk.sync
            def _(sync):
                for c in range(NDT):
                    sync.wait_ge(s["ev5b"], c + 1)
                    sync.dma_start(dt_o[c * 128:(c + 1) * 128, :], dt_t[c % 2][:, :]
                                   ).then_inc(s["st5"], 16)
                    sync.dma_start(u_o[c * 128:(c + 1) * 128, :], u_t[c % 2][:, :]
                                   ).then_inc(s["st5"], 16)


    with nc.Block() as blk:
        @blk.sync
        def _(sync):
            sync.wait_ge(s["stout"], 16)
            sync.wait_ge(s["st4"], 16 * NDT)
            sync.wait_ge(s["st5"], 32 * NDT)

    return nc


def host_L1_inputs(inp, core):
    b = core // 4
    ts = (core % 4) * TS
    f = np.ascontiguousarray
    A = inp["L0_ffn_lora_A"]
    Bm = inp["L0_ffn_lora_B"]
    return dict(
        x=f(inp["L0_attn_x"][b, ts:ts + TS, :]),
        xpw=f(inp["L0_attn_x_proj_w"].reshape(160, NDT, 128).transpose(2, 1, 0).reshape(128, -1)),
        dtpw=f(inp["L0_attn_dt_proj_w"].T),
        dtpb=f(inp["L0_attn_dt_proj_b"].reshape(NDT, 128)),
        dvec=f(inp["L0_attn_D"].reshape(NDT, 128)),
        gw=f(inp["L0_ffn_gate_w"].reshape(NEXP, NDT, 128).transpose(2, 1, 0).reshape(128, -1)),
        lA=f(A.reshape(NEXP, RK, NDT, 128).transpose(3, 0, 2, 1).reshape(128, -1)),
        lB=f(Bm.reshape(NEXP // 2, 2, D, RK).transpose(1, 3, 0, 2).reshape(128, -1)),
        idm=np.eye(128, dtype=np.float32),
    )


def _f(a):
    return np.ascontiguousarray(a)


def kernel(**inp):
    inp = {k: np.asarray(v) for k, v in inp.items()}
    f = _f
    ident = np.eye(128, dtype=np.float32)
    ones = np.ones((1, TS), np.float32)

    nc1 = build_L1()
    maps1 = [host_L1_inputs(inp, c) for c in range(NCORE)]
    r1 = run_bass_kernel_spmd(nc1, maps1, core_ids=list(range(NCORE))).results

    dt_g = [np.hstack([r1[4 * b + j]["dt_o"] for j in range(4)]) for b in range(2)]
    u_g = [np.hstack([r1[4 * b + j]["u_o"] for j in range(4)]) for b in range(2)]
    bc_g = [np.hstack([r1[4 * b + j]["bc_o"] for j in range(4)]) for b in range(2)]
    md_g = [np.hstack([r1[4 * b + j]["md_o"] for j in range(4)]) for b in range(2)]
    nc2 = build_L2()
    maps2 = []
    for c in range(NCORE):
        b, q = c // 4, c % 4
        maps2.append(dict(
            dt=f(dt_g[b][q * DQ:(q + 1) * DQ]),
            u=f(u_g[b][q * DQ:(q + 1) * DQ]),
            bc=f(bc_g[b]),
            alog=f(inp["L0_attn_A_log"][q * DQ:(q + 1) * DQ]),
        ))
    r2 = run_bass_kernel_spmd(nc2, maps2, core_ids=list(range(NCORE))).results
    y_g = [np.vstack([r2[4 * b + q]["y_o"] for q in range(4)]) for b in range(2)]

    Wkv = inp["L1_attn_Wkv_d"]
    Wvu = inp["L1_attn_Wv_u"]
    Wo = inp["L1_attn_Wo"]
    nc3 = build_L3()
    maps3 = []
    for c in range(NCORE):
        b, ts = c // 4, (c % 4) * TS
        sl = (c % 4)
        maps3.append(dict(
            x=f(inp["L0_attn_x"][b, ts:ts + TS]),
            yt=f(y_g[b][:, ts:ts + TS].T),
            mdt=f(md_g[b][:, ts:ts + TS].T),
            wkv=f(Wkv.reshape(NDT, 128, KVL).transpose(1, 0, 2).reshape(128, -1)),
            bkv=f(inp["L1_attn_bkv_d"][:, None]),
            ones=ones, idm=ident,
            wvu=f(Wvu.T.reshape(NDT, 128, KVL).transpose(1, 0, 2).reshape(128, -1)),
            wocol=f(Wo[:, sl * TS:(sl + 1) * TS]),
            bvu=f(inp["L1_attn_bv_u"].reshape(NDT, 128).T),
            bo=f(inp["L1_attn_bo"][None, sl * TS:(sl + 1) * TS]),
        ))
    r3 = run_bass_kernel_spmd(nc3, maps3, core_ids=list(range(NCORE))).results

    ckvT_g = [np.hstack([np.asarray(r3[4 * b + q]["ckv_o"]) for q in range(4)])
              for b in range(2)]
    w2_full = np.hstack([np.asarray(r3[q]["w2s_o"]) for q in range(4)])
    c0_full = np.hstack([np.asarray(r3[q]["c0_o"]) for q in range(4)])[0]
    bd8 = np.zeros((NEXP, 128), np.float32)
    for h in range(NEXP):
        bd8[h, h * 16:(h + 1) * 16] = 1.0
    bdq = np.zeros((NH, 128), np.float32)
    for h in range(NH):
        bdq[h, h * 4:(h + 1) * 4] = 1.0
    rep4 = np.zeros((KVL, 128), np.float32)
    for h in range(NH):
        for l in range(KVL):
            rep4[l, h * 4 + l] = 1.0
    bd32 = np.zeros((128, NH), np.float32)
    for h in range(NH):
        bd32[h * 4:(h + 1) * 4, h] = 1.0
    wqr = np.zeros((128, NDT, 40), np.float32)
    wqr[:, :, 0:QL] = inp["L1_attn_Wq_d"].reshape(NDT, 128, QL).transpose(1, 0, 2)
    wqr[:, :, 32:40] = inp["L1_ffn_Wr"].T.reshape(NDT, 128, NEXP).transpose(1, 0, 2)
    nc4 = build_L4()
    maps4 = []
    for c in range(NCORE):
        b = c // 4
        ckvT_b = ckvT_g[b]
        maps4.append(dict(
            x1=f(np.asarray(r3[c]["x1_o"])),
            xnT=f(np.asarray(r3[c]["xnT_o"])),
            ckvT=f(ckvT_b),
            ckvtm=f(ckvT_b.T.reshape(32, 128, 5).transpose(1, 0, 2).reshape(128, -1)),
            w2=f(w2_full),
            c0p=f(c0_full.reshape(NDT, 128).T),
            l1a=f(inp["L1_ffn_A"].reshape(128, D).T.reshape(NDT, 128, 128)
                  .transpose(1, 0, 2).reshape(128, -1)),
            l1b=f(inp["L1_ffn_B"].transpose(0, 2, 1).reshape(128, D)),
            wqr=f(wqr.reshape(128, -1)),
            wqka=f(np.concatenate([inp["L1_attn_W_qk"],
                                   inp["L1_attn_b_qk"][None, :]], 0)),
            bq=f(inp["L1_attn_bq_d"][:, None]),
            br=f(inp["L1_ffn_br"][:, None]),
            ones=ones, idm=ident,
            bd8=bd8, bdq=bdq, rep4=rep4, bd32=bd32,
        ))
    r4 = run_bass_kernel_spmd(nc4, maps4, core_ids=list(range(NCORE))).results
    out = np.stack([np.vstack([np.asarray(r4[4 * b + q]["out_o"], np.float32)
                               for q in range(4)]) for b in range(2)], 0)
    return out


def build_L2(dbg=False):
    nc = bass.Bass("TRN2", target_bir_lowering=False)
    dt_d = nc.dram_tensor("dt", [DQ, LTOT], F32, kind="ExternalInput")
    u_d = nc.dram_tensor("u", [DQ, LTOT], BF16, kind="ExternalInput")
    bc_d = nc.dram_tensor("bc", [32, LTOT], F32, kind="ExternalInput")
    alog_d = nc.dram_tensor("alog", [DQ, NST], F32, kind="ExternalInput")
    y_o = nc.dram_tensor("y_o", [DQ, LTOT], BF16, kind="ExternalOutput")
    if dbg:
        dbg_a = nc.dram_tensor("dbg_a", [128, LTOT], F32, kind="ExternalOutput")
        dbg_b = nc.dram_tensor("dbg_b", [128, LTOT], F32, kind="ExternalOutput")
        dbg_h = nc.dram_tensor("dbg_h", [128, LTOT], BF16, kind="ExternalOutput")
        dbg_B = nc.dram_tensor("dbg_B", [128, LTOT], BF16, kind="ExternalOutput")

    NJ = 2
    NB = (DQ // 128) // NJ

    A_s = nc.alloc_sbuf_tensor("A_s", [128, (DQ // 128) * NST], F32)
    dt_s = [nc.alloc_sbuf_tensor(f"dt_s{j}", [128, LTOT], F32) for j in range(NJ)]
    u_s = [nc.alloc_sbuf_tensor(f"u_s{j}", [128, LTOT], BF16) for j in range(NJ)]
    Bbc = [nc.alloc_sbuf_tensor(f"Bbc{i}", [128, LTOT], BF16) for i in range(2)]
    Cbc = [nc.alloc_sbuf_tensor(f"Cbc{i}", [128, LTOT], BF16) for i in range(2)]
    dA_s = [nc.alloc_sbuf_tensor(f"dA_s{i}", [128, LTOT], F32) for i in range(2)]
    dBx_s = [nc.alloc_sbuf_tensor(f"dBx_s{i}", [128, LTOT], F32) for i in range(2)]
    h_s = [nc.alloc_sbuf_tensor(f"h_s{i}", [128, LTOT], BF16) for i in range(2)]
    tmp_s = nc.alloc_sbuf_tensor("tmp_s", [128, LTOT], BF16)
    y_s = [nc.alloc_sbuf_tensor(f"y_s{j}", [128, LTOT], F32) for j in range(NJ)]

    s = {}
    for name in ["ald", "dld", "bld", "act", "pool", "scan", "y", "st", "dbgv", "dbgs", "fence"]:
        s[name] = nc.alloc_semaphore("s_" + name)

    with nc.Block() as blk:
        @blk.sync
        def _(sync):
            src = bass.AP(alog_d, 0, [[NST, 128], [128 * NST, DQ // 128], [1, NST]])
            dst = A_s[:, :].rearrange("p (c n) -> p c n", c=DQ // 128)
            sync.dma_start(dst, src).then_inc(s["ald"], 16)

        @blk.scalar
        def _(sc):
            sc.wait_ge(s["ald"], 16)
            sc.activation(A_s[:, :], A_s[:, :], AF.Exp).then_inc(s["act"], 1)

        @blk.vector
        def _(v):
            v.wait_ge(s["act"], 1)
            v.tensor_scalar(A_s[:, :], A_s[:, :], -1.0, None, ALU.mult)

    with nc.Block() as blk:
        @blk.sync
        def _(sync):
            for b in range(NB):
                if b >= 1:
                    sync.wait_ge(s["st"], 16 * NJ * b)
                for j in range(NJ):
                    r0 = (b * NJ + j) * 128
                    sync.dma_start(dt_s[j][:, :], dt_d[r0:r0 + 128, :]).then_inc(s["dld"], 16)
                    sync.dma_start(u_s[j][:, :], u_d[r0:r0 + 128, :]).then_inc(s["dld"], 16)

        @blk.gpsimd
        def _(g):
            it = 0
            for b in range(NB):
                for n in range(NST):
                    gi = b * NST + n
                    if gi >= 2:
                        g.wait_ge(s["scan"], (gi - 2) * NJ + NJ)
                    g.dma_start(Bbc[gi % 2][:, :],
                                bass.AP(bc_d, n * LTOT, [[0, 128], [1, LTOT]])
                                ).then_inc(s["bld"], 16)
                    g.dma_start(Cbc[gi % 2][:, :],
                                bass.AP(bc_d, (16 + n) * LTOT, [[0, 128], [1, LTOT]])
                                ).then_inc(s["bld"], 16)
                    for j in range(NJ):
                        g.wait_ge(s["dld"], 32 * (b * NJ + j) + 32)
                        g.wait_ge(s["bld"], 32 * gi + 16)
                        if it >= 2:
                            g.wait_ge(s["scan"], it - 1)
                        g.tensor_tensor(dBx_s[it % 2][:, :], u_s[j][:, :],
                                        Bbc[gi % 2][:, :], ALU.mult
                                        ).then_inc(s["pool"], 1)
                        it += 1
                for j in range(NJ):
                    g.wait_ge(s["y"], b * NJ + j + 1)
                    r0 = (b * NJ + j) * 128
                    g.dma_start(y_o[r0:r0 + 128, :], y_s[j][:, :]).then_inc(s["st"], 16)
            g.wait_ge(s["st"], 16 * NJ * NB)

        @blk.scalar
        def _(sc):
            it = 0
            for b in range(NB):
                for n in range(NST):
                    for j in range(NJ):
                        sc.wait_ge(s["dld"], 32 * (b * NJ + j) + 32)
                        if it >= 2:
                            sc.wait_ge(s["scan"], it - 1)
                        c_glob = b * NJ + j
                        sc.activation(dA_s[it % 2][:, :], dt_s[j][:, :], AF.Exp,
                                      scale=A_s[:, c_glob * NST + n:c_glob * NST + n + 1]
                                      ).then_inc(s["act"], 1)
                        if dbg and it == 1:
                            sc.wait_ge(s["dbgv"], 1)
                            sc.dma_start(dbg_a[:, :], dA_s[0][:, :]).then_inc(s["dbgs"], 16)
                            sc.dma_start(dbg_b[:, :], dBx_s[0][:, :]).then_inc(s["dbgs"], 16)
                            sc.dma_start(dbg_h[:, :], h_s[0][:, :]).then_inc(s["dbgs"], 16)
                            sc.dma_start(dbg_B[:, :], Bbc[0][:, :]).then_inc(s["dbgs"], 16)
                        it += 1

        @blk.vector
        def _(v):
            it = 0
            for b in range(NB):
                if b >= 1:
                    v.wait_ge(s["st"], 16 * NJ * b)
                for j in range(NJ):
                    v.memset(y_s[j][:, :], 0.0)
                for n in range(NST):
                    gi = b * NST + n
                    for j in range(NJ):
                        v.wait_ge(s["act"], 1 + it + 1)
                        v.wait_ge(s["pool"], it + 1)
                        v.tensor_tensor_scan(h_s[it % 2][:, :], dA_s[it % 2][:, :],
                                             dBx_s[it % 2][:, :], 0.0,
                                             ALU.mult, ALU.add).then_inc(s["fence"], 1)
                        v.wait_ge(s["fence"], 2 * it + 1)
                        v.wait_ge(s["bld"], 32 * gi + 32)
                        v.tensor_mul(tmp_s[:, :], h_s[it % 2][:, :], Cbc[gi % 2][:, :]
                                     ).then_inc(s["fence"], 1)
                        v.wait_ge(s["fence"], 2 * it + 2)
                        v.tensor_add(y_s[j][:, :], y_s[j][:, :], tmp_s[:, :]
                                     ).then_inc(s["scan"], 1)
                        if dbg and it == 0:
                            v.sem_inc(s["dbgv"], 1)
                            v.wait_ge(s["dbgs"], 64)
                        it += 1
                for j in range(NJ):
                    v.sem_inc(s["y"], 1)



    with nc.Block() as blk:
        pass

    return nc


def build_L3(upto=99):
    nc = bass.Bass("TRN2", target_bir_lowering=False)
    x_d = nc.dram_tensor("x", [TS, D], F32, kind="ExternalInput")
    yt_d = nc.dram_tensor("yt", [TS, D], BF16, kind="ExternalInput")
    mdt_d = nc.dram_tensor("mdt", [TS, D], BF16, kind="ExternalInput")
    wkv_d = nc.dram_tensor("wkv", [128, NDT * KVL], F32, kind="ExternalInput")
    bkv_d = nc.dram_tensor("bkv", [KVL, 1], F32, kind="ExternalInput")
    ones_d = nc.dram_tensor("ones", [1, TS], F32, kind="ExternalInput")
    idm_d = nc.dram_tensor("idm", [128, 128], F32, kind="ExternalInput")
    wvu_d = nc.dram_tensor("wvu", [128, NH * KVL], F32, kind="ExternalInput")
    wocol_d = nc.dram_tensor("wocol", [D, TS], F32, kind="ExternalInput")
    bvu_d = nc.dram_tensor("bvu", [128, NDT], F32, kind="ExternalInput")
    bo_d = nc.dram_tensor("bo", [1, TS], F32, kind="ExternalInput")

    x1_o = nc.dram_tensor("x1_o", [TS, D], F32, kind="ExternalOutput")
    xnT_o = nc.dram_tensor("xnT_o", [D, TS], BF16, kind="ExternalOutput")
    ckv_o = nc.dram_tensor("ckv_o", [KVL + 1, TS], F32, kind="ExternalOutput")
    w2s_o = nc.dram_tensor("w2s_o", [128, TS], F32, kind="ExternalOutput")
    c0_o = nc.dram_tensor("c0_o", [1, TS], F32, kind="ExternalOutput")

    xnT = nc.alloc_sbuf_tensor("xnT", [128, NDT * TS], BF16)
    wkv_s = nc.alloc_sbuf_tensor("wkv_s", [128, NDT * KVL], BF16)
    bkvc_s = nc.alloc_sbuf_tensor("bkvc_s", [KVL, 1], F32)
    ones_s = nc.alloc_sbuf_tensor("ones_s", [1, TS], BF16)
    id_b = nc.alloc_sbuf_tensor("id_b", [128, 128], BF16)
    id_f = nc.alloc_sbuf_tensor("id_f", [128, 128], F32)
    cols = nc.alloc_sbuf_tensor("cols", [128, 8], F32)
    x_t = [nc.alloc_sbuf_tensor(f"x_t{i}", [128, D], F32) for i in range(2)]
    yt_t = nc.alloc_sbuf_tensor("yt_t", [128, D], BF16)
    mdt_t = nc.alloc_sbuf_tensor("mdt_t", [128, D], BF16)
    xsq = nc.alloc_sbuf_tensor("xsq", [128, D], BF16)
    xn_t = [nc.alloc_sbuf_tensor(f"xn_t{i}", [128, D], BF16) for i in range(2)]
    ckv_sb = nc.alloc_sbuf_tensor("ckv_sb", [KVL + 1, TS], F32)
    wvu_s = nc.alloc_sbuf_tensor("wvu_s", [128, NH * KVL], BF16)
    wo_t = [nc.alloc_sbuf_tensor(f"wo_t{i}", [128, TS], BF16) for i in range(2)]
    bvu_s = nc.alloc_sbuf_tensor("bvu_s", [128, NDT], BF16)
    w2row = nc.alloc_sbuf_tensor("w2row", [KVL, TS], F32)
    c0row = nc.alloc_sbuf_tensor("c0row", [1, TS], F32)
    bo_s = nc.alloc_sbuf_tensor("bo_s", [1, TS], F32)

    P0 = nc.alloc_psum_tensor("P0", [128, 1024], F32)
    P1 = nc.alloc_psum_tensor("P1", [128, 1024], F32)
    P2 = nc.alloc_psum_tensor("P2", [128, 1024], F32)
    P3 = nc.alloc_psum_tensor("P3", [128, 1024], BF16)
    P4 = nc.alloc_psum_tensor("P4", [128, 1024], BF16)
    p_tr = [P3, P4]

    s = {}
    for name in ["wld", "xld", "c1", "c2", "xn", "gtr", "gtrd", "x1st",
                 "mmk", "evk", "wol", "mmw", "evw", "st"]:
        s[name] = nc.alloc_semaphore("s_" + name)

    with nc.Block() as blk:
        @blk.sync
        def _(sync):
            sync.dma_start(bo_s[:, :], bo_d[:, :]).then_inc(s["wld"], 16)

        @blk.gpsimd
        def _(g):
            g.dma_start(wkv_s[:, :], wkv_d[:, :]).then_inc(s["wld"], 16)
            g.dma_start(bkvc_s[:, :], bkv_d[:, :]).then_inc(s["wld"], 16)
            g.dma_start(ones_s[:, :], ones_d[:, :]).then_inc(s["wld"], 16)
            g.dma_start(id_b[:, :], idm_d[:, :]).then_inc(s["wld"], 16)
            g.dma_start(id_f[:, :], idm_d[:, :]).then_inc(s["wld"], 16)
            g.dma_start(wvu_s[:, :], wvu_d[:, :]).then_inc(s["wld"], 16)
            g.dma_start(bvu_s[:, :], bvu_d[:, :]).then_inc(s["wld"], 16)

    with nc.Block() as blk:
        @blk.sync
        def _(sync):
            for i in range(NTT):
                if i >= 2:
                    sync.wait_ge(s["xn"], i - 1)
                    sync.wait_ge(s["x1st"], 16 * (i - 1))
                sync.dma_start(x_t[i % 2][:, :], x_d[i * 128:(i + 1) * 128, :]
                               ).then_inc(s["xld"], 16)
                sync.wait_ge(s["c2"], 2 * i)
                sync.dma_start(yt_t[:, :], yt_d[i * 128:(i + 1) * 128, :]
                               ).then_inc(s["xld"], 16)
                sync.dma_start(mdt_t[:, :], mdt_d[i * 128:(i + 1) * 128, :]
                               ).then_inc(s["xld"], 16)

        @blk.gpsimd
        def _(gx):
            for i in range(NTT):
                gx.wait_ge(s["xn"], i + 1)
                gx.dma_start(x1_o[i * 128:(i + 1) * 128, :], x_t[i % 2][:, :]
                             ).then_inc(s["x1st"], 16)

        @blk.vector
        def _(v):
            for i in range(NTT):
                v.wait_ge(s["xld"], 48 * (i + 1))
                xt = x_t[i % 2]
                v.tensor_add(xt[:, :], xt[:, :], yt_t[:, :])
                v.tensor_add(xt[:, :], xt[:, :], mdt_t[:, :]).then_inc(s["c2"], 1)
                v.wait_ge(s["c1"], 2 * i + 1)
                v.tensor_scalar(cols[:, 1:2], cols[:, 0:1], 1.0 / D, None, ALU.mult)
                v.tensor_scalar(cols[:, 3:4], cols[:, 2:3], 1.0 / D, None, ALU.mult)
                v.drain()
                v.tensor_mul(cols[:, 4:5], cols[:, 1:2], cols[:, 1:2])
                v.drain()
                v.tensor_sub(cols[:, 5:6], cols[:, 3:4], cols[:, 4:5])
                v.drain()
                v.tensor_scalar(cols[:, 5:6], cols[:, 5:6], 1e-5, None, ALU.add
                                ).then_inc(s["c2"], 1)
                v.wait_ge(s["c1"], 2 * i + 2)
                if i >= 1:
                    v.wait_ge(s["gtr"], i * NDT)
                v.tensor_scalar(xn_t[i % 2][:, :], xt[:, :], cols[:, 1:2],
                                cols[:, 7:8], ALU.subtract, ALU.mult
                                ).then_inc(s["xn"], 1)
                for c in range(1, NDT, 2):
                    v.wait_ge(s["gtr"], i * NDT + c + 1)
                    v.tensor_copy(xnT[:, c * TS + i * 128: c * TS + (i + 1) * 128],
                                  p_tr[c % 2][:, 0:128]).then_inc(s["gtrd"], 1)

        @blk.scalar
        def _(sc):
            for i in range(NTT):
                sc.wait_ge(s["c2"], 2 * i + 1)
                sc.activation(xsq[:, :], x_t[i % 2][:, :], AF.Copy,
                              accum_out=cols[:, 0:1])
                sc.activation(xsq[:, :], x_t[i % 2][:, :], AF.Square,
                              accum_out=cols[:, 2:3]).then_inc(s["c1"], 1)
                sc.wait_ge(s["c2"], 2 * i + 2)
                sc.activation(cols[:, 6:7], cols[:, 5:6], AF.Ln)
                sc.drain()
                sc.activation(cols[:, 7:8], cols[:, 6:7], AF.Exp, scale=-0.5
                              ).then_inc(s["c1"], 1)
                for c in range(0, NDT, 2):
                    sc.wait_ge(s["gtr"], i * NDT + c + 1)
                    sc.copy(xnT[:, c * TS + i * 128: c * TS + (i + 1) * 128],
                            p_tr[c % 2][:, 0:128]).then_inc(s["gtrd"], 1)

        @blk.tensor
        def _(t):
            for i in range(NTT):
                t.wait_ge(s["xn"], i + 1)
                for c in range(NDT):
                    idx = i * NDT + c
                    if idx >= 2:
                        t.wait_ge(s["gtrd"], idx - 1)
                    t.transpose(p_tr[c % 2][:, 0:128],
                                xn_t[i % 2][:, c * 128:(c + 1) * 128],
                                id_b[:, :]).then_inc(s["gtr"], 1)

    if upto <= 1:
        return nc
    with nc.Block() as blk:
        @blk.sync
        def _(sync):
            dst = bass.AP(xnT_o, 0, [[TS, 128], [128 * TS, NDT], [1, TS]])
            src = xnT[:, :].rearrange("p (c t) -> p c t", c=NDT)
            sync.dma_start(dst, src).then_inc(s["st"], 16)

        @blk.tensor
        def _(t):
            for c in range(NDT):
                lhs = wkv_s[:, c * KVL:(c + 1) * KVL]
                for hf in range(2):
                    mm = t.matmul(P0[0:KVL, hf * 512:(hf + 1) * 512], lhs,
                                  xnT[:, c * TS + hf * 512:c * TS + (hf + 1) * 512],
                                  start=(c == 0), stop=(c == NDT - 1))
                    if c == NDT - 1 and hf == 1:
                        mm.then_inc(s["mmk"], 1)

        @blk.vector
        def _(v):
            v.wait_ge(s["mmk"], 1)
            v.tensor_scalar(ckv_sb[0:KVL, :], P0[0:KVL, :], bkvc_s[:, :], None,
                            ALU.add).then_inc(s["evk"], 1)

        @blk.gpsimd
        def _(g):
            g.wait_ge(s["evk"], 1)
            g.dma_start(ckv_sb[KVL:KVL + 1, :], ones_d[:, :]).then_inc(s["evk"], 16)


        @blk.sync
        def _(sync2):
            pass

    if upto <= 2:
        return nc
    with nc.Block() as blk:
        @blk.gpsimd
        def _(g):
            for c in range(NDT):
                if c >= 2:
                    g.wait_ge(s["mmw"], c - 1)
                g.dma_start(wo_t[c % 2][:, :], wocol_d[c * 128:(c + 1) * 128, :]
                            ).then_inc(s["wol"], 16)

        @blk.sync
        def _(sync):
            sync.wait_ge(s["evk"], 17)
            sync.dma_start(ckv_o[:, :], ckv_sb[:, :]).then_inc(s["st"], 16)

        @blk.tensor
        def _(t):
            for c in range(NDT):
                t.wait_ge(s["wol"], 16 * (c + 1))
                if c >= 2:
                    t.wait_ge(s["evw"], c - 1)
                PW = P1 if c % 2 == 0 else P2
                for hf in range(2):
                    t.matmul(PW[0:KVL, hf * 512:(hf + 1) * 512],
                             wvu_s[:, c * KVL:(c + 1) * KVL],
                             wo_t[c % 2][:, hf * 512:(hf + 1) * 512],
                             start=True, stop=True)
                for hf in range(2):
                    mm = t.matmul(P0[0:1, hf * 512:(hf + 1) * 512],
                                  bvu_s[:, c:c + 1],
                                  wo_t[c % 2][:, hf * 512:(hf + 1) * 512],
                                  start=(c == 0), stop=(c == NDT - 1))
                    if hf == 1:
                        mm.then_inc(s["mmw"], 1)

        @blk.vector
        def _(v):
            for c in range(NDT):
                v.wait_ge(s["mmw"], c + 1)
                v.tensor_copy(w2row[:, :], (P1 if c % 2 == 0 else P2)[0:KVL, :]
                              ).then_inc(s["evw"], 1)
                v.wait_ge(s["st"], 48 + 16 * c)
            v.tensor_add(c0row[:, :], P0[0:1, :], bo_s[:, :]).then_inc(s["evw"], 1)

        @blk.sync
        def _(sync2):
            for c in range(NDT):
                sync2.wait_ge(s["evw"], c + 1)
                sync2.dma_start(w2s_o[c * KVL:(c + 1) * KVL, :], w2row[:, :]
                                ).then_inc(s["st"], 16)
            sync2.wait_ge(s["evw"], NDT + 1)
            sync2.dma_start(c0_o[:, :], c0row[:, :]).then_inc(s["st"], 16)

    with nc.Block() as blk:
        @blk.sync
        def _(sync):
            sync.wait_ge(s["st"], 16 * (1 + 1 + NDT + 1))
            sync.wait_ge(s["x1st"], 16 * NTT)

    return nc


def build_L4(upto=99, dbg=False):
    nc = bass.Bass("TRN2", target_bir_lowering=False)
    x1_d = nc.dram_tensor("x1", [TS, D], F32, kind="ExternalInput")
    xnT_d = nc.dram_tensor("xnT", [D, TS], BF16, kind="ExternalInput")
    ckvT_d = nc.dram_tensor("ckvT", [KVL + 1, LTOT], F32, kind="ExternalInput")
    ckvtm_d = nc.dram_tensor("ckvtm", [128, 32 * (KVL + 1)], F32, kind="ExternalInput")
    w2_d = nc.dram_tensor("w2", [128, D], F32, kind="ExternalInput")
    c0p_d = nc.dram_tensor("c0p", [128, NDT], F32, kind="ExternalInput")
    l1a_d = nc.dram_tensor("l1a", [128, NDT * 128], F32, kind="ExternalInput")
    l1b_d = nc.dram_tensor("l1b", [128, D], F32, kind="ExternalInput")
    wqr_d = nc.dram_tensor("wqr", [128, NDT * 40], F32, kind="ExternalInput")
    wqka_d = nc.dram_tensor("wqka", [QL + 1, 128], F32, kind="ExternalInput")
    bq_d = nc.dram_tensor("bq", [QL, 1], F32, kind="ExternalInput")
    br_d = nc.dram_tensor("br", [NEXP, 1], F32, kind="ExternalInput")
    ones_d = nc.dram_tensor("ones", [1, TS], F32, kind="ExternalInput")
    idm_d = nc.dram_tensor("idm", [128, 128], F32, kind="ExternalInput")
    bd8_d = nc.dram_tensor("bd8", [NEXP, 128], F32, kind="ExternalInput")
    bdq_d = nc.dram_tensor("bdq", [NH, 128], F32, kind="ExternalInput")
    rep4_d = nc.dram_tensor("rep4", [KVL, 128], F32, kind="ExternalInput")
    bd32_d = nc.dram_tensor("bd32", [128, NH], F32, kind="ExternalInput")
    out_o = nc.dram_tensor("out_o", [TS, D], F32, kind="ExternalOutput")
    q_dram = nc.dram_tensor("q_scr", [128, TS], BF16, kind="Internal")
    m_dram = nc.dram_tensor("m_scr", [NH, TS], BF16, kind="Internal")
    if dbg:
        dbg_o = {}
        for nm, shp, dt in [("lowT", [128, TS], BF16), ("cqT", [QL + 1, TS], BF16),
                            ("routT", [NEXP, TS], F32), ("scoresT", [NEXP, TS], BF16),
                            ("lowsc", [128, TS], BF16), ("qT05", [128, TS], BF16),
                            ("mneg", [NH, TS], F32), ("urawT", [128, TS], BF16),
                            ("dnT", [NH, TS], F32), ("unT", [128, TS], BF16)]:
            dbg_o[nm] = nc.dram_tensor("dbg_" + nm, shp, dt, kind="ExternalOutput")
        dbg_o["big"] = nc.dram_tensor("dbg_big", [128, NDT * TS], BF16, kind="ExternalOutput")
        dbg_o["l1a"] = nc.dram_tensor("dbg_l1a", [128, NDT * 128], BF16, kind="ExternalOutput")
        dbg_o["wqka"] = nc.dram_tensor("dbg_wqka", [QL + 1, 128], BF16, kind="ExternalOutput")
        dbg_o["qph"] = nc.dram_tensor("dbg_qph", [5, TS], BF16, kind="ExternalOutput")
        dbg_o["eb"] = nc.dram_tensor("dbg_eb", [128, 2 * TS], BF16, kind="ExternalOutput")
        dbg_o["cktm"] = nc.dram_tensor("dbg_cktm", [128, 32 * 5], BF16, kind="ExternalOutput")
        dbg_o["wqr"] = nc.dram_tensor("dbg_wqr", [128, NDT * 40], BF16, kind="ExternalOutput")
    u_dram = nc.dram_tensor("u_scr", [128, TS], F32, kind="Internal")
    d_dram = nc.dram_tensor("d_scr", [NH, TS], F32, kind="Internal")

    bigbuf = nc.alloc_sbuf_tensor("bigbuf", [128, NDT * TS], BF16)
    ckvT_s = nc.alloc_sbuf_tensor("ckvT_s", [KVL + 1, LTOT], BF16)
    ckvtm_s = nc.alloc_sbuf_tensor("ckvtm_s", [128, 32 * (KVL + 1)], BF16)
    w2_s = nc.alloc_sbuf_tensor("w2_s", [128, D], BF16)
    c0p_s = nc.alloc_sbuf_tensor("c0p_s", [128, NDT], F32)
    l1a_s = nc.alloc_sbuf_tensor("l1a_s", [128, NDT * 128], BF16)
    l1b_s = nc.alloc_sbuf_tensor("l1b_s", [128, D], BF16)
    wqr_s = nc.alloc_sbuf_tensor("wqr_s", [128, NDT * 40], BF16)
    wqka_s = nc.alloc_sbuf_tensor("wqka_s", [QL + 1, 128], BF16)
    bq_s = nc.alloc_sbuf_tensor("bq_s", [QL, 1], F32)
    br_s = nc.alloc_sbuf_tensor("br_s", [NEXP, 1], F32)
    ones_s = nc.alloc_sbuf_tensor("ones_s", [1, TS], BF16)
    id_f = nc.alloc_sbuf_tensor("id_f", [128, 128], F32)
    id_b = nc.alloc_sbuf_tensor("id_b", [128, 128], BF16)
    bd8_s = nc.alloc_sbuf_tensor("bd8_s", [NEXP, 128], BF16)
    bdq_s = nc.alloc_sbuf_tensor("bdq_s", [NH, 128], BF16)
    rep4_s = nc.alloc_sbuf_tensor("rep4_s", [KVL, 128], BF16)
    bd32_s = nc.alloc_sbuf_tensor("bd32_s", [128, NH], BF16)
    lowT = nc.alloc_sbuf_tensor("lowT", [128, TS], BF16)
    lowsc = nc.alloc_sbuf_tensor("lowsc", [128, TS], BF16)
    cqT = nc.alloc_sbuf_tensor("cqT", [QL + 1, TS], BF16)
    routT = nc.alloc_sbuf_tensor("routT", [NEXP, TS], F32)
    g_tm = nc.alloc_sbuf_tensor("g_tm", [128, NEXP], F32)
    ge_tm = nc.alloc_sbuf_tensor("ge_tm", [128, NEXP], F32)
    gcol = nc.alloc_sbuf_tensor("gcol", [128, 4], F32)
    scoresT = nc.alloc_sbuf_tensor("scoresT", [NEXP, TS], BF16)
    kc8 = nc.alloc_sbuf_tensor("kc8", [KVL, 8], F32)
    kabs_f = nc.alloc_sbuf_tensor("kabs_f", [KVL, 1], F32)
    kabs_b = nc.alloc_sbuf_tensor("kabs_b", [KVL, 1], BF16)
    kabscol = nc.alloc_sbuf_tensor("kabscol", [128, 1], F32)
    kabsbd = nc.alloc_sbuf_tensor("kabsbd", [128, NH], BF16)
    qph = [nc.alloc_sbuf_tensor(f"qph{i}", [5, TS], BF16) for i in range(2)]
    sfr = nc.alloc_sbuf_tensor("sfr", [128, 4 * TS], F32)
    expb = [nc.alloc_sbuf_tensor(f"expb{i}", [128, 2 * TS], BF16) for i in range(2)]
    u5 = [nc.alloc_sbuf_tensor(f"u5_{i}", [5, TS], F32) for i in range(2)]
    urawT = nc.alloc_sbuf_tensor("urawT", [128, TS], BF16)
    dnT = nc.alloc_sbuf_tensor("dnT", [NH, TS], F32)
    rc32 = nc.alloc_sbuf_tensor("rc32", [NH, TS], BF16)
    mneg2 = nc.alloc_sbuf_tensor("mneg2", [NH, TS], F32)
    unT = nc.alloc_sbuf_tensor("unT", [128, TS], BF16)
    x1t = nc.alloc_sbuf_tensor("x1t", [128, D], BF16)
    outt = nc.alloc_sbuf_tensor("outt", [128, D // 2], F32)

    G0 = nc.alloc_psum_tensor("G0", [128, 1024], F32)
    G1 = nc.alloc_psum_tensor("G1", [128, 1024], F32)

    s = {}
    for name in ["wld", "mm1", "ev1", "tr", "g", "mm2", "ev2", "mm3", "ev3",
                 "qst", "qld", "smm", "sev", "sact", "su", "sue", "ust",
                 "uld", "mm6", "ev6", "x1ld", "otr", "otrd", "ost", "f", "dbgf",
                 "sac5"]:
        s[name] = nc.alloc_semaphore("s_" + name)

    with nc.Block() as blk:
        @blk.sync
        def _(sync):
            src = bass.AP(xnT_d, 0, [[TS, 128], [128 * TS, NDT], [1, TS]])
            dst = bigbuf[:, :].rearrange("p (c t) -> p c t", c=NDT)
            sync.dma_start(dst, src).then_inc(s["wld"], 16)
            sync.dma_start(c0p_s[:, :], c0p_d[:, :]).then_inc(s["wld"], 16)
            sync.dma_start(bq_s[:, :], bq_d[:, :]).then_inc(s["wld"], 16)
            sync.dma_start(br_s[:, :], br_d[:, :]).then_inc(s["wld"], 16)
            sync.dma_start(id_f[:, :], idm_d[:, :]).then_inc(s["wld"], 16)

        @blk.gpsimd
        def _(g):
            for dst, srcd in ((ckvT_s, ckvT_d), (ckvtm_s, ckvtm_d), (w2_s, w2_d),
                              (l1a_s, l1a_d), (l1b_s, l1b_d), (wqr_s, wqr_d),
                              (wqka_s, wqka_d), (ones_s, ones_d), (id_b, idm_d),
                              (bd8_s, bd8_d), (bdq_s, bdq_d), (rep4_s, rep4_d),
                              (bd32_s, bd32_d)):
                g.dma_start(dst[:, :], srcd[:, :]).then_inc(s["wld"], 16)

    if upto <= 0:
        return nc
    if dbg:
        with nc.Block() as blk:
            @blk.sync
            def _(sync):
                sync.wait_ge(s["wld"], 16 * 18)
                sync.dma_start(dbg_o["big"][:, :], bigbuf[:, :]).then_inc(s["dbgf"], 16)
                sync.dma_start(dbg_o["l1a"][:, :], l1a_s[:, :]).then_inc(s["dbgf"], 16)
                sync.dma_start(dbg_o["wqr"][:, :], wqr_s[:, :]).then_inc(s["dbgf"], 16)
                sync.wait_ge(s["dbgf"], 48)
    with nc.Block() as blk:
        @blk.tensor
        def _(t):
            t.wait_ge(s["wld"], 16 * 18)
            for c in range(NDT):
                st_, sp = (c == 0), (c == NDT - 1)
                rhs = bigbuf[:, c * TS:(c + 1) * TS]
                for hf in range(2):
                    r = rhs[:, hf * 512:(hf + 1) * 512]
                    t.matmul(G0[:, hf * 512:(hf + 1) * 512],
                             l1a_s[:, c * 128:(c + 1) * 128], r, start=st_, stop=sp)
                    mm = t.matmul(G1[0:40, hf * 512:(hf + 1) * 512],
                                  wqr_s[:, c * 40:(c + 1) * 40], r, start=st_, stop=sp)
                    if sp and hf == 1:
                        mm.then_inc(s["mm1"], 1)

        @blk.vector
        def _(v):
            v.wait_ge(s["mm1"], 1)
            v.tensor_copy(lowT[:, :], G0[:, :])
            v.tensor_scalar(cqT[0:QL, :], G1[0:QL, :], bq_s[:, :], None, ALU.add)
            v.tensor_scalar(routT[:, :], G1[32:40, :], br_s[:, :], None, ALU.add
                            ).then_inc(s["ev1"], 1)

        @blk.gpsimd
        def _(g):
            g.wait_ge(s["ev1"], 1)
            g.dma_start(cqT[QL:QL + 1, :], ones_d[:, :]).then_inc(s["ev1"], 16)

    if dbg:
        with nc.Block() as blk:
            @blk.sync
            def _(sync):
                sync.dma_start(dbg_o["lowT"][:, :], lowT[:, :]).then_inc(s["dbgf"], 16)
                sync.dma_start(dbg_o["cqT"][:, :], cqT[:, :]).then_inc(s["dbgf"], 16)
                sync.wait_ge(s["dbgf"], 80)
    if upto <= 1:
        return nc
    with nc.Block() as blk:
        @blk.tensor
        def _(t):
            for j in range(NTT):
                if j >= 2:
                    t.wait_ge(s["g"], 6 * (j - 1))
                bank = G0[:, 0:512] if j % 2 == 0 else G0[:, 512:1024]
                t.transpose(bank[:, 0:8], routT[:, j * 128:(j + 1) * 128],
                            id_f[0:NEXP, 0:NEXP]).then_inc(s["tr"], 1)
                t.wait_ge(s["g"], 6 * j + 5)
                t.transpose(bank[0:8, 256:384], g_tm[:, :], id_f[:, :]
                            ).then_inc(s["tr"], 1)
            t.wait_ge(s["g"], 6 * NTT)
            for hf in range(2):
                mm = t.matmul(G1[:, hf * 512:(hf + 1) * 512], bd8_s[:, :],
                              scoresT[:, hf * 512:(hf + 1) * 512],
                              start=True, stop=True)
                if hf == 1:
                    mm.then_inc(s["mm2"], 1)

        @blk.vector
        def _(v):
            for j in range(NTT):
                bank = G0[:, 0:512] if j % 2 == 0 else G0[:, 512:1024]
                v.wait_ge(s["tr"], 2 * j + 1)
                v.tensor_copy(g_tm[:, :], bank[:, 0:8])
                v.drain()
                v.reduce_max(gcol[:, 0:1], g_tm[:, :], axis=AX.X, negate=True
                             ).then_inc(s["g"], 1)
                v.wait_ge(s["g"], 6 * j + 2)
                v.reduce_sum(gcol[:, 1:2], ge_tm[:, :], axis=AX.X)
                v.drain()
                v.reciprocal(gcol[:, 2:3], gcol[:, 1:2])
                v.drain()
                v.tensor_scalar(gcol[:, 2:3], gcol[:, 2:3], 16.0, None, ALU.mult
                                ).then_inc(s["g"], 1)
                v.wait_ge(s["g"], 6 * j + 3)
                v.tensor_scalar(g_tm[:, :], ge_tm[:, :], gcol[:, 2:3], None,
                                ALU.mult).then_inc(s["g"], 3)
                v.wait_ge(s["tr"], 2 * j + 2)
                v.tensor_copy(scoresT[:, j * 128:(j + 1) * 128], bank[0:8, 256:384])
                v.drain()
                v.sem_inc(s["f"], 1)
            v.wait_ge(s["mm2"], 1)
            v.tensor_mul(lowsc[:, :], lowT[:, :], G1[:, :]).then_inc(s["ev2"], 1)

        @blk.scalar
        def _(sc):
            for j in range(NTT):
                sc.wait_ge(s["g"], 6 * j + 1)
                sc.activation(ge_tm[:, :], g_tm[:, :], AF.Exp, bias=gcol[:, 0:1]
                              ).then_inc(s["g"], 1)

    if dbg and int(dbg) >= 2:
        with nc.Block() as blk:
            @blk.sync
            def _(sync):
                sync.dma_start(dbg_o["routT"][:, :], routT[:, :]).then_inc(s["dbgf"], 16)
                sync.dma_start(dbg_o["scoresT"][:, :], scoresT[:, :]).then_inc(s["dbgf"], 16)
                sync.dma_start(dbg_o["lowsc"][:, :], lowsc[:, :]).then_inc(s["dbgf"], 16)
                sync.wait_ge(s["dbgf"], 128)
    if upto <= 2:
        return nc
    stk3 = contextlib.ExitStack()
    qT05 = stk3.enter_context(nc.sbuf_tensor("qT05", [128, TS], BF16))
    absq = stk3.enter_context(nc.sbuf_tensor("absq", [128, TS], BF16))
    mneg = stk3.enter_context(nc.sbuf_tensor("mneg", [NH, TS], F32))
    with nc.Block() as blk:
        @blk.tensor
        def _(t):
            for hf in range(2):
                mm = t.matmul(G0[:, hf * 512:(hf + 1) * 512], wqka_s[:, :],
                              cqT[:, hf * 512:(hf + 1) * 512], start=True, stop=True)
                if hf == 1:
                    mm.then_inc(s["mm3"], 1)
            t.wait_ge(s["ev3"], 2)
            t.matmul(G1[:, 0:1], rep4_s[:, :], kabs_b[:, :], start=True, stop=True
                     ).then_inc(s["mm3"], 1)
            t.wait_ge(s["ev3"], 3)
            t.wait_ge(s["sact"], 1)
            for hf in range(2):
                mm = t.matmul(G0[0:NH, hf * 512:(hf + 1) * 512], kabsbd[:, :],
                              absq[:, hf * 512:(hf + 1) * 512], start=True, stop=True)
                if hf == 1:
                    mm.then_inc(s["mm3"], 1)

        @blk.vector
        def _(v):
            v.wait_ge(s["mm3"], 1)
            v.tensor_scalar(qT05[:, :], G0[:, :], 0.5, None, ALU.mult
                            ).then_inc(s["ev3"], 1)
            for k in range(8):
                v.tensor_reduce(kc8[:, k:k + 1], ckvT_s[0:KVL, k * 512:(k + 1) * 512],
                                AX.X, ALU.max, apply_absolute_value=True)
            v.drain()
            v.reduce_max(kabs_f[:, :], kc8[:, :], axis=AX.X)
            v.drain()
            v.tensor_copy(kabs_b[:, :], kabs_f[:, :])
            v.drain()
            v.sem_inc(s["ev3"], 1)
            v.wait_ge(s["mm3"], 2)
            v.tensor_copy(kabscol[:, :], G1[:, 0:1])
            v.drain()
            v.tensor_scalar(kabsbd[:, :], bd32_s[:, :], kabscol[:, :], None, ALU.mult)
            v.drain()
            v.sem_inc(s["ev3"], 1)
            v.wait_ge(s["mm3"], 3)
            v.tensor_scalar(mneg[:, :], G0[0:NH, :], -1.0, None, ALU.mult
                            ).then_inc(s["ev3"], 1)

        @blk.scalar
        def _(sc):
            sc.wait_ge(s["mm3"], 1)
            sc.activation(absq[:, :], G0[:, :], AF.Abs, scale=0.5).then_inc(s["sact"], 1)

        @blk.gpsimd
        def _(g):
            g.wait_ge(s["ev3"], 1)
            if dbg:
                g.dma_start(dbg_o["qT05"][:, :], qT05[:, :]).then_inc(s["dbgf"], 16)
                g.dma_start(dbg_o["wqka"][:, :], wqka_s[:, :]).then_inc(s["dbgf"], 16)
                g.wait_ge(s["dbgf"], 128 + 32)
            g.dma_start(q_dram[:, :], qT05[:, :]).then_inc(s["qst"], 16)
            g.wait_ge(s["ev3"], 4)
            if dbg:
                g.dma_start(dbg_o["mneg"][:, :], mneg[:, :]).then_inc(s["dbgf"], 16)
            g.dma_start(m_dram[:, :], mneg[:, :]).then_inc(s["qst"], 16)
            g.wait_ge(s["qst"], 32)

    stk3.close()
    if upto <= 3:
        return nc
    with contextlib.ExitStack() as stk:
        QU = [stk.enter_context(nc.psum_tensor(f"QU{i}", [5, 1024], F32))
              for i in range(2)]
        GS = [G0, G1]

        with nc.Block() as blk:
            @blk.gpsimd
            def _(g):
                for h in range(NH):
                    if h >= 2:
                        g.wait_ge(s["smm"], 32 * (h - 1))
                    g.dma_start(qph[h % 2][0:4, :], q_dram[4 * h:4 * h + 4, :]
                                ).then_inc(s["qld"], 16)
                    g.dma_start(qph[h % 2][4:5, :], m_dram[h:h + 1, :]
                                ).then_inc(s["qld"], 16)
                    if dbg and h == 0:
                        g.wait_ge(s["qld"], 32)
                        g.dma_start(dbg_o["qph"][:, :], qph[0][:, :]).then_inc(s["dbgf"], 16)
                    if dbg and h == 1:
                        g.dma_start(dbg_o["cktm"][:, :], ckvtm_s[:, :]).then_inc(s["dbgf"], 16)

            @blk.tensor
            def _(t):
                for h in range(NH):
                    t.wait_ge(s["qld"], 32 * (h + 1))
                    if h >= 2:
                        t.wait_ge(s["sue"], h - 1)
                    for kt in range(NST * 2):
                        it = h * 32 + kt
                        if it >= 1:
                            t.wait_ge(s["sev"], it)
                        for hf in range(2):
                            mm = t.matmul(
                                GS[it % 2][:, hf * 512:(hf + 1) * 512],
                                ckvT_s[:, kt * 128:(kt + 1) * 128],
                                qph[h % 2][:, hf * 512:(hf + 1) * 512],
                                start=True, stop=True)
                            if hf == 1:
                                mm.then_inc(s["smm"], 1)
                        if it >= 2 and it % 2 == 0:
                            for itp in range(it - 2, it):
                                t.wait_ge(s["sac5"], 1 + itp // 2 + 1)
                                hp = itp // 32
                                sl = itp % 2
                                eb = expb[(itp // 2) % 2]
                                for hf in range(2):
                                    mm = t.matmul(
                                        QU[hp % 2][:, hf * 512:(hf + 1) * 512],
                                        ckvtm_s[:, (itp % 32) * 5:((itp % 32) + 1) * 5],
                                        eb[:, sl * TS + hf * 512:sl * TS + (hf + 1) * 512],
                                        start=(itp % 32 == 0), stop=(itp % 32 == 31))
                                    if hf == 1:
                                        mm.then_inc(s["su"], 1)
                for itp in range(NH * 32 - 2, NH * 32):
                    t.wait_ge(s["sac5"], 1 + itp // 2 + 1)
                    hp = itp // 32
                    sl = itp % 2
                    eb = expb[(itp // 2) % 2]
                    for hf in range(2):
                        mm = t.matmul(
                            QU[hp % 2][:, hf * 512:(hf + 1) * 512],
                            ckvtm_s[:, (itp % 32) * 5:((itp % 32) + 1) * 5],
                            eb[:, sl * TS + hf * 512:sl * TS + (hf + 1) * 512],
                            start=(itp % 32 == 0), stop=(itp % 32 == 31))
                        if hf == 1:
                            mm.then_inc(s["su"], 1)

            @blk.vector
            def _(v):
                for it in range(NH * 32):
                    v.wait_ge(s["smm"], it + 1)
                    if it >= 4:
                        v.wait_ge(s["sac5"], it // 2)
                    v.tensor_copy(sfr[:, (it % 4) * TS:(it % 4 + 1) * TS],
                                  GS[it % 2][:, :])
                    v.drain()
                    v.sem_inc(s["sev"], 1)
                    if it % 32 == 31:
                        h = it // 32
                        if h >= 2:
                            v.wait_ge(s["ust"], 32 * (h - 1))
                        v.wait_ge(s["su"], 32 * (h + 1))
                        v.tensor_copy(u5[h % 2][:, :], QU[h % 2][:, :])
                        v.drain()
                        v.sem_inc(s["sue"], 1)

            @blk.scalar
            def _(sc):
                sc.sem_inc(s["sac5"], 1)
                for g in range(NH * 16):
                    sc.wait_ge(s["sev"], 2 * (g + 1))
                    if g >= 2:
                        sc.wait_ge(s["su"], 2 * g - 2)
                    sc.activation(expb[g % 2][:, :],
                                  sfr[:, 2 * (g % 2) * TS:(2 * (g % 2) + 2) * TS],
                                  AF.Exp)
                    sc.drain()
                    sc.sem_inc(s["sac5"], 1)
                    if dbg and g == 0:
                        sc.dma_start(dbg_o["eb"][:, :], expb[0][:, :]).then_inc(s["dbgf"], 16)

            @blk.sync
            def _(sync):
                for h in range(NH):
                    sync.wait_ge(s["sue"], h + 1)
                    sync.dma_start(u_dram[h * 4:(h + 1) * 4, :], u5[h % 2][0:4, :]
                                   ).then_inc(s["ust"], 16)
                    sync.dma_start(d_dram[h:h + 1, :], u5[h % 2][4:5, :]
                                   ).then_inc(s["ust"], 16)

    if upto <= 4:
        return nc
    with nc.Block() as blk:
        @blk.gpsimd
        def _(g):
            g.wait_ge(s["ust"], 32 * NH)
            g.dma_start(urawT[:, :], u_dram[:, :]).then_inc(s["uld"], 16)
            g.dma_start(dnT[:, :], d_dram[:, :]).then_inc(s["uld"], 16)

        @blk.vector
        def _(v):
            v.wait_ge(s["uld"], 32)
            v.reciprocal(mneg2[:, :], dnT[:, :])
            v.drain()
            v.tensor_copy(rc32[:, :], mneg2[:, :])
            v.drain()
            v.sem_inc(s["ev2"], 1)
            v.wait_ge(s["mm2"], 2)
            v.tensor_mul(unT[:, :], urawT[:, :], G0[:, :]).then_inc(s["ev2"], 1)

        @blk.tensor
        def _(t):
            t.wait_ge(s["ev2"], 2)
            for hf in range(2):
                mm = t.matmul(G0[:, hf * 512:(hf + 1) * 512], bdq_s[:, :],
                              rc32[:, hf * 512:(hf + 1) * 512], start=True, stop=True)
                if hf == 1:
                    mm.then_inc(s["mm2"], 1)

    if dbg and int(dbg) >= 3:
        with nc.Block() as blk:
            @blk.sync
            def _(sync):
                for nm, t_ in [("urawT", urawT), ("dnT", dnT), ("unT", unT)]:
                    sync.dma_start(dbg_o[nm][:, :], t_[:, :]).then_inc(s["dbgf"], 16)
                sync.wait_ge(s["dbgf"], 224 if int(dbg) >= 2 else 128)
    if upto <= 5:
        return nc
    with nc.Block() as blk:
        @blk.tensor
        def _(t):
            for c in range(NDT):
                if c >= 2:
                    t.wait_ge(s["ev6"], c - 1)
                for hf in range(2):
                    t.matmul(GS[c % 2][:, hf * 512:(hf + 1) * 512],
                             w2_s[:, c * 128:(c + 1) * 128],
                             unT[:, hf * 512:(hf + 1) * 512], start=True, stop=False)
                    mm = t.matmul(GS[c % 2][:, hf * 512:(hf + 1) * 512],
                                  l1b_s[:, c * 128:(c + 1) * 128],
                                  lowsc[:, hf * 512:(hf + 1) * 512],
                                  start=False, stop=True)
                    if hf == 1:
                        mm.then_inc(s["mm6"], 1)

        @blk.vector
        def _(v):
            for c in range(NDT):
                v.wait_ge(s["mm6"], c + 1)
                v.tensor_scalar(bigbuf[:, c * TS:(c + 1) * TS], GS[c % 2][:, :],
                                c0p_s[:, c:c + 1], None, ALU.add).then_inc(s["ev6"], 1)

    if upto <= 6:
        return nc
    if dbg and int(dbg) >= 3:
        with nc.Block() as blk:
            @blk.sync
            def _(sync):
                sync.dma_start(dbg_o["big"][:, :], bigbuf[:, :]).then_inc(s["dbgf"], 16)
                sync.wait_ge(s["dbgf"], 16 * 17)
    with contextlib.ExitStack() as stk:
        PT = [stk.enter_context(nc.psum_tensor(f"PT{i}", [128, 1024], BF16))
              for i in range(2)]

        with nc.Block() as blk:
            @blk.sync
            def _(sync):
                for i in range(NTT):
                    for half in range(2):
                        sync.wait_ge(s["otrd"], NDT * i + 16 * (half + 1))
                        sync.dma_start(
                            out_o[i * 128:(i + 1) * 128,
                                  half * 2048:(half + 1) * 2048],
                            outt[:, :]).then_inc(s["ost"], 16)

            @blk.gpsimd
            def _(g):
                for i in range(NTT):
                    if i >= 1:
                        g.wait_ge(s["otrd"], NDT * i)
                    g.dma_start(x1t[:, :], x1_d[i * 128:(i + 1) * 128, :]
                                ).then_inc(s["x1ld"], 16)

            @blk.tensor
            def _(t):
                for i in range(NTT):
                    for c in range(NDT):
                        idx = i * NDT + c
                        if idx >= 2:
                            t.wait_ge(s["otrd"], idx - 1)
                        t.transpose(PT[c % 2][:, 0:128],
                                    bigbuf[:, c * TS + i * 128:c * TS + (i + 1) * 128],
                                    id_b[:, :]).then_inc(s["otr"], 1)

            @blk.vector
            def _(v):
                for i in range(NTT):
                    v.wait_ge(s["x1ld"], 16 * (i + 1))
                    for c in range(NDT):
                        idx = i * NDT + c
                        if c % 16 == 0:
                            prev = 2 * i + c // 16 - 1
                            if prev >= 0:
                                v.wait_ge(s["ost"], 16 * (prev + 1))
                        v.wait_ge(s["otr"], idx + 1)
                        v.tensor_add(outt[:, (c % 16) * 128:(c % 16 + 1) * 128],
                                     x1t[:, c * 128:(c + 1) * 128],
                                     PT[c % 2][:, 0:128]).then_inc(s["otrd"], 1)

    with nc.Block() as blk:
        @blk.sync
        def _(sync):
            sync.wait_ge(s["ost"], 16 * NTT)

    return nc


# version 04
# speedup vs baseline: 1.1949x; 1.1949x over previous
import contextlib
import numpy as np
import ml_dtypes
import concourse.bass as bass
import concourse.mybir as mybir
from concourse.bass_utils import run_bass_kernel_spmd

F32 = mybir.dt.float32
BF16 = mybir.dt.bfloat16
AF = mybir.ActivationFunctionType
ALU = mybir.AluOpType
AX = mybir.AxisListType

NCORE = 8
BATCH = 2
LTOT = 4096
D = 4096
TS = LTOT // 4
DQ = D // 4
NDT = D // 128
NTT = TS // 128
DTR = 128
NST = 16
NEXP = 8
RK = 64
NH = 32
HD = 128
QL = 12
KVL = 4


def build_L1(upto=99, dbg=False):
    nc = bass.Bass("TRN2", target_bir_lowering=False)
    x_d = nc.dram_tensor("x", [TS, D], F32, kind="ExternalInput")
    xpw_d = nc.dram_tensor("xpw", [128, NDT * 160], F32, kind="ExternalInput")
    dtpw_d = nc.dram_tensor("dtpw", [DTR, D], F32, kind="ExternalInput")
    dtpb_d = nc.dram_tensor("dtpb", [NDT, 128], F32, kind="ExternalInput")
    dvec_d = nc.dram_tensor("dvec", [NDT, 128], F32, kind="ExternalInput")
    gw_d = nc.dram_tensor("gw", [128, NDT * NEXP], F32, kind="ExternalInput")
    lA_d = nc.dram_tensor("lA", [128, NEXP * NDT * RK], F32, kind="ExternalInput")
    lB_d = nc.dram_tensor("lB", [128, NEXP * D // 2], F32, kind="ExternalInput")
    idm_d = nc.dram_tensor("idm", [128, 128], F32, kind="ExternalInput")

    dt_o = nc.dram_tensor("dt_o", [D, TS], F32, kind="ExternalOutput")
    u_o = nc.dram_tensor("u_o", [D, TS], BF16, kind="ExternalOutput")
    bc_o = nc.dram_tensor("bc_o", [32, TS], F32, kind="ExternalOutput")
    md_o = nc.dram_tensor("md_o", [D, TS], BF16, kind="ExternalOutput")
    w_dram = nc.dram_tensor("w_scr", [NEXP, TS], F32, kind="Internal")
    if dbg:
        xn_o = nc.dram_tensor("xn_o", [128, NDT * TS], BF16, kind="ExternalOutput")
        gp_o = nc.dram_tensor("gp_o", [NEXP, TS], F32, kind="ExternalOutput")
        w_o = nc.dram_tensor("w_o", [NEXP, TS], F32, kind="ExternalOutput")
        t1_o = nc.dram_tensor("t1_o", [128, NEXP * TS // 2], BF16, kind="ExternalOutput")
        cols_o = nc.dram_tensor("cols_o", [128, 8 * NTT], F32, kind="ExternalOutput")

    xnT = nc.alloc_sbuf_tensor("xnT", [128, NDT * TS], BF16)
    xpw_s = nc.alloc_sbuf_tensor("xpw_s", [128, NDT * 160], BF16)
    dtpw_s = nc.alloc_sbuf_tensor("dtpw_s", [128, D], BF16)
    gw_s = nc.alloc_sbuf_tensor("gw_s", [128, NDT * NEXP], BF16)
    id_b = nc.alloc_sbuf_tensor("id_b", [128, 128], BF16)
    id_f = nc.alloc_sbuf_tensor("id_f", [128, 128], F32)
    cols = nc.alloc_sbuf_tensor("cols", [128, 8], F32)
    colsnap = nc.alloc_sbuf_tensor("colsnap", [128, 8 * NTT], F32)
    dtpb_r = nc.alloc_sbuf_tensor("dtpb_r", [NDT, 128], F32)
    dvec_r = nc.alloc_sbuf_tensor("dvec_r", [NDT, 128], F32)
    bcol_s = nc.alloc_sbuf_tensor("bcol_s", [128, NDT], F32)
    dcol_s = nc.alloc_sbuf_tensor("dcol_s", [128, NDT], F32)
    dt_in_s = nc.alloc_sbuf_tensor("dt_in_s", [128, TS], BF16)
    bcT_s = nc.alloc_sbuf_tensor("bcT_s", [32, TS], F32)
    g_sb = nc.alloc_sbuf_tensor("g_sb", [NEXP, TS], F32)
    gt_sb = nc.alloc_sbuf_tensor("gt_sb", [128, NEXP], F32)
    ge_sb = nc.alloc_sbuf_tensor("ge_sb", [128, NEXP], F32)
    ge2_sb = nc.alloc_sbuf_tensor("ge2_sb", [128, NEXP], F32)
    gw_tm = nc.alloc_sbuf_tensor("gw_tm", [128, NEXP], F32)
    gcol = nc.alloc_sbuf_tensor("gcol", [128, 8], F32)
    wT_s = nc.alloc_sbuf_tensor("wT_s", [NEXP, TS], F32)
    md_t = [nc.alloc_sbuf_tensor(f"md_t{i}", [128, TS], BF16) for i in range(2)]

    P0 = nc.alloc_psum_tensor("P0", [128, 1024], F32)
    P1 = nc.alloc_psum_tensor("P1", [128, 1024], F32)
    P2 = nc.alloc_psum_tensor("P2", [40, 1024], F32)
    P3 = nc.alloc_psum_tensor("P3", [128, 1024], BF16)
    P4 = nc.alloc_psum_tensor("P4", [128, 1024], BF16)
    p_tr = [P3, P4]

    s = {}
    for name in ["wld", "wld2", "xld", "c1", "c2", "xn", "tr", "trd", "g",
                 "gtr", "gtrd", "wrt", "mm1", "ev1", "mm3", "ev3", "mm4",
                 "ev4", "st4", "mm5", "ev5a", "ev5b", "act5", "st5", "stout"]:
        s[name] = nc.alloc_semaphore("s_" + name)

    with nc.Block() as blk:
        @blk.sync
        def _(sync):
            sync.dma_start(dtpb_r[:, :], dtpb_d[:, :]).then_inc(s["wld"], 16)
            sync.dma_start(dvec_r[:, :], dvec_d[:, :]).then_inc(s["wld"], 16)

        @blk.gpsimd
        def _(g):
            g.dma_start(id_b[:, :], idm_d[:, :]).then_inc(s["wld"], 16)
            g.dma_start(id_f[:, :], idm_d[:, :]).then_inc(s["wld"], 16)
            g.dma_start(xpw_s[:, :], xpw_d[:, :]).then_inc(s["wld"], 16)
            g.dma_start(dtpw_s[:, :], dtpw_d[:, :]).then_inc(s["wld"], 16)
            g.dma_start(gw_s[:, :], gw_d[:, :]).then_inc(s["wld"], 16)

    with nc.Block() as blk:
        @blk.tensor
        def _(t):
            t.wait_ge(s["wld"], 16 * 7)
            t.transpose(P0[:, 0:32], dtpb_r[:, :], id_f[0:32, 0:32])
            t.transpose(P0[:, 32:64], dvec_r[:, :], id_f[0:32, 0:32]).then_inc(s["tr"], 1)

        @blk.vector
        def _(v):
            v.wait_ge(s["tr"], 1)
            v.tensor_copy(bcol_s[:, :], P0[:, 0:32])
            v.tensor_copy(dcol_s[:, :], P0[:, 32:64]).then_inc(s["trd"], 1)

    if upto <= 0:
        return nc
    with contextlib.ExitStack() as stk:
        x_t = [stk.enter_context(nc.sbuf_tensor(f"x_t{i}", [128, D], F32)) for i in range(2)]
        xsq = stk.enter_context(nc.sbuf_tensor("xsq", [128, D], BF16))
        xn_t = [stk.enter_context(nc.sbuf_tensor(f"xn_t{i}", [128, D], BF16)) for i in range(2)]

        with nc.Block() as blk:
            @blk.sync
            def _(sync):
                for i in range(NTT):
                    if i >= 2:
                        sync.wait_ge(s["xn"], i - 1)
                    sync.dma_start(x_t[i % 2][:, :], x_d[i * 128:(i + 1) * 128, :]
                                   ).then_inc(s["xld"], 16)

            @blk.vector
            def _(v):
                for i in range(NTT):
                    v.wait_ge(s["c2"], 2 * i + 1)
                    v.tensor_scalar(cols[:, 1:2], cols[:, 0:1], 1.0 / D, None, ALU.mult)
                    v.tensor_scalar(cols[:, 3:4], cols[:, 2:3], 1.0 / D, None, ALU.mult)
                    v.drain()
                    v.tensor_mul(cols[:, 4:5], cols[:, 1:2], cols[:, 1:2])
                    v.drain()
                    v.tensor_sub(cols[:, 5:6], cols[:, 3:4], cols[:, 4:5])
                    v.drain()
                    v.tensor_scalar(cols[:, 5:6], cols[:, 5:6], 1e-5, None, ALU.add
                                    ).then_inc(s["c1"], 1)
                    xt = x_t[i % 2]
                    v.wait_ge(s["c2"], 2 * i + 2)
                    if i >= 1:
                        v.wait_ge(s["gtr"], i * NDT)
                    v.tensor_scalar(xn_t[i % 2][:, :], xt[:, :], cols[:, 1:2],
                                    cols[:, 7:8], ALU.subtract, ALU.mult
                                    ).then_inc(s["xn"], 1)
                    if dbg:
                        v.tensor_copy(colsnap[:, i * 8:(i + 1) * 8], cols[:, :])
                    for c in range(1, NDT, 2):
                        v.wait_ge(s["gtr"], i * NDT + c + 1)
                        v.tensor_copy(xnT[:, c * TS + i * 128: c * TS + (i + 1) * 128],
                                      p_tr[c % 2][:, 0:128]).then_inc(s["gtrd"], 1)

            @blk.scalar
            def _(sc):
                for i in range(NTT):
                    sc.wait_ge(s["xld"], 16 * (i + 1))
                    sc.activation(xsq[:, :], x_t[i % 2][:, :], AF.Copy,
                                  accum_out=cols[:, 0:1])
                    sc.activation(xsq[:, :], x_t[i % 2][:, :], AF.Square,
                                  accum_out=cols[:, 2:3]).then_inc(s["c2"], 1)
                    sc.wait_ge(s["c1"], i + 1)
                    sc.activation(cols[:, 6:7], cols[:, 5:6], AF.Ln)
                    sc.drain()
                    sc.activation(cols[:, 7:8], cols[:, 6:7], AF.Exp, scale=-0.5
                                  ).then_inc(s["c2"], 1)
                    for c in range(0, NDT, 2):
                        sc.wait_ge(s["gtr"], i * NDT + c + 1)
                        sc.copy(xnT[:, c * TS + i * 128: c * TS + (i + 1) * 128],
                                p_tr[c % 2][:, 0:128]).then_inc(s["gtrd"], 1)

            @blk.tensor
            def _(t):
                for i in range(NTT):
                    t.wait_ge(s["xn"], i + 1)
                    for c in range(NDT):
                        idx = i * NDT + c
                        if idx >= 2:
                            t.wait_ge(s["gtrd"], idx - 1)
                        t.transpose(p_tr[c % 2][:, 0:128],
                                    xn_t[i % 2][:, c * 128:(c + 1) * 128],
                                    id_b[:, :]).then_inc(s["gtr"], 1)

    if upto <= 1:
        return nc
    with nc.Block() as blk:
        @blk.tensor
        def _(t):
            for c in range(NDT):
                st_ = (c == 0)
                sp = (c == NDT - 1)
                rhs = xnT[:, c * TS:(c + 1) * TS]
                for li, (lhs, pp) in enumerate((
                        (xpw_s[:, c * 160:c * 160 + 128], P0),
                        (xpw_s[:, c * 160 + 128:(c + 1) * 160], P2[0:32, :]),
                        (gw_s[:, c * NEXP:(c + 1) * NEXP], P1[0:8, :]))):
                    for hf in range(2):
                        mm = t.matmul(pp[:, hf * 512:(hf + 1) * 512],
                                      lhs, rhs[:, hf * 512:(hf + 1) * 512],
                                      start=st_, stop=sp)
                        if c == NDT - 1 and li == 2 and hf == 1:
                            mm.then_inc(s["mm1"], 1)

        @blk.vector
        def _(v):
            v.wait_ge(s["mm1"], 1)
            v.tensor_copy(dt_in_s[:, :], P0[:, :])
            v.tensor_copy(bcT_s[:, :], P2[0:32, :])
            v.tensor_copy(g_sb[:, :], P1[0:8, :]).then_inc(s["ev1"], 1)

        @blk.sync
        def _(sync):
            sync.wait_ge(s["ev1"], 1)
            sync.dma_start(bc_o[:, :], bcT_s[:, :]).then_inc(s["stout"], 16)
            if dbg:
                sync.dma_start(xn_o[:, :], xnT[:, :]).then_inc(s["stout"], 16)
                sync.dma_start(cols_o[:, :], colsnap[:, :]).then_inc(s["stout"], 16)
                sync.dma_start(gp_o[:, :], g_sb[:, :]).then_inc(s["stout"], 16)

    if upto <= 2:
        return nc
    with contextlib.ExitStack() as stk:
        lA_s = stk.enter_context(nc.sbuf_tensor("lA_s", [128, NEXP * NDT * RK], BF16))
        lB_s = stk.enter_context(nc.sbuf_tensor("lB_s", [128, NEXP * D // 2], BF16))
        w_bc = stk.enter_context(nc.sbuf_tensor("w_bc", [128, NEXP * TS // 2], BF16))
        t1w_s = stk.enter_context(nc.sbuf_tensor("t1w_s", [128, NEXP * TS // 2], BF16))

        def lBsl(e, c):
            return lB_s[64 * (e % 2):64 * (e % 2) + 64,
                        (e // 2) * D + c * 128:(e // 2) * D + (c + 1) * 128]

        def wbsl(e):
            return w_bc[64 * (e % 2):64 * (e % 2) + 64, (e // 2) * TS:(e // 2 + 1) * TS]

        def t1sl(e):
            return t1w_s[64 * (e % 2):64 * (e % 2) + 64, (e // 2) * TS:(e // 2 + 1) * TS]

        with nc.Block() as blk:
            @blk.gpsimd
            def _(g):
                g.dma_start(lA_s[:, :], lA_d[:, :]).then_inc(s["wld2"], 16)
                g.dma_start(lB_s[:, :], lB_d[:, :]).then_inc(s["wld2"], 16)

            @blk.tensor
            def _(t):
                for j in range(NTT):
                    if j >= 2:
                        t.wait_ge(s["g"], 7 * (j - 1))
                    bank = P1[:, 0:512] if j % 2 == 0 else P1[:, 512:1024]
                    t.transpose(bank[:, 0:8], g_sb[:, j * 128:(j + 1) * 128],
                                id_f[0:8, 0:8]).then_inc(s["tr"], 1)
                    t.wait_ge(s["g"], 7 * j + 6)
                    t.transpose(bank[0:8, 256:384], gw_tm[:, :], id_f[:, :]
                                ).then_inc(s["tr"], 1)

            @blk.vector
            def _(v):
                tr0 = 1
                for j in range(NTT):
                    bank = P1[:, 0:512] if j % 2 == 0 else P1[:, 512:1024]
                    v.wait_ge(s["tr"], tr0 + 2 * j + 1)
                    v.tensor_copy(gt_sb[:, :], bank[:, 0:8])
                    v.drain()
                    v.sem_inc(s["g"], 1)
                    v.reduce_max(gcol[:, 0:1], gt_sb[:, :], axis=AX.X, negate=True
                                 ).then_inc(s["g"], 1)
                    v.wait_ge(s["g"], 7 * j + 3)
                    v.max(ge2_sb[:, :], ge_sb[:, :]).then_inc(s["g"], 1)
                    v.wait_ge(s["g"], 7 * j + 4)
                    v.tensor_scalar(gw_tm[:, :], ge_sb[:, :], ge2_sb[:, 1:2], None,
                                    ALU.is_ge)
                    v.drain()
                    v.tensor_mul(gw_tm[:, :], gw_tm[:, :], ge_sb[:, :])
                    v.tensor_add(gcol[:, 3:4], ge2_sb[:, 0:1], ge2_sb[:, 1:2])
                    v.drain()
                    v.reciprocal(gcol[:, 4:5], gcol[:, 3:4]).then_inc(s["g"], 1)
                    v.wait_ge(s["g"], 7 * j + 5)
                    v.tensor_scalar(gw_tm[:, :], gw_tm[:, :], gcol[:, 4:5], None,
                                    ALU.mult).then_inc(s["g"], 1)
                    v.wait_ge(s["tr"], tr0 + 2 * j + 2)
                    v.tensor_copy(wT_s[:, j * 128:(j + 1) * 128], bank[0:8, 256:384]
                                  ).then_inc(s["g"], 1)

            @blk.scalar
            def _(sc):
                for j in range(NTT):
                    sc.wait_ge(s["g"], 7 * j + 2)
                    sc.activation(ge_sb[:, :], gt_sb[:, :], AF.Exp, bias=gcol[:, 0:1]
                                  ).then_inc(s["g"], 1)

        if upto <= 3:
            return nc
        with nc.Block() as blk:
            @blk.sync
            def _(sync):
                sync.dma_start(w_dram[:, :], wT_s[:, :]).then_inc(s["wrt"], 16)
                if dbg:
                    sync.dma_start(w_o[:, :], wT_s[:, :]).then_inc(s["stout"], 16)

            @blk.gpsimd
            def _(g):
                g.wait_ge(s["wrt"], 16)
                for e in range(NEXP):
                    src = bass.AP(w_dram, e * TS, [[0, 64], [1, TS]])
                    g.dma_start(wbsl(e), src).then_inc(s["wrt"], 16)

        if upto <= 4:
            return nc
        with nc.Block() as blk:
            @blk.tensor
            def _(t):
                t.wait_ge(s["wrt"], 16 * (1 + NEXP))
                t.wait_ge(s["wld2"], 16)
                for e in range(NEXP):
                    pt = P0 if e % 2 == 0 else P1
                    if e >= 2:
                        t.wait_ge(s["ev3"], e - 1)
                    for c in range(NDT):
                        lhs = lA_s[:, (e * NDT + c) * RK:(e * NDT + c + 1) * RK]
                        for hf in range(2):
                            mm = t.matmul(pt[0:64, hf * 512:(hf + 1) * 512], lhs,
                                          xnT[:, c * TS + hf * 512:c * TS + (hf + 1) * 512],
                                          start=(c == 0), stop=(c == NDT - 1))
                            if c == NDT - 1 and hf == 1:
                                mm.then_inc(s["mm3"], 1)

            @blk.vector
            def _(v):
                for e in range(NEXP):
                    v.wait_ge(s["mm3"], e + 1)
                    pt = P0 if e % 2 == 0 else P1
                    v.tensor_mul(t1sl(e), pt[0:64, :], wbsl(e)).then_inc(s["ev3"], 1)

        if upto <= 5:
            return nc
        if dbg:
            with nc.Block() as blk:
                @blk.sync
                def _(sync):
                    sync.dma_start(t1_o[:, :], t1w_s[:, :]).then_inc(s["stout"], 16)

        with nc.Block() as blk:
            @blk.tensor
            def _(t):
                t.wait_ge(s["wld2"], 32)
                for c in range(NDT):
                    pt = P0 if c % 2 == 0 else P1
                    if c >= 2:
                        t.wait_ge(s["ev4"], c - 1)
                    for p in range(NEXP // 2):
                        lhs = lB_s[:, p * D + c * 128:p * D + (c + 1) * 128]
                        rhs_p = t1w_s[:, p * TS:(p + 1) * TS]
                        for hf in range(2):
                            mm = t.matmul(pt[:, hf * 512:(hf + 1) * 512], lhs,
                                          rhs_p[:, hf * 512:(hf + 1) * 512],
                                          start=(p == 0), stop=(p == NEXP // 2 - 1))
                            if p == NEXP // 2 - 1 and hf == 1:
                                mm.then_inc(s["mm4"], 1)

            @blk.vector
            def _(v):
                for c in range(NDT):
                    v.wait_ge(s["mm4"], c + 1)
                    if c >= 2:
                        v.wait_ge(s["st4"], 16 * (c - 1))
                    pt = P0 if c % 2 == 0 else P1
                    v.scalar_tensor_tensor(md_t[c % 2][:, :],
                                           xnT[:, c * TS:(c + 1) * TS],
                                           dcol_s[:, c:c + 1], pt[:, :],
                                           ALU.mult, ALU.add).then_inc(s["ev4"], 1)

            @blk.sync
            def _(sync):
                for c in range(NDT):
                    sync.wait_ge(s["ev4"], c + 1)
                    sync.dma_start(md_o[c * 128:(c + 1) * 128, :], md_t[c % 2][:, :]
                                   ).then_inc(s["st4"], 16)

    if upto <= 6:
        return nc
    with contextlib.ExitStack() as stk5:
        zb_abs = [stk5.enter_context(nc.sbuf_tensor(f"zb_abs{i}", [128, TS], F32)) for i in range(2)]
        zb_rel = [stk5.enter_context(nc.sbuf_tensor(f"zb_rel{i}", [128, TS], F32)) for i in range(2)]
        e_t = stk5.enter_context(nc.sbuf_tensor("e_t", [128, TS], F32))
        ee_t = stk5.enter_context(nc.sbuf_tensor("ee_t", [128, TS], F32))
        l_t = [stk5.enter_context(nc.sbuf_tensor(f"l_t{i}", [128, TS], F32)) for i in range(2)]
        dt_t = [stk5.enter_context(nc.sbuf_tensor(f"dt_t{i}", [128, TS], F32)) for i in range(2)]
        u_t = [stk5.enter_context(nc.sbuf_tensor(f"u_t{i}", [128, TS], BF16)) for i in range(2)]

        with nc.Block() as blk:
            @blk.tensor
            def _(t):
                for c in range(NDT):
                    pt = P0 if c % 2 == 0 else P1
                    if c >= 2:
                        t.wait_ge(s["ev5a"], c - 1)
                    lhs = dtpw_s[:, c * 128:(c + 1) * 128]
                    for hf in range(2):
                        mm = t.matmul(pt[:, hf * 512:(hf + 1) * 512], lhs,
                                      dt_in_s[:, hf * 512:(hf + 1) * 512],
                                      start=True, stop=True)
                        if hf == 1:
                            mm.then_inc(s["mm5"], 1)

            @blk.vector
            def _(v):
                for c in range(NDT):
                    v.wait_ge(s["mm5"], c + 1)
                    if c >= 2:
                        v.wait_ge(s["st5"], 32 * (c - 1))
                    pt = P0 if c % 2 == 0 else P1
                    v.tensor_scalar(zb_abs[c % 2][:, :], pt[:, :],
                                    bcol_s[:, c:c + 1], None, ALU.add)
                    v.tensor_scalar_max(zb_rel[c % 2][:, :], zb_abs[c % 2][:, :], 0.0
                                        ).then_inc(s["ev5a"], 1)
                    v.wait_ge(s["act5"], c + 1)
                    v.tensor_add(dt_t[c % 2][:, :], zb_rel[c % 2][:, :],
                                 l_t[c % 2][:, :])
                    v.tensor_mul(u_t[c % 2][:, :], dt_t[c % 2][:, :],
                                 xnT[:, c * TS:(c + 1) * TS]).then_inc(s["ev5b"], 1)

            @blk.scalar
            def _(sc):
                for c in range(NDT):
                    sc.wait_ge(s["ev5a"], c + 1)
                    sc.activation(e_t[:, :], zb_abs[c % 2][:, :], AF.Abs)
                    sc.activation(ee_t[:, :], e_t[:, :], AF.Exp, scale=-1.0)
                    sc.activation(l_t[c % 2][:, :], ee_t[:, :], AF.Ln, bias=1.0
                                  ).then_inc(s["act5"], 1)

            @blk.sync
            def _(sync):
                for c in range(NDT):
                    sync.wait_ge(s["ev5b"], c + 1)
                    sync.dma_start(dt_o[c * 128:(c + 1) * 128, :], dt_t[c % 2][:, :]
                                   ).then_inc(s["st5"], 16)
                    sync.dma_start(u_o[c * 128:(c + 1) * 128, :], u_t[c % 2][:, :]
                                   ).then_inc(s["st5"], 16)


    with nc.Block() as blk:
        @blk.sync
        def _(sync):
            sync.wait_ge(s["stout"], 16)
            sync.wait_ge(s["st4"], 16 * NDT)
            sync.wait_ge(s["st5"], 32 * NDT)

    return nc


def host_L1_inputs(inp, core):
    b = core // 4
    ts = (core % 4) * TS
    f = np.ascontiguousarray
    A = inp["L0_ffn_lora_A"]
    Bm = inp["L0_ffn_lora_B"]
    return dict(
        x=f(inp["L0_attn_x"][b, ts:ts + TS, :]),
        xpw=f(inp["L0_attn_x_proj_w"].reshape(160, NDT, 128).transpose(2, 1, 0).reshape(128, -1)),
        dtpw=f(inp["L0_attn_dt_proj_w"].T),
        dtpb=f(inp["L0_attn_dt_proj_b"].reshape(NDT, 128)),
        dvec=f(inp["L0_attn_D"].reshape(NDT, 128)),
        gw=f(inp["L0_ffn_gate_w"].reshape(NEXP, NDT, 128).transpose(2, 1, 0).reshape(128, -1)),
        lA=f(A.reshape(NEXP, RK, NDT, 128).transpose(3, 0, 2, 1).reshape(128, -1)),
        lB=f(Bm.reshape(NEXP // 2, 2, D, RK).transpose(1, 3, 0, 2).reshape(128, -1)),
        idm=np.eye(128, dtype=np.float32),
    )


def _f(a):
    return np.ascontiguousarray(a)


def kernel(**inp):
    inp = {k: np.asarray(v) for k, v in inp.items()}
    f = _f
    ident = np.eye(128, dtype=np.float32)
    ones = np.ones((1, TS), np.float32)

    nc1 = build_L1()
    maps1 = [host_L1_inputs(inp, c) for c in range(NCORE)]
    r1 = run_bass_kernel_spmd(nc1, maps1, core_ids=list(range(NCORE))).results

    dt_g = [np.hstack([r1[4 * b + j]["dt_o"] for j in range(4)]) for b in range(2)]
    u_g = [np.hstack([r1[4 * b + j]["u_o"] for j in range(4)]) for b in range(2)]
    bc_g = [np.hstack([r1[4 * b + j]["bc_o"] for j in range(4)]) for b in range(2)]
    md_g = [np.hstack([r1[4 * b + j]["md_o"] for j in range(4)]) for b in range(2)]
    nc2 = build_L2()
    maps2 = []
    for c in range(NCORE):
        b, q = c // 4, c % 4
        maps2.append(dict(
            dt=f(dt_g[b][q * DQ:(q + 1) * DQ]),
            u=f(u_g[b][q * DQ:(q + 1) * DQ]),
            bc=f(bc_g[b]),
            alog=f(inp["L0_attn_A_log"][q * DQ:(q + 1) * DQ]),
        ))
    r2 = run_bass_kernel_spmd(nc2, maps2, core_ids=list(range(NCORE))).results
    y_g = [np.vstack([r2[4 * b + q]["y_o"] for q in range(4)]) for b in range(2)]

    Wkv = inp["L1_attn_Wkv_d"]
    Wvu = inp["L1_attn_Wv_u"]
    Wo = inp["L1_attn_Wo"]
    nc3 = build_L3()
    maps3 = []
    for c in range(NCORE):
        b, ts = c // 4, (c % 4) * TS
        sl = (c % 4)
        maps3.append(dict(
            x=f(inp["L0_attn_x"][b, ts:ts + TS]),
            yt=f(y_g[b][:, ts:ts + TS].T),
            mdt=f(md_g[b][:, ts:ts + TS].T),
            wkv=f(Wkv.reshape(NDT, 128, KVL).transpose(1, 0, 2).reshape(128, -1)),
            bkv=f(inp["L1_attn_bkv_d"][:, None]),
            ones=ones, idm=ident,
            wvu=f(Wvu.T.reshape(NDT, 128, KVL).transpose(1, 0, 2).reshape(128, -1)),
            wocol=f(Wo[:, sl * TS:(sl + 1) * TS]),
            bvu=f(inp["L1_attn_bv_u"].reshape(NDT, 128).T),
            bo=f(inp["L1_attn_bo"][None, sl * TS:(sl + 1) * TS]),
        ))
    r3 = run_bass_kernel_spmd(nc3, maps3, core_ids=list(range(NCORE))).results

    ckvT_g = [np.hstack([np.asarray(r3[4 * b + q]["ckv_o"]) for q in range(4)])
              for b in range(2)]
    w2_full = np.hstack([np.asarray(r3[q]["w2s_o"]) for q in range(4)])
    c0_full = np.hstack([np.asarray(r3[q]["c0_o"]) for q in range(4)])[0]
    bd8 = np.zeros((NEXP, 128), np.float32)
    for h in range(NEXP):
        bd8[h, h * 16:(h + 1) * 16] = 1.0
    bdq = np.zeros((NH, 128), np.float32)
    for h in range(NH):
        bdq[h, h * 4:(h + 1) * 4] = 1.0
    rep4 = np.zeros((KVL, 128), np.float32)
    for h in range(NH):
        for l in range(KVL):
            rep4[l, h * 4 + l] = 1.0
    bd32 = np.zeros((128, NH), np.float32)
    for h in range(NH):
        bd32[h * 4:(h + 1) * 4, h] = 1.0
    wqr = np.zeros((128, NDT, 40), np.float32)
    wqr[:, :, 0:QL] = inp["L1_attn_Wq_d"].reshape(NDT, 128, QL).transpose(1, 0, 2)
    wqr[:, :, 32:40] = inp["L1_ffn_Wr"].T.reshape(NDT, 128, NEXP).transpose(1, 0, 2)
    nc4 = build_L4()
    maps4 = []
    for c in range(NCORE):
        b = c // 4
        ckvT_b = ckvT_g[b]
        maps4.append(dict(
            x1=f(np.asarray(r3[c]["x1_o"])),
            xnT=f(np.asarray(r3[c]["xnT_o"])),
            ckvT=f(ckvT_b),
            ckvtm=f(ckvT_b.T.reshape(32, 128, 5).transpose(1, 0, 2).reshape(128, -1)),
            w2=f(w2_full),
            c0p=f(c0_full.reshape(NDT, 128).T),
            l1a=f(inp["L1_ffn_A"].reshape(128, D).T.reshape(NDT, 128, 128)
                  .transpose(1, 0, 2).reshape(128, -1)),
            l1b=f(inp["L1_ffn_B"].transpose(0, 2, 1).reshape(128, D)),
            wqr=f(wqr.reshape(128, -1)),
            wqka=f(np.concatenate([inp["L1_attn_W_qk"],
                                   inp["L1_attn_b_qk"][None, :]], 0)),
            bq=f(inp["L1_attn_bq_d"][:, None]),
            br=f(inp["L1_ffn_br"][:, None]),
            ones=ones, idm=ident,
            bd8=bd8, bdq=bdq, rep4=rep4, bd32=bd32,
        ))
    r4 = run_bass_kernel_spmd(nc4, maps4, core_ids=list(range(NCORE))).results
    out = np.stack([np.vstack([np.asarray(r4[4 * b + q]["out_o"], np.float32)
                               for q in range(4)]) for b in range(2)], 0)
    return out


def build_L2(dbg=False):
    nc = bass.Bass("TRN2", target_bir_lowering=False)
    dt_d = nc.dram_tensor("dt", [DQ, LTOT], F32, kind="ExternalInput")
    u_d = nc.dram_tensor("u", [DQ, LTOT], BF16, kind="ExternalInput")
    bc_d = nc.dram_tensor("bc", [32, LTOT], F32, kind="ExternalInput")
    alog_d = nc.dram_tensor("alog", [DQ, NST], F32, kind="ExternalInput")
    y_o = nc.dram_tensor("y_o", [DQ, LTOT], BF16, kind="ExternalOutput")
    if dbg:
        dbg_a = nc.dram_tensor("dbg_a", [128, LTOT], F32, kind="ExternalOutput")
        dbg_b = nc.dram_tensor("dbg_b", [128, LTOT], F32, kind="ExternalOutput")
        dbg_h = nc.dram_tensor("dbg_h", [128, LTOT], BF16, kind="ExternalOutput")
        dbg_B = nc.dram_tensor("dbg_B", [128, LTOT], BF16, kind="ExternalOutput")

    NJ = 2
    NB = (DQ // 128) // NJ

    A_s = nc.alloc_sbuf_tensor("A_s", [128, (DQ // 128) * NST], F32)
    dt_s = [nc.alloc_sbuf_tensor(f"dt_s{j}", [128, LTOT], F32) for j in range(NJ)]
    u_s = [nc.alloc_sbuf_tensor(f"u_s{j}", [128, LTOT], BF16) for j in range(NJ)]
    Bbc = [nc.alloc_sbuf_tensor(f"Bbc{i}", [128, LTOT], BF16) for i in range(2)]
    Cbc = [nc.alloc_sbuf_tensor(f"Cbc{i}", [128, LTOT], BF16) for i in range(2)]
    dA_s = [nc.alloc_sbuf_tensor(f"dA_s{i}", [128, LTOT], F32) for i in range(2)]
    dBx_s = [nc.alloc_sbuf_tensor(f"dBx_s{i}", [128, LTOT], F32) for i in range(2)]
    h_s = [nc.alloc_sbuf_tensor(f"h_s{i}", [128, LTOT], BF16) for i in range(2)]
    tmp_s = nc.alloc_sbuf_tensor("tmp_s", [128, LTOT], BF16)
    y_s = [nc.alloc_sbuf_tensor(f"y_s{j}", [128, LTOT], F32) for j in range(NJ)]

    s = {}
    for name in ["ald", "dld", "bld", "act", "pool", "scan", "y", "st", "dbgv", "dbgs", "fence"]:
        s[name] = nc.alloc_semaphore("s_" + name)

    with nc.Block() as blk:
        @blk.sync
        def _(sync):
            src = bass.AP(alog_d, 0, [[NST, 128], [128 * NST, DQ // 128], [1, NST]])
            dst = A_s[:, :].rearrange("p (c n) -> p c n", c=DQ // 128)
            sync.dma_start(dst, src).then_inc(s["ald"], 16)

        @blk.scalar
        def _(sc):
            sc.wait_ge(s["ald"], 16)
            sc.activation(A_s[:, :], A_s[:, :], AF.Exp).then_inc(s["act"], 1)

        @blk.vector
        def _(v):
            v.wait_ge(s["act"], 1)
            v.tensor_scalar(A_s[:, :], A_s[:, :], -1.0, None, ALU.mult)

    with nc.Block() as blk:
        @blk.sync
        def _(sync):
            for b in range(NB):
                if b >= 1:
                    sync.wait_ge(s["st"], 16 * NJ * b)
                for j in range(NJ):
                    r0 = (b * NJ + j) * 128
                    sync.dma_start(dt_s[j][:, :], dt_d[r0:r0 + 128, :]).then_inc(s["dld"], 16)
                    sync.dma_start(u_s[j][:, :], u_d[r0:r0 + 128, :]).then_inc(s["dld"], 16)

        @blk.gpsimd
        def _(g):
            it = 0
            for b in range(NB):
                for n in range(NST):
                    gi = b * NST + n
                    if gi >= 2:
                        g.wait_ge(s["scan"], (gi - 2) * NJ + NJ)
                    g.dma_start(Bbc[gi % 2][:, :],
                                bass.AP(bc_d, n * LTOT, [[0, 128], [1, LTOT]])
                                ).then_inc(s["bld"], 16)
                    g.dma_start(Cbc[gi % 2][:, :],
                                bass.AP(bc_d, (16 + n) * LTOT, [[0, 128], [1, LTOT]])
                                ).then_inc(s["bld"], 16)
                    for j in range(NJ):
                        g.wait_ge(s["dld"], 32 * (b * NJ + j) + 32)
                        g.wait_ge(s["bld"], 32 * gi + 16)
                        if it >= 2:
                            g.wait_ge(s["scan"], it - 1)
                        g.tensor_tensor(dBx_s[it % 2][:, :], u_s[j][:, :],
                                        Bbc[gi % 2][:, :], ALU.mult
                                        ).then_inc(s["pool"], 1)
                        it += 1
                for j in range(NJ):
                    g.wait_ge(s["y"], b * NJ + j + 1)
                    r0 = (b * NJ + j) * 128
                    g.dma_start(y_o[r0:r0 + 128, :], y_s[j][:, :]).then_inc(s["st"], 16)
            g.wait_ge(s["st"], 16 * NJ * NB)

        @blk.scalar
        def _(sc):
            it = 0
            for b in range(NB):
                for n in range(NST):
                    for j in range(NJ):
                        sc.wait_ge(s["dld"], 32 * (b * NJ + j) + 32)
                        if it >= 2:
                            sc.wait_ge(s["scan"], it - 1)
                        c_glob = b * NJ + j
                        sc.activation(dA_s[it % 2][:, :], dt_s[j][:, :], AF.Exp,
                                      scale=A_s[:, c_glob * NST + n:c_glob * NST + n + 1]
                                      ).then_inc(s["act"], 1)
                        if dbg and it == 1:
                            sc.wait_ge(s["dbgv"], 1)
                            sc.dma_start(dbg_a[:, :], dA_s[0][:, :]).then_inc(s["dbgs"], 16)
                            sc.dma_start(dbg_b[:, :], dBx_s[0][:, :]).then_inc(s["dbgs"], 16)
                            sc.dma_start(dbg_h[:, :], h_s[0][:, :]).then_inc(s["dbgs"], 16)
                            sc.dma_start(dbg_B[:, :], Bbc[0][:, :]).then_inc(s["dbgs"], 16)
                        it += 1

        @blk.vector
        def _(v):
            it = 0
            for b in range(NB):
                if b >= 1:
                    v.wait_ge(s["st"], 16 * NJ * b)
                for j in range(NJ):
                    v.memset(y_s[j][:, :], 0.0)
                for n in range(NST):
                    gi = b * NST + n
                    for j in range(NJ):
                        v.wait_ge(s["act"], 1 + it + 1)
                        v.wait_ge(s["pool"], it + 1)
                        v.tensor_tensor_scan(h_s[it % 2][:, :], dA_s[it % 2][:, :],
                                             dBx_s[it % 2][:, :], 0.0,
                                             ALU.mult, ALU.add).then_inc(s["fence"], 1)
                        v.wait_ge(s["fence"], 2 * it + 1)
                        v.wait_ge(s["bld"], 32 * gi + 32)
                        v.tensor_mul(tmp_s[:, :], h_s[it % 2][:, :], Cbc[gi % 2][:, :]
                                     ).then_inc(s["fence"], 1)
                        v.wait_ge(s["fence"], 2 * it + 2)
                        v.tensor_add(y_s[j][:, :], y_s[j][:, :], tmp_s[:, :]
                                     ).then_inc(s["scan"], 1)
                        if dbg and it == 0:
                            v.sem_inc(s["dbgv"], 1)
                            v.wait_ge(s["dbgs"], 64)
                        it += 1
                for j in range(NJ):
                    v.sem_inc(s["y"], 1)



    with nc.Block() as blk:
        pass

    return nc


def build_L3(upto=99):
    nc = bass.Bass("TRN2", target_bir_lowering=False)
    x_d = nc.dram_tensor("x", [TS, D], F32, kind="ExternalInput")
    yt_d = nc.dram_tensor("yt", [TS, D], BF16, kind="ExternalInput")
    mdt_d = nc.dram_tensor("mdt", [TS, D], BF16, kind="ExternalInput")
    wkv_d = nc.dram_tensor("wkv", [128, NDT * KVL], F32, kind="ExternalInput")
    bkv_d = nc.dram_tensor("bkv", [KVL, 1], F32, kind="ExternalInput")
    ones_d = nc.dram_tensor("ones", [1, TS], F32, kind="ExternalInput")
    idm_d = nc.dram_tensor("idm", [128, 128], F32, kind="ExternalInput")
    wvu_d = nc.dram_tensor("wvu", [128, NH * KVL], F32, kind="ExternalInput")
    wocol_d = nc.dram_tensor("wocol", [D, TS], F32, kind="ExternalInput")
    bvu_d = nc.dram_tensor("bvu", [128, NDT], F32, kind="ExternalInput")
    bo_d = nc.dram_tensor("bo", [1, TS], F32, kind="ExternalInput")

    x1_o = nc.dram_tensor("x1_o", [TS, D], F32, kind="ExternalOutput")
    xnT_o = nc.dram_tensor("xnT_o", [D, TS], BF16, kind="ExternalOutput")
    ckv_o = nc.dram_tensor("ckv_o", [KVL + 1, TS], F32, kind="ExternalOutput")
    w2s_o = nc.dram_tensor("w2s_o", [128, TS], F32, kind="ExternalOutput")
    c0_o = nc.dram_tensor("c0_o", [1, TS], F32, kind="ExternalOutput")

    xnT = nc.alloc_sbuf_tensor("xnT", [128, NDT * TS], BF16)
    wkv_s = nc.alloc_sbuf_tensor("wkv_s", [128, NDT * KVL], BF16)
    bkvc_s = nc.alloc_sbuf_tensor("bkvc_s", [KVL, 1], F32)
    ones_s = nc.alloc_sbuf_tensor("ones_s", [1, TS], BF16)
    id_b = nc.alloc_sbuf_tensor("id_b", [128, 128], BF16)
    id_f = nc.alloc_sbuf_tensor("id_f", [128, 128], F32)
    cols = nc.alloc_sbuf_tensor("cols", [128, 8], F32)
    x_t = [nc.alloc_sbuf_tensor(f"x_t{i}", [128, D], F32) for i in range(2)]
    yt_t = nc.alloc_sbuf_tensor("yt_t", [128, D], BF16)
    mdt_t = nc.alloc_sbuf_tensor("mdt_t", [128, D], BF16)
    xsq = nc.alloc_sbuf_tensor("xsq", [128, D], BF16)
    xn_t = [nc.alloc_sbuf_tensor(f"xn_t{i}", [128, D], BF16) for i in range(2)]
    ckv_sb = nc.alloc_sbuf_tensor("ckv_sb", [KVL + 1, TS], F32)
    wvu_s = nc.alloc_sbuf_tensor("wvu_s", [128, NH * KVL], BF16)
    wo_t = [nc.alloc_sbuf_tensor(f"wo_t{i}", [128, TS], BF16) for i in range(2)]
    bvu_s = nc.alloc_sbuf_tensor("bvu_s", [128, NDT], BF16)
    w2row = nc.alloc_sbuf_tensor("w2row", [KVL, TS], F32)
    c0row = nc.alloc_sbuf_tensor("c0row", [1, TS], F32)
    bo_s = nc.alloc_sbuf_tensor("bo_s", [1, TS], F32)

    P0 = nc.alloc_psum_tensor("P0", [128, 1024], F32)
    P1 = nc.alloc_psum_tensor("P1", [128, 1024], F32)
    P2 = nc.alloc_psum_tensor("P2", [128, 1024], F32)
    P3 = nc.alloc_psum_tensor("P3", [128, 1024], BF16)
    P4 = nc.alloc_psum_tensor("P4", [128, 1024], BF16)
    p_tr = [P3, P4]

    s = {}
    for name in ["wld", "xld", "c1", "c2", "xn", "gtr", "gtrd", "x1st",
                 "mmk", "evk", "wol", "mmw", "evw", "st"]:
        s[name] = nc.alloc_semaphore("s_" + name)

    with nc.Block() as blk:
        @blk.sync
        def _(sync):
            sync.dma_start(bo_s[:, :], bo_d[:, :]).then_inc(s["wld"], 16)

        @blk.gpsimd
        def _(g):
            g.dma_start(wkv_s[:, :], wkv_d[:, :]).then_inc(s["wld"], 16)
            g.dma_start(bkvc_s[:, :], bkv_d[:, :]).then_inc(s["wld"], 16)
            g.dma_start(ones_s[:, :], ones_d[:, :]).then_inc(s["wld"], 16)
            g.dma_start(id_b[:, :], idm_d[:, :]).then_inc(s["wld"], 16)
            g.dma_start(id_f[:, :], idm_d[:, :]).then_inc(s["wld"], 16)
            g.dma_start(wvu_s[:, :], wvu_d[:, :]).then_inc(s["wld"], 16)
            g.dma_start(bvu_s[:, :], bvu_d[:, :]).then_inc(s["wld"], 16)

    with nc.Block() as blk:
        @blk.sync
        def _(sync):
            for i in range(NTT):
                if i >= 2:
                    sync.wait_ge(s["xn"], i - 1)
                    sync.wait_ge(s["x1st"], 16 * (i - 1))
                sync.dma_start(x_t[i % 2][:, :], x_d[i * 128:(i + 1) * 128, :]
                               ).then_inc(s["xld"], 16)
                sync.wait_ge(s["c2"], 2 * i)
                sync.dma_start(yt_t[:, :], yt_d[i * 128:(i + 1) * 128, :]
                               ).then_inc(s["xld"], 16)
                sync.dma_start(mdt_t[:, :], mdt_d[i * 128:(i + 1) * 128, :]
                               ).then_inc(s["xld"], 16)

        @blk.gpsimd
        def _(gx):
            for i in range(NTT):
                gx.wait_ge(s["xn"], i + 1)
                gx.dma_start(x1_o[i * 128:(i + 1) * 128, :], x_t[i % 2][:, :]
                             ).then_inc(s["x1st"], 16)

        @blk.vector
        def _(v):
            for i in range(NTT):
                v.wait_ge(s["xld"], 48 * (i + 1))
                xt = x_t[i % 2]
                v.tensor_add(xt[:, :], xt[:, :], yt_t[:, :])
                v.tensor_add(xt[:, :], xt[:, :], mdt_t[:, :]).then_inc(s["c2"], 1)
                v.wait_ge(s["c1"], 2 * i + 1)
                v.tensor_scalar(cols[:, 1:2], cols[:, 0:1], 1.0 / D, None, ALU.mult)
                v.tensor_scalar(cols[:, 3:4], cols[:, 2:3], 1.0 / D, None, ALU.mult)
                v.drain()
                v.tensor_mul(cols[:, 4:5], cols[:, 1:2], cols[:, 1:2])
                v.drain()
                v.tensor_sub(cols[:, 5:6], cols[:, 3:4], cols[:, 4:5])
                v.drain()
                v.tensor_scalar(cols[:, 5:6], cols[:, 5:6], 1e-5, None, ALU.add
                                ).then_inc(s["c2"], 1)
                v.wait_ge(s["c1"], 2 * i + 2)
                if i >= 1:
                    v.wait_ge(s["gtr"], i * NDT)
                v.tensor_scalar(xn_t[i % 2][:, :], xt[:, :], cols[:, 1:2],
                                cols[:, 7:8], ALU.subtract, ALU.mult
                                ).then_inc(s["xn"], 1)
                for c in range(1, NDT, 2):
                    v.wait_ge(s["gtr"], i * NDT + c + 1)
                    v.tensor_copy(xnT[:, c * TS + i * 128: c * TS + (i + 1) * 128],
                                  p_tr[c % 2][:, 0:128]).then_inc(s["gtrd"], 1)

        @blk.scalar
        def _(sc):
            for i in range(NTT):
                sc.wait_ge(s["c2"], 2 * i + 1)
                sc.activation(xsq[:, :], x_t[i % 2][:, :], AF.Copy,
                              accum_out=cols[:, 0:1])
                sc.activation(xsq[:, :], x_t[i % 2][:, :], AF.Square,
                              accum_out=cols[:, 2:3]).then_inc(s["c1"], 1)
                sc.wait_ge(s["c2"], 2 * i + 2)
                sc.activation(cols[:, 6:7], cols[:, 5:6], AF.Ln)
                sc.drain()
                sc.activation(cols[:, 7:8], cols[:, 6:7], AF.Exp, scale=-0.5
                              ).then_inc(s["c1"], 1)
                for c in range(0, NDT, 2):
                    sc.wait_ge(s["gtr"], i * NDT + c + 1)
                    sc.copy(xnT[:, c * TS + i * 128: c * TS + (i + 1) * 128],
                            p_tr[c % 2][:, 0:128]).then_inc(s["gtrd"], 1)

        @blk.tensor
        def _(t):
            for i in range(NTT):
                t.wait_ge(s["xn"], i + 1)
                for c in range(NDT):
                    idx = i * NDT + c
                    if idx >= 2:
                        t.wait_ge(s["gtrd"], idx - 1)
                    t.transpose(p_tr[c % 2][:, 0:128],
                                xn_t[i % 2][:, c * 128:(c + 1) * 128],
                                id_b[:, :]).then_inc(s["gtr"], 1)

    if upto <= 1:
        return nc
    with nc.Block() as blk:
        @blk.sync
        def _(sync):
            dst = bass.AP(xnT_o, 0, [[TS, 128], [128 * TS, NDT], [1, TS]])
            src = xnT[:, :].rearrange("p (c t) -> p c t", c=NDT)
            sync.dma_start(dst, src).then_inc(s["st"], 16)

        @blk.tensor
        def _(t):
            for c in range(NDT):
                lhs = wkv_s[:, c * KVL:(c + 1) * KVL]
                for hf in range(2):
                    mm = t.matmul(P0[0:KVL, hf * 512:(hf + 1) * 512], lhs,
                                  xnT[:, c * TS + hf * 512:c * TS + (hf + 1) * 512],
                                  start=(c == 0), stop=(c == NDT - 1))
                    if c == NDT - 1 and hf == 1:
                        mm.then_inc(s["mmk"], 1)

        @blk.vector
        def _(v):
            v.wait_ge(s["mmk"], 1)
            v.tensor_scalar(ckv_sb[0:KVL, :], P0[0:KVL, :], bkvc_s[:, :], None,
                            ALU.add).then_inc(s["evk"], 1)

        @blk.gpsimd
        def _(g):
            g.wait_ge(s["evk"], 1)
            g.dma_start(ckv_sb[KVL:KVL + 1, :], ones_d[:, :]).then_inc(s["evk"], 16)


        @blk.sync
        def _(sync2):
            pass

    if upto <= 2:
        return nc
    with nc.Block() as blk:
        @blk.gpsimd
        def _(g):
            for c in range(NDT):
                if c >= 2:
                    g.wait_ge(s["mmw"], c - 1)
                g.dma_start(wo_t[c % 2][:, :], wocol_d[c * 128:(c + 1) * 128, :]
                            ).then_inc(s["wol"], 16)

        @blk.sync
        def _(sync):
            sync.wait_ge(s["evk"], 17)
            sync.dma_start(ckv_o[:, :], ckv_sb[:, :]).then_inc(s["st"], 16)

        @blk.tensor
        def _(t):
            for c in range(NDT):
                t.wait_ge(s["wol"], 16 * (c + 1))
                if c >= 2:
                    t.wait_ge(s["evw"], c - 1)
                PW = P1 if c % 2 == 0 else P2
                for hf in range(2):
                    t.matmul(PW[0:KVL, hf * 512:(hf + 1) * 512],
                             wvu_s[:, c * KVL:(c + 1) * KVL],
                             wo_t[c % 2][:, hf * 512:(hf + 1) * 512],
                             start=True, stop=True)
                for hf in range(2):
                    mm = t.matmul(P0[0:1, hf * 512:(hf + 1) * 512],
                                  bvu_s[:, c:c + 1],
                                  wo_t[c % 2][:, hf * 512:(hf + 1) * 512],
                                  start=(c == 0), stop=(c == NDT - 1))
                    if hf == 1:
                        mm.then_inc(s["mmw"], 1)

        @blk.vector
        def _(v):
            for c in range(NDT):
                v.wait_ge(s["mmw"], c + 1)
                v.tensor_copy(w2row[:, :], (P1 if c % 2 == 0 else P2)[0:KVL, :]
                              ).then_inc(s["evw"], 1)
                v.wait_ge(s["st"], 48 + 16 * c)
            v.tensor_add(c0row[:, :], P0[0:1, :], bo_s[:, :]).then_inc(s["evw"], 1)

        @blk.sync
        def _(sync2):
            for c in range(NDT):
                sync2.wait_ge(s["evw"], c + 1)
                sync2.dma_start(w2s_o[c * KVL:(c + 1) * KVL, :], w2row[:, :]
                                ).then_inc(s["st"], 16)
            sync2.wait_ge(s["evw"], NDT + 1)
            sync2.dma_start(c0_o[:, :], c0row[:, :]).then_inc(s["st"], 16)

    with nc.Block() as blk:
        @blk.sync
        def _(sync):
            sync.wait_ge(s["st"], 16 * (1 + 1 + NDT + 1))
            sync.wait_ge(s["x1st"], 16 * NTT)

    return nc


def build_L4(upto=99, dbg=False):
    nc = bass.Bass("TRN2", target_bir_lowering=False)
    x1_d = nc.dram_tensor("x1", [TS, D], F32, kind="ExternalInput")
    xnT_d = nc.dram_tensor("xnT", [D, TS], BF16, kind="ExternalInput")
    ckvT_d = nc.dram_tensor("ckvT", [KVL + 1, LTOT], F32, kind="ExternalInput")
    ckvtm_d = nc.dram_tensor("ckvtm", [128, 32 * (KVL + 1)], F32, kind="ExternalInput")
    w2_d = nc.dram_tensor("w2", [128, D], F32, kind="ExternalInput")
    c0p_d = nc.dram_tensor("c0p", [128, NDT], F32, kind="ExternalInput")
    l1a_d = nc.dram_tensor("l1a", [128, NDT * 128], F32, kind="ExternalInput")
    l1b_d = nc.dram_tensor("l1b", [128, D], F32, kind="ExternalInput")
    wqr_d = nc.dram_tensor("wqr", [128, NDT * 40], F32, kind="ExternalInput")
    wqka_d = nc.dram_tensor("wqka", [QL + 1, 128], F32, kind="ExternalInput")
    bq_d = nc.dram_tensor("bq", [QL, 1], F32, kind="ExternalInput")
    br_d = nc.dram_tensor("br", [NEXP, 1], F32, kind="ExternalInput")
    ones_d = nc.dram_tensor("ones", [1, TS], F32, kind="ExternalInput")
    idm_d = nc.dram_tensor("idm", [128, 128], F32, kind="ExternalInput")
    bd8_d = nc.dram_tensor("bd8", [NEXP, 128], F32, kind="ExternalInput")
    bdq_d = nc.dram_tensor("bdq", [NH, 128], F32, kind="ExternalInput")
    rep4_d = nc.dram_tensor("rep4", [KVL, 128], F32, kind="ExternalInput")
    bd32_d = nc.dram_tensor("bd32", [128, NH], F32, kind="ExternalInput")
    out_o = nc.dram_tensor("out_o", [TS, D], F32, kind="ExternalOutput")
    q_dram = nc.dram_tensor("q_scr", [128, TS], BF16, kind="Internal")
    m_dram = nc.dram_tensor("m_scr", [NH, TS], BF16, kind="Internal")
    if dbg:
        dbg_o = {}
        for nm, shp, dt in [("lowT", [128, TS], BF16), ("cqT", [QL + 1, TS], BF16),
                            ("routT", [NEXP, TS], F32), ("scoresT", [NEXP, TS], BF16),
                            ("lowsc", [128, TS], BF16), ("qT05", [128, TS], BF16),
                            ("mneg", [NH, TS], F32), ("urawT", [128, TS], BF16),
                            ("dnT", [NH, TS], F32), ("unT", [128, TS], BF16)]:
            dbg_o[nm] = nc.dram_tensor("dbg_" + nm, shp, dt, kind="ExternalOutput")
        dbg_o["big"] = nc.dram_tensor("dbg_big", [128, NDT * TS], BF16, kind="ExternalOutput")
        dbg_o["l1a"] = nc.dram_tensor("dbg_l1a", [128, NDT * 128], BF16, kind="ExternalOutput")
        dbg_o["wqka"] = nc.dram_tensor("dbg_wqka", [QL + 1, 128], BF16, kind="ExternalOutput")
        dbg_o["qph"] = nc.dram_tensor("dbg_qph", [5, TS], BF16, kind="ExternalOutput")
        dbg_o["eb"] = nc.dram_tensor("dbg_eb", [128, 2 * TS], BF16, kind="ExternalOutput")
        dbg_o["cktm"] = nc.dram_tensor("dbg_cktm", [128, 32 * 5], BF16, kind="ExternalOutput")
        dbg_o["wqr"] = nc.dram_tensor("dbg_wqr", [128, NDT * 40], BF16, kind="ExternalOutput")
    u_dram = nc.dram_tensor("u_scr", [128, TS], F32, kind="Internal")
    d_dram = nc.dram_tensor("d_scr", [NH, TS], F32, kind="Internal")

    bigbuf = nc.alloc_sbuf_tensor("bigbuf", [128, NDT * TS], BF16)
    ckvT_s = nc.alloc_sbuf_tensor("ckvT_s", [KVL + 1, LTOT], BF16)
    ckvtm_s = nc.alloc_sbuf_tensor("ckvtm_s", [128, 32 * (KVL + 1)], BF16)
    w2_s = nc.alloc_sbuf_tensor("w2_s", [128, D], BF16)
    c0p_s = nc.alloc_sbuf_tensor("c0p_s", [128, NDT], F32)
    l1a_s = nc.alloc_sbuf_tensor("l1a_s", [128, NDT * 128], BF16)
    l1b_s = nc.alloc_sbuf_tensor("l1b_s", [128, D], BF16)
    wqr_s = nc.alloc_sbuf_tensor("wqr_s", [128, NDT * 40], BF16)
    wqka_s = nc.alloc_sbuf_tensor("wqka_s", [QL + 1, 128], BF16)
    bq_s = nc.alloc_sbuf_tensor("bq_s", [QL, 1], F32)
    br_s = nc.alloc_sbuf_tensor("br_s", [NEXP, 1], F32)
    ones_s = nc.alloc_sbuf_tensor("ones_s", [1, TS], BF16)
    id_f = nc.alloc_sbuf_tensor("id_f", [128, 128], F32)
    id_b = nc.alloc_sbuf_tensor("id_b", [128, 128], BF16)
    bd8_s = nc.alloc_sbuf_tensor("bd8_s", [NEXP, 128], BF16)
    bdq_s = nc.alloc_sbuf_tensor("bdq_s", [NH, 128], BF16)
    rep4_s = nc.alloc_sbuf_tensor("rep4_s", [KVL, 128], BF16)
    bd32_s = nc.alloc_sbuf_tensor("bd32_s", [128, NH], BF16)
    lowT = nc.alloc_sbuf_tensor("lowT", [128, TS], BF16)
    lowsc = nc.alloc_sbuf_tensor("lowsc", [128, TS], BF16)
    cqT = nc.alloc_sbuf_tensor("cqT", [QL + 1, TS], BF16)
    routT = nc.alloc_sbuf_tensor("routT", [NEXP, TS], F32)
    g_tm = nc.alloc_sbuf_tensor("g_tm", [128, NEXP], F32)
    ge_tm = nc.alloc_sbuf_tensor("ge_tm", [128, NEXP], F32)
    gcol = nc.alloc_sbuf_tensor("gcol", [128, 4], F32)
    scoresT = nc.alloc_sbuf_tensor("scoresT", [NEXP, TS], BF16)
    kc8 = nc.alloc_sbuf_tensor("kc8", [KVL, 8], F32)
    kabs_f = nc.alloc_sbuf_tensor("kabs_f", [KVL, 1], F32)
    kabs_b = nc.alloc_sbuf_tensor("kabs_b", [KVL, 1], BF16)
    kabscol = nc.alloc_sbuf_tensor("kabscol", [128, 1], F32)
    kabsbd = nc.alloc_sbuf_tensor("kabsbd", [128, NH], BF16)
    qph = [nc.alloc_sbuf_tensor(f"qph{i}", [5, TS], BF16) for i in range(2)]
    sfr = nc.alloc_sbuf_tensor("sfr", [128, 4 * TS], F32)
    expb = [nc.alloc_sbuf_tensor(f"expb{i}", [128, 2 * TS], BF16) for i in range(2)]
    u5 = [nc.alloc_sbuf_tensor(f"u5_{i}", [5, TS], F32) for i in range(2)]
    urawT = nc.alloc_sbuf_tensor("urawT", [128, TS], BF16)
    dnT = nc.alloc_sbuf_tensor("dnT", [NH, TS], F32)
    rc32 = nc.alloc_sbuf_tensor("rc32", [NH, TS], BF16)
    mneg2 = nc.alloc_sbuf_tensor("mneg2", [NH, TS], F32)
    unT = nc.alloc_sbuf_tensor("unT", [128, TS], BF16)
    x1t = nc.alloc_sbuf_tensor("x1t", [128, D], BF16)
    outt = nc.alloc_sbuf_tensor("outt", [128, D // 2], F32)

    G0 = nc.alloc_psum_tensor("G0", [128, 1024], F32)
    G1 = nc.alloc_psum_tensor("G1", [128, 1024], F32)

    s = {}
    for name in ["wld", "mm1", "ev1", "tr", "g", "mm2", "ev2", "mm3", "ev3",
                 "qst", "qld", "smm", "sev", "sact", "su", "sue", "ust",
                 "uld", "mm6", "ev6", "x1ld", "otr", "otrd", "ost", "f", "dbgf",
                 "sac5"]:
        s[name] = nc.alloc_semaphore("s_" + name)

    with nc.Block() as blk:
        @blk.sync
        def _(sync):
            src = bass.AP(xnT_d, 0, [[TS, 128], [128 * TS, NDT], [1, TS]])
            dst = bigbuf[:, :].rearrange("p (c t) -> p c t", c=NDT)
            sync.dma_start(dst, src).then_inc(s["wld"], 16)
            sync.dma_start(c0p_s[:, :], c0p_d[:, :]).then_inc(s["wld"], 16)
            sync.dma_start(bq_s[:, :], bq_d[:, :]).then_inc(s["wld"], 16)
            sync.dma_start(br_s[:, :], br_d[:, :]).then_inc(s["wld"], 16)
            sync.dma_start(id_f[:, :], idm_d[:, :]).then_inc(s["wld"], 16)

        @blk.gpsimd
        def _(g):
            for dst, srcd in ((ckvT_s, ckvT_d), (ckvtm_s, ckvtm_d), (w2_s, w2_d),
                              (l1a_s, l1a_d), (l1b_s, l1b_d), (wqr_s, wqr_d),
                              (wqka_s, wqka_d), (ones_s, ones_d), (id_b, idm_d),
                              (bd8_s, bd8_d), (bdq_s, bdq_d), (rep4_s, rep4_d),
                              (bd32_s, bd32_d)):
                g.dma_start(dst[:, :], srcd[:, :]).then_inc(s["wld"], 16)

    if upto <= 0:
        return nc
    if dbg:
        with nc.Block() as blk:
            @blk.sync
            def _(sync):
                sync.wait_ge(s["wld"], 16 * 18)
                sync.dma_start(dbg_o["big"][:, :], bigbuf[:, :]).then_inc(s["dbgf"], 16)
                sync.dma_start(dbg_o["l1a"][:, :], l1a_s[:, :]).then_inc(s["dbgf"], 16)
                sync.dma_start(dbg_o["wqr"][:, :], wqr_s[:, :]).then_inc(s["dbgf"], 16)
                sync.wait_ge(s["dbgf"], 48)
    with nc.Block() as blk:
        @blk.tensor
        def _(t):
            t.wait_ge(s["wld"], 16 * 18)
            for c in range(NDT):
                st_, sp = (c == 0), (c == NDT - 1)
                rhs = bigbuf[:, c * TS:(c + 1) * TS]
                for hf in range(2):
                    r = rhs[:, hf * 512:(hf + 1) * 512]
                    t.matmul(G0[:, hf * 512:(hf + 1) * 512],
                             l1a_s[:, c * 128:(c + 1) * 128], r, start=st_, stop=sp)
                    mm = t.matmul(G1[0:40, hf * 512:(hf + 1) * 512],
                                  wqr_s[:, c * 40:(c + 1) * 40], r, start=st_, stop=sp)
                    if sp and hf == 1:
                        mm.then_inc(s["mm1"], 1)

        @blk.vector
        def _(v):
            v.wait_ge(s["mm1"], 1)
            v.tensor_copy(lowT[:, :], G0[:, :])
            v.tensor_scalar(cqT[0:QL, :], G1[0:QL, :], bq_s[:, :], None, ALU.add)
            v.tensor_scalar(routT[:, :], G1[32:40, :], br_s[:, :], None, ALU.add
                            ).then_inc(s["ev1"], 1)

        @blk.gpsimd
        def _(g):
            g.wait_ge(s["ev1"], 1)
            g.dma_start(cqT[QL:QL + 1, :], ones_d[:, :]).then_inc(s["ev1"], 16)

    if dbg:
        with nc.Block() as blk:
            @blk.sync
            def _(sync):
                sync.dma_start(dbg_o["lowT"][:, :], lowT[:, :]).then_inc(s["dbgf"], 16)
                sync.dma_start(dbg_o["cqT"][:, :], cqT[:, :]).then_inc(s["dbgf"], 16)
                sync.wait_ge(s["dbgf"], 80)
    if upto <= 1:
        return nc
    with nc.Block() as blk:
        @blk.tensor
        def _(t):
            for j in range(NTT):
                if j >= 2:
                    t.wait_ge(s["g"], 6 * (j - 1))
                bank = G0[:, 0:512] if j % 2 == 0 else G0[:, 512:1024]
                t.transpose(bank[:, 0:8], routT[:, j * 128:(j + 1) * 128],
                            id_f[0:NEXP, 0:NEXP]).then_inc(s["tr"], 1)
                t.wait_ge(s["g"], 6 * j + 5)
                t.transpose(bank[0:8, 256:384], g_tm[:, :], id_f[:, :]
                            ).then_inc(s["tr"], 1)
            t.wait_ge(s["g"], 6 * NTT)
            for hf in range(2):
                mm = t.matmul(G1[:, hf * 512:(hf + 1) * 512], bd8_s[:, :],
                              scoresT[:, hf * 512:(hf + 1) * 512],
                              start=True, stop=True)
                if hf == 1:
                    mm.then_inc(s["mm2"], 1)

        @blk.vector
        def _(v):
            for j in range(NTT):
                bank = G0[:, 0:512] if j % 2 == 0 else G0[:, 512:1024]
                v.wait_ge(s["tr"], 2 * j + 1)
                v.tensor_copy(g_tm[:, :], bank[:, 0:8])
                v.drain()
                v.reduce_max(gcol[:, 0:1], g_tm[:, :], axis=AX.X, negate=True
                             ).then_inc(s["g"], 1)
                v.wait_ge(s["g"], 6 * j + 2)
                v.reduce_sum(gcol[:, 1:2], ge_tm[:, :], axis=AX.X)
                v.drain()
                v.reciprocal(gcol[:, 2:3], gcol[:, 1:2])
                v.drain()
                v.tensor_scalar(gcol[:, 2:3], gcol[:, 2:3], 16.0, None, ALU.mult
                                ).then_inc(s["g"], 1)
                v.wait_ge(s["g"], 6 * j + 3)
                v.tensor_scalar(g_tm[:, :], ge_tm[:, :], gcol[:, 2:3], None,
                                ALU.mult).then_inc(s["g"], 3)
                v.wait_ge(s["tr"], 2 * j + 2)
                v.tensor_copy(scoresT[:, j * 128:(j + 1) * 128], bank[0:8, 256:384])
                v.drain()
                v.sem_inc(s["f"], 1)
            v.wait_ge(s["mm2"], 1)
            v.tensor_mul(lowsc[:, :], lowT[:, :], G1[:, :]).then_inc(s["ev2"], 1)

        @blk.scalar
        def _(sc):
            for j in range(NTT):
                sc.wait_ge(s["g"], 6 * j + 1)
                sc.activation(ge_tm[:, :], g_tm[:, :], AF.Exp, bias=gcol[:, 0:1]
                              ).then_inc(s["g"], 1)

    if dbg and int(dbg) >= 2:
        with nc.Block() as blk:
            @blk.sync
            def _(sync):
                sync.dma_start(dbg_o["routT"][:, :], routT[:, :]).then_inc(s["dbgf"], 16)
                sync.dma_start(dbg_o["scoresT"][:, :], scoresT[:, :]).then_inc(s["dbgf"], 16)
                sync.dma_start(dbg_o["lowsc"][:, :], lowsc[:, :]).then_inc(s["dbgf"], 16)
                sync.wait_ge(s["dbgf"], 128)
    if upto <= 2:
        return nc
    stk3 = contextlib.ExitStack()
    qT05 = stk3.enter_context(nc.sbuf_tensor("qT05", [128, TS], BF16))
    absq = stk3.enter_context(nc.sbuf_tensor("absq", [128, TS], BF16))
    mneg = stk3.enter_context(nc.sbuf_tensor("mneg", [NH, TS], F32))
    with nc.Block() as blk:
        @blk.tensor
        def _(t):
            for hf in range(2):
                mm = t.matmul(G0[:, hf * 512:(hf + 1) * 512], wqka_s[:, :],
                              cqT[:, hf * 512:(hf + 1) * 512], start=True, stop=True)
                if hf == 1:
                    mm.then_inc(s["mm3"], 1)
            t.wait_ge(s["ev3"], 2)
            t.matmul(G1[:, 0:1], rep4_s[:, :], kabs_b[:, :], start=True, stop=True
                     ).then_inc(s["mm3"], 1)
            t.wait_ge(s["ev3"], 3)
            t.wait_ge(s["sact"], 1)
            for hf in range(2):
                mm = t.matmul(G0[0:NH, hf * 512:(hf + 1) * 512], kabsbd[:, :],
                              absq[:, hf * 512:(hf + 1) * 512], start=True, stop=True)
                if hf == 1:
                    mm.then_inc(s["mm3"], 1)

        @blk.vector
        def _(v):
            v.wait_ge(s["mm3"], 1)
            v.tensor_scalar(qT05[:, :], G0[:, :], 0.5, None, ALU.mult
                            ).then_inc(s["ev3"], 1)
            for k in range(8):
                v.tensor_reduce(kc8[:, k:k + 1], ckvT_s[0:KVL, k * 512:(k + 1) * 512],
                                AX.X, ALU.max, apply_absolute_value=True)
            v.drain()
            v.reduce_max(kabs_f[:, :], kc8[:, :], axis=AX.X)
            v.drain()
            v.tensor_copy(kabs_b[:, :], kabs_f[:, :])
            v.drain()
            v.sem_inc(s["ev3"], 1)
            v.wait_ge(s["mm3"], 2)
            v.tensor_copy(kabscol[:, :], G1[:, 0:1])
            v.drain()
            v.tensor_scalar(kabsbd[:, :], bd32_s[:, :], kabscol[:, :], None, ALU.mult)
            v.drain()
            v.sem_inc(s["ev3"], 1)
            v.wait_ge(s["mm3"], 3)
            v.tensor_scalar(mneg[:, :], G0[0:NH, :], -1.0, None, ALU.mult
                            ).then_inc(s["ev3"], 1)

        @blk.scalar
        def _(sc):
            sc.wait_ge(s["mm3"], 1)
            sc.activation(absq[:, :], G0[:, :], AF.Abs, scale=0.5).then_inc(s["sact"], 1)

        @blk.gpsimd
        def _(g):
            g.wait_ge(s["ev3"], 1)
            if dbg:
                g.dma_start(dbg_o["qT05"][:, :], qT05[:, :]).then_inc(s["dbgf"], 16)
                g.dma_start(dbg_o["wqka"][:, :], wqka_s[:, :]).then_inc(s["dbgf"], 16)
                g.wait_ge(s["dbgf"], 128 + 32)
            g.dma_start(q_dram[:, :], qT05[:, :]).then_inc(s["qst"], 16)
            g.wait_ge(s["ev3"], 4)
            if dbg:
                g.dma_start(dbg_o["mneg"][:, :], mneg[:, :]).then_inc(s["dbgf"], 16)
            g.dma_start(m_dram[:, :], mneg[:, :]).then_inc(s["qst"], 16)
            g.wait_ge(s["qst"], 32)

    stk3.close()
    if upto <= 3:
        return nc
    with contextlib.ExitStack() as stk:
        QU = [stk.enter_context(nc.psum_tensor(f"QU{i}", [5, 1024], F32))
              for i in range(2)]
        GS = [G0, G1]

        with nc.Block() as blk:
            @blk.gpsimd
            def _(g):
                for h in range(NH):
                    if h >= 2:
                        g.wait_ge(s["smm"], 32 * (h - 1))
                    g.dma_start(qph[h % 2][0:4, :], q_dram[4 * h:4 * h + 4, :]
                                ).then_inc(s["qld"], 16)
                    g.dma_start(qph[h % 2][4:5, :], m_dram[h:h + 1, :]
                                ).then_inc(s["qld"], 16)
                    if dbg and h == 0:
                        g.wait_ge(s["qld"], 32)
                        g.dma_start(dbg_o["qph"][:, :], qph[0][:, :]).then_inc(s["dbgf"], 16)
                    if dbg and h == 1:
                        g.dma_start(dbg_o["cktm"][:, :], ckvtm_s[:, :]).then_inc(s["dbgf"], 16)

            @blk.tensor
            def _(t):
                for h in range(NH):
                    t.wait_ge(s["qld"], 32 * (h + 1))
                    if h >= 2:
                        t.wait_ge(s["sue"], h - 1)
                    for kt in range(NST * 2):
                        it = h * 32 + kt
                        if it >= 2:
                            t.wait_ge(s["sev"], it - 1)
                        for hf in range(2):
                            mm = t.matmul(
                                GS[it % 2][:, hf * 512:(hf + 1) * 512],
                                ckvT_s[:, kt * 128:(kt + 1) * 128],
                                qph[h % 2][:, hf * 512:(hf + 1) * 512],
                                start=True, stop=True)
                            if hf == 1:
                                mm.then_inc(s["smm"], 1)
                        if it >= 2 and it % 2 == 0:
                            for itp in range(it - 2, it):
                                t.wait_ge(s["sac5"], 1 + itp // 2 + 1)
                                hp = itp // 32
                                sl = itp % 2
                                eb = expb[(itp // 2) % 2]
                                for hf in range(2):
                                    mm = t.matmul(
                                        QU[hp % 2][:, hf * 512:(hf + 1) * 512],
                                        ckvtm_s[:, (itp % 32) * 5:((itp % 32) + 1) * 5],
                                        eb[:, sl * TS + hf * 512:sl * TS + (hf + 1) * 512],
                                        start=(itp % 32 == 0), stop=(itp % 32 == 31))
                                    if hf == 1:
                                        mm.then_inc(s["su"], 1)
                for itp in range(NH * 32 - 2, NH * 32):
                    t.wait_ge(s["sac5"], 1 + itp // 2 + 1)
                    hp = itp // 32
                    sl = itp % 2
                    eb = expb[(itp // 2) % 2]
                    for hf in range(2):
                        mm = t.matmul(
                            QU[hp % 2][:, hf * 512:(hf + 1) * 512],
                            ckvtm_s[:, (itp % 32) * 5:((itp % 32) + 1) * 5],
                            eb[:, sl * TS + hf * 512:sl * TS + (hf + 1) * 512],
                            start=(itp % 32 == 0), stop=(itp % 32 == 31))
                        if hf == 1:
                            mm.then_inc(s["su"], 1)

            @blk.vector
            def _(v):
                for it in range(NH * 32):
                    v.wait_ge(s["smm"], it + 1)
                    if it >= 2:
                        v.wait_ge(s["sac5"], 1 + it // 2)
                    v.tensor_copy(sfr[:, (it % 4) * TS:(it % 4 + 1) * TS],
                                  GS[it % 2][:, :])
                    v.drain()
                    v.sem_inc(s["sev"], 1)
                    if it % 32 == 31:
                        h = it // 32
                        if h >= 2:
                            v.wait_ge(s["ust"], 32 * (h - 1))
                        v.wait_ge(s["su"], 32 * (h + 1))
                        v.tensor_copy(u5[h % 2][:, :], QU[h % 2][:, :])
                        v.drain()
                        v.sem_inc(s["sue"], 1)

            @blk.scalar
            def _(sc):
                sc.sem_inc(s["sac5"], 1)
                for g in range(NH * 16):
                    sc.wait_ge(s["sev"], 2 * (g + 1))
                    if g >= 1:
                        sc.wait_ge(s["su"], 2 * g)
                    sc.activation(expb[g % 2][:, :],
                                  sfr[:, 2 * (g % 2) * TS:(2 * (g % 2) + 2) * TS],
                                  AF.Exp)
                    sc.drain()
                    sc.sem_inc(s["sac5"], 1)
                    if dbg and g == 0:
                        sc.dma_start(dbg_o["eb"][:, :], expb[0][:, :]).then_inc(s["dbgf"], 16)

            @blk.sync
            def _(sync):
                for h in range(NH):
                    sync.wait_ge(s["sue"], h + 1)
                    sync.dma_start(u_dram[h * 4:(h + 1) * 4, :], u5[h % 2][0:4, :]
                                   ).then_inc(s["ust"], 16)
                    sync.dma_start(d_dram[h:h + 1, :], u5[h % 2][4:5, :]
                                   ).then_inc(s["ust"], 16)

    if upto <= 4:
        return nc
    with nc.Block() as blk:
        @blk.gpsimd
        def _(g):
            g.wait_ge(s["ust"], 32 * NH)
            g.dma_start(urawT[:, :], u_dram[:, :]).then_inc(s["uld"], 16)
            g.dma_start(dnT[:, :], d_dram[:, :]).then_inc(s["uld"], 16)

        @blk.vector
        def _(v):
            v.wait_ge(s["uld"], 32)
            v.reciprocal(mneg2[:, :], dnT[:, :])
            v.drain()
            v.tensor_copy(rc32[:, :], mneg2[:, :])
            v.drain()
            v.sem_inc(s["ev2"], 1)
            v.wait_ge(s["mm2"], 2)
            v.tensor_mul(unT[:, :], urawT[:, :], G0[:, :]).then_inc(s["ev2"], 1)

        @blk.tensor
        def _(t):
            t.wait_ge(s["ev2"], 2)
            for hf in range(2):
                mm = t.matmul(G0[:, hf * 512:(hf + 1) * 512], bdq_s[:, :],
                              rc32[:, hf * 512:(hf + 1) * 512], start=True, stop=True)
                if hf == 1:
                    mm.then_inc(s["mm2"], 1)

    if dbg and int(dbg) >= 3:
        with nc.Block() as blk:
            @blk.sync
            def _(sync):
                for nm, t_ in [("urawT", urawT), ("dnT", dnT), ("unT", unT)]:
                    sync.dma_start(dbg_o[nm][:, :], t_[:, :]).then_inc(s["dbgf"], 16)
                sync.wait_ge(s["dbgf"], 224 if int(dbg) >= 2 else 128)
    if upto <= 5:
        return nc
    with nc.Block() as blk:
        @blk.tensor
        def _(t):
            for c in range(NDT):
                if c >= 2:
                    t.wait_ge(s["ev6"], c - 1)
                for hf in range(2):
                    t.matmul(GS[c % 2][:, hf * 512:(hf + 1) * 512],
                             w2_s[:, c * 128:(c + 1) * 128],
                             unT[:, hf * 512:(hf + 1) * 512], start=True, stop=False)
                    mm = t.matmul(GS[c % 2][:, hf * 512:(hf + 1) * 512],
                                  l1b_s[:, c * 128:(c + 1) * 128],
                                  lowsc[:, hf * 512:(hf + 1) * 512],
                                  start=False, stop=True)
                    if hf == 1:
                        mm.then_inc(s["mm6"], 1)

        @blk.vector
        def _(v):
            for c in range(NDT):
                v.wait_ge(s["mm6"], c + 1)
                v.tensor_scalar(bigbuf[:, c * TS:(c + 1) * TS], GS[c % 2][:, :],
                                c0p_s[:, c:c + 1], None, ALU.add).then_inc(s["ev6"], 1)

    if upto <= 6:
        return nc
    if dbg and int(dbg) >= 3:
        with nc.Block() as blk:
            @blk.sync
            def _(sync):
                sync.dma_start(dbg_o["big"][:, :], bigbuf[:, :]).then_inc(s["dbgf"], 16)
                sync.wait_ge(s["dbgf"], 16 * 17)
    with contextlib.ExitStack() as stk:
        PT = [stk.enter_context(nc.psum_tensor(f"PT{i}", [128, 1024], BF16))
              for i in range(2)]

        with nc.Block() as blk:
            @blk.sync
            def _(sync):
                for i in range(NTT):
                    for half in range(2):
                        sync.wait_ge(s["otrd"], NDT * i + 16 * (half + 1))
                        sync.dma_start(
                            out_o[i * 128:(i + 1) * 128,
                                  half * 2048:(half + 1) * 2048],
                            outt[:, :]).then_inc(s["ost"], 16)

            @blk.gpsimd
            def _(g):
                for i in range(NTT):
                    if i >= 1:
                        g.wait_ge(s["otrd"], NDT * i)
                    g.dma_start(x1t[:, :], x1_d[i * 128:(i + 1) * 128, :]
                                ).then_inc(s["x1ld"], 16)

            @blk.tensor
            def _(t):
                for i in range(NTT):
                    for c in range(NDT):
                        idx = i * NDT + c
                        if idx >= 2:
                            t.wait_ge(s["otrd"], idx - 1)
                        t.transpose(PT[c % 2][:, 0:128],
                                    bigbuf[:, c * TS + i * 128:c * TS + (i + 1) * 128],
                                    id_b[:, :]).then_inc(s["otr"], 1)

            @blk.vector
            def _(v):
                for i in range(NTT):
                    v.wait_ge(s["x1ld"], 16 * (i + 1))
                    for c in range(NDT):
                        idx = i * NDT + c
                        if c % 16 == 0:
                            prev = 2 * i + c // 16 - 1
                            if prev >= 0:
                                v.wait_ge(s["ost"], 16 * (prev + 1))
                        v.wait_ge(s["otr"], idx + 1)
                        v.tensor_add(outt[:, (c % 16) * 128:(c % 16 + 1) * 128],
                                     x1t[:, c * 128:(c + 1) * 128],
                                     PT[c % 2][:, 0:128]).then_inc(s["otrd"], 1)

    with nc.Block() as blk:
        @blk.sync
        def _(sync):
            sync.wait_ge(s["ost"], 16 * NTT)

    return nc


# version 06
# speedup vs baseline: 1.5021x; 1.2571x over previous
import contextlib
import numpy as np
import ml_dtypes
import concourse.bass as bass
import concourse.mybir as mybir
from concourse.bass_utils import run_bass_kernel_spmd

F32 = mybir.dt.float32
BF16 = mybir.dt.bfloat16
AF = mybir.ActivationFunctionType
ALU = mybir.AluOpType
AX = mybir.AxisListType

NCORE = 8
BATCH = 2
LTOT = 4096
D = 4096
TS = LTOT // 4
DQ = D // 4
NDT = D // 128
NTT = TS // 128
DTR = 128
NST = 16
NEXP = 8
RK = 64
NH = 32
HD = 128
QL = 12
KVL = 4


def build_L1(upto=99, dbg=False):
    nc = bass.Bass("TRN2", target_bir_lowering=False)
    x_d = nc.dram_tensor("x", [TS, D], F32, kind="ExternalInput")
    xpw_d = nc.dram_tensor("xpw", [128, NDT * 160], F32, kind="ExternalInput")
    dtpw_d = nc.dram_tensor("dtpw", [DTR, D], F32, kind="ExternalInput")
    dtpb_d = nc.dram_tensor("dtpb", [NDT, 128], F32, kind="ExternalInput")
    dvec_d = nc.dram_tensor("dvec", [NDT, 128], F32, kind="ExternalInput")
    gw_d = nc.dram_tensor("gw", [128, NDT * NEXP], F32, kind="ExternalInput")
    lA_d = nc.dram_tensor("lA", [128, NEXP * NDT * RK], F32, kind="ExternalInput")
    lB_d = nc.dram_tensor("lB", [128, NEXP * D // 2], F32, kind="ExternalInput")
    idm_d = nc.dram_tensor("idm", [128, 128], F32, kind="ExternalInput")

    dt_o = nc.dram_tensor("dt_o", [D, TS], F32, kind="ExternalOutput")
    u_o = nc.dram_tensor("u_o", [D, TS], BF16, kind="ExternalOutput")
    bc_o = nc.dram_tensor("bc_o", [32, TS], F32, kind="ExternalOutput")
    md_o = nc.dram_tensor("md_o", [D, TS], BF16, kind="ExternalOutput")
    w_dram = nc.dram_tensor("w_scr", [NEXP, TS], F32, kind="Internal")
    if dbg:
        xn_o = nc.dram_tensor("xn_o", [128, NDT * TS], BF16, kind="ExternalOutput")
        gp_o = nc.dram_tensor("gp_o", [NEXP, TS], F32, kind="ExternalOutput")
        w_o = nc.dram_tensor("w_o", [NEXP, TS], F32, kind="ExternalOutput")
        t1_o = nc.dram_tensor("t1_o", [128, NEXP * TS // 2], BF16, kind="ExternalOutput")
        cols_o = nc.dram_tensor("cols_o", [128, 8 * NTT], F32, kind="ExternalOutput")

    xnT = nc.alloc_sbuf_tensor("xnT", [128, NDT * TS], BF16)
    xpw_s = nc.alloc_sbuf_tensor("xpw_s", [128, NDT * 160], BF16)
    dtpw_s = nc.alloc_sbuf_tensor("dtpw_s", [128, D], BF16)
    gw_s = nc.alloc_sbuf_tensor("gw_s", [128, NDT * NEXP], BF16)
    id_b = nc.alloc_sbuf_tensor("id_b", [128, 128], BF16)
    id_f = nc.alloc_sbuf_tensor("id_f", [128, 128], F32)
    cols = nc.alloc_sbuf_tensor("cols", [128, 8], F32)
    colsnap = nc.alloc_sbuf_tensor("colsnap", [128, 8 * NTT], F32)
    dtpb_r = nc.alloc_sbuf_tensor("dtpb_r", [NDT, 128], F32)
    dvec_r = nc.alloc_sbuf_tensor("dvec_r", [NDT, 128], F32)
    bcol_s = nc.alloc_sbuf_tensor("bcol_s", [128, NDT], F32)
    dcol_s = nc.alloc_sbuf_tensor("dcol_s", [128, NDT], F32)
    dt_in_s = nc.alloc_sbuf_tensor("dt_in_s", [128, TS], BF16)
    bcT_s = nc.alloc_sbuf_tensor("bcT_s", [32, TS], F32)
    g_sb = nc.alloc_sbuf_tensor("g_sb", [NEXP, TS], F32)
    gt_sb = nc.alloc_sbuf_tensor("gt_sb", [128, NEXP], F32)
    ge_sb = nc.alloc_sbuf_tensor("ge_sb", [128, NEXP], F32)
    ge2_sb = nc.alloc_sbuf_tensor("ge2_sb", [128, NEXP], F32)
    gw_tm = nc.alloc_sbuf_tensor("gw_tm", [128, NEXP], F32)
    gcol = nc.alloc_sbuf_tensor("gcol", [128, 8], F32)
    wT_s = nc.alloc_sbuf_tensor("wT_s", [NEXP, TS], F32)
    md_t = [nc.alloc_sbuf_tensor(f"md_t{i}", [128, TS], BF16) for i in range(2)]

    P0 = nc.alloc_psum_tensor("P0", [128, 1024], F32)
    P1 = nc.alloc_psum_tensor("P1", [128, 1024], F32)
    P2 = nc.alloc_psum_tensor("P2", [40, 1024], F32)
    P3 = nc.alloc_psum_tensor("P3", [128, 1024], BF16)
    P4 = nc.alloc_psum_tensor("P4", [128, 1024], BF16)
    p_tr = [P3, P4]

    s = {}
    for name in ["wld", "wld2", "xld", "c1", "c2", "xn", "tr", "trd", "g",
                 "gtr", "gtrd", "wrt", "mm1", "ev1", "mm3", "ev3", "mm4",
                 "ev4", "st4", "mm5", "ev5a", "ev5b", "act5", "st5", "stout"]:
        s[name] = nc.alloc_semaphore("s_" + name)

    with nc.Block() as blk:
        @blk.sync
        def _(sync):
            sync.dma_start(dtpb_r[:, :], dtpb_d[:, :]).then_inc(s["wld"], 16)
            sync.dma_start(dvec_r[:, :], dvec_d[:, :]).then_inc(s["wld"], 16)

        @blk.gpsimd
        def _(g):
            g.dma_start(id_b[:, :], idm_d[:, :]).then_inc(s["wld"], 16)
            g.dma_start(id_f[:, :], idm_d[:, :]).then_inc(s["wld"], 16)
            g.dma_start(xpw_s[:, :], xpw_d[:, :]).then_inc(s["wld"], 16)
            g.dma_start(dtpw_s[:, :], dtpw_d[:, :]).then_inc(s["wld"], 16)
            g.dma_start(gw_s[:, :], gw_d[:, :]).then_inc(s["wld"], 16)

    with nc.Block() as blk:
        @blk.tensor
        def _(t):
            t.wait_ge(s["wld"], 16 * 7)
            t.transpose(P0[:, 0:32], dtpb_r[:, :], id_f[0:32, 0:32])
            t.transpose(P0[:, 32:64], dvec_r[:, :], id_f[0:32, 0:32]).then_inc(s["tr"], 1)

        @blk.vector
        def _(v):
            v.wait_ge(s["tr"], 1)
            v.tensor_copy(bcol_s[:, :], P0[:, 0:32])
            v.tensor_copy(dcol_s[:, :], P0[:, 32:64]).then_inc(s["trd"], 1)

    if upto <= 0:
        return nc
    with contextlib.ExitStack() as stk:
        x_t = [stk.enter_context(nc.sbuf_tensor(f"x_t{i}", [128, D], F32)) for i in range(2)]
        xsq = stk.enter_context(nc.sbuf_tensor("xsq", [128, D], BF16))
        xn_t = [stk.enter_context(nc.sbuf_tensor(f"xn_t{i}", [128, D], BF16)) for i in range(2)]

        with nc.Block() as blk:
            @blk.sync
            def _(sync):
                for i in range(NTT):
                    if i >= 2:
                        sync.wait_ge(s["xn"], i - 1)
                    sync.dma_start(x_t[i % 2][:, :], x_d[i * 128:(i + 1) * 128, :]
                                   ).then_inc(s["xld"], 16)

            @blk.vector
            def _(v):
                for i in range(NTT):
                    v.wait_ge(s["c2"], 2 * i + 1)
                    v.tensor_scalar(cols[:, 1:2], cols[:, 0:1], 1.0 / D, None, ALU.mult)
                    v.tensor_scalar(cols[:, 3:4], cols[:, 2:3], 1.0 / D, None, ALU.mult)
                    v.drain()
                    v.tensor_mul(cols[:, 4:5], cols[:, 1:2], cols[:, 1:2])
                    v.drain()
                    v.tensor_sub(cols[:, 5:6], cols[:, 3:4], cols[:, 4:5])
                    v.drain()
                    v.tensor_scalar(cols[:, 5:6], cols[:, 5:6], 1e-5, None, ALU.add
                                    ).then_inc(s["c1"], 1)
                    xt = x_t[i % 2]
                    v.wait_ge(s["c2"], 2 * i + 2)
                    if i >= 1:
                        v.wait_ge(s["gtr"], i * NDT)
                    v.tensor_scalar(xn_t[i % 2][:, :], xt[:, :], cols[:, 1:2],
                                    cols[:, 7:8], ALU.subtract, ALU.mult
                                    ).then_inc(s["xn"], 1)
                    if dbg:
                        v.tensor_copy(colsnap[:, i * 8:(i + 1) * 8], cols[:, :])
                    for c in range(1, NDT, 2):
                        v.wait_ge(s["gtr"], i * NDT + c + 1)
                        v.tensor_copy(xnT[:, c * TS + i * 128: c * TS + (i + 1) * 128],
                                      p_tr[c % 2][:, 0:128]).then_inc(s["gtrd"], 1)

            @blk.scalar
            def _(sc):
                for i in range(NTT):
                    sc.wait_ge(s["xld"], 16 * (i + 1))
                    sc.activation(xsq[:, :], x_t[i % 2][:, :], AF.Copy,
                                  accum_out=cols[:, 0:1])
                    sc.activation(xsq[:, :], x_t[i % 2][:, :], AF.Square,
                                  accum_out=cols[:, 2:3]).then_inc(s["c2"], 1)
                    sc.wait_ge(s["c1"], i + 1)
                    sc.activation(cols[:, 6:7], cols[:, 5:6], AF.Ln)
                    sc.drain()
                    sc.activation(cols[:, 7:8], cols[:, 6:7], AF.Exp, scale=-0.5
                                  ).then_inc(s["c2"], 1)
                    for c in range(0, NDT, 2):
                        sc.wait_ge(s["gtr"], i * NDT + c + 1)
                        sc.copy(xnT[:, c * TS + i * 128: c * TS + (i + 1) * 128],
                                p_tr[c % 2][:, 0:128]).then_inc(s["gtrd"], 1)

            @blk.tensor
            def _(t):
                for i in range(NTT):
                    t.wait_ge(s["xn"], i + 1)
                    for c in range(NDT):
                        idx = i * NDT + c
                        if idx >= 2:
                            t.wait_ge(s["gtrd"], idx - 1)
                        t.transpose(p_tr[c % 2][:, 0:128],
                                    xn_t[i % 2][:, c * 128:(c + 1) * 128],
                                    id_b[:, :]).then_inc(s["gtr"], 1)

    if upto <= 1:
        return nc
    with nc.Block() as blk:
        @blk.tensor
        def _(t):
            for c in range(NDT):
                st_ = (c == 0)
                sp = (c == NDT - 1)
                rhs = xnT[:, c * TS:(c + 1) * TS]
                for li, (lhs, pp) in enumerate((
                        (xpw_s[:, c * 160:c * 160 + 128], P0),
                        (xpw_s[:, c * 160 + 128:(c + 1) * 160], P2[0:32, :]),
                        (gw_s[:, c * NEXP:(c + 1) * NEXP], P1[0:8, :]))):
                    for hf in range(2):
                        mm = t.matmul(pp[:, hf * 512:(hf + 1) * 512],
                                      lhs, rhs[:, hf * 512:(hf + 1) * 512],
                                      start=st_, stop=sp)
                        if c == NDT - 1 and li == 2 and hf == 1:
                            mm.then_inc(s["mm1"], 1)

        @blk.vector
        def _(v):
            v.wait_ge(s["mm1"], 1)
            v.tensor_copy(dt_in_s[:, :], P0[:, :])
            v.tensor_copy(bcT_s[:, :], P2[0:32, :])
            v.tensor_copy(g_sb[:, :], P1[0:8, :]).then_inc(s["ev1"], 1)

        @blk.sync
        def _(sync):
            sync.wait_ge(s["ev1"], 1)
            sync.dma_start(bc_o[:, :], bcT_s[:, :]).then_inc(s["stout"], 16)
            if dbg:
                sync.dma_start(xn_o[:, :], xnT[:, :]).then_inc(s["stout"], 16)
                sync.dma_start(cols_o[:, :], colsnap[:, :]).then_inc(s["stout"], 16)
                sync.dma_start(gp_o[:, :], g_sb[:, :]).then_inc(s["stout"], 16)

    if upto <= 2:
        return nc
    with contextlib.ExitStack() as stk:
        lA_s = stk.enter_context(nc.sbuf_tensor("lA_s", [128, NEXP * NDT * RK], BF16))
        lB_s = stk.enter_context(nc.sbuf_tensor("lB_s", [128, NEXP * D // 2], BF16))
        w_bc = stk.enter_context(nc.sbuf_tensor("w_bc", [128, NEXP * TS // 2], BF16))
        t1w_s = stk.enter_context(nc.sbuf_tensor("t1w_s", [128, NEXP * TS // 2], BF16))

        def lBsl(e, c):
            return lB_s[64 * (e % 2):64 * (e % 2) + 64,
                        (e // 2) * D + c * 128:(e // 2) * D + (c + 1) * 128]

        def wbsl(e):
            return w_bc[64 * (e % 2):64 * (e % 2) + 64, (e // 2) * TS:(e // 2 + 1) * TS]

        def t1sl(e):
            return t1w_s[64 * (e % 2):64 * (e % 2) + 64, (e // 2) * TS:(e // 2 + 1) * TS]

        with nc.Block() as blk:
            @blk.gpsimd
            def _(g):
                g.dma_start(lA_s[:, :], lA_d[:, :]).then_inc(s["wld2"], 16)
                g.dma_start(lB_s[:, :], lB_d[:, :]).then_inc(s["wld2"], 16)

            @blk.tensor
            def _(t):
                for j in range(NTT):
                    if j >= 2:
                        t.wait_ge(s["g"], 7 * (j - 1))
                    bank = P1[:, 0:512] if j % 2 == 0 else P1[:, 512:1024]
                    t.transpose(bank[:, 0:8], g_sb[:, j * 128:(j + 1) * 128],
                                id_f[0:8, 0:8]).then_inc(s["tr"], 1)
                    t.wait_ge(s["g"], 7 * j + 6)
                    t.transpose(bank[0:8, 256:384], gw_tm[:, :], id_f[:, :]
                                ).then_inc(s["tr"], 1)

            @blk.vector
            def _(v):
                tr0 = 1
                for j in range(NTT):
                    bank = P1[:, 0:512] if j % 2 == 0 else P1[:, 512:1024]
                    v.wait_ge(s["tr"], tr0 + 2 * j + 1)
                    v.tensor_copy(gt_sb[:, :], bank[:, 0:8])
                    v.drain()
                    v.sem_inc(s["g"], 1)
                    v.reduce_max(gcol[:, 0:1], gt_sb[:, :], axis=AX.X, negate=True
                                 ).then_inc(s["g"], 1)
                    v.wait_ge(s["g"], 7 * j + 3)
                    v.max(ge2_sb[:, :], ge_sb[:, :]).then_inc(s["g"], 1)
                    v.wait_ge(s["g"], 7 * j + 4)
                    v.tensor_scalar(gw_tm[:, :], ge_sb[:, :], ge2_sb[:, 1:2], None,
                                    ALU.is_ge)
                    v.drain()
                    v.tensor_mul(gw_tm[:, :], gw_tm[:, :], ge_sb[:, :])
                    v.tensor_add(gcol[:, 3:4], ge2_sb[:, 0:1], ge2_sb[:, 1:2])
                    v.drain()
                    v.reciprocal(gcol[:, 4:5], gcol[:, 3:4]).then_inc(s["g"], 1)
                    v.wait_ge(s["g"], 7 * j + 5)
                    v.tensor_scalar(gw_tm[:, :], gw_tm[:, :], gcol[:, 4:5], None,
                                    ALU.mult).then_inc(s["g"], 1)
                    v.wait_ge(s["tr"], tr0 + 2 * j + 2)
                    v.tensor_copy(wT_s[:, j * 128:(j + 1) * 128], bank[0:8, 256:384]
                                  ).then_inc(s["g"], 1)

            @blk.scalar
            def _(sc):
                for j in range(NTT):
                    sc.wait_ge(s["g"], 7 * j + 2)
                    sc.activation(ge_sb[:, :], gt_sb[:, :], AF.Exp, bias=gcol[:, 0:1]
                                  ).then_inc(s["g"], 1)

        if upto <= 3:
            return nc
        with nc.Block() as blk:
            @blk.sync
            def _(sync):
                sync.dma_start(w_dram[:, :], wT_s[:, :]).then_inc(s["wrt"], 16)
                if dbg:
                    sync.dma_start(w_o[:, :], wT_s[:, :]).then_inc(s["stout"], 16)

            @blk.gpsimd
            def _(g):
                g.wait_ge(s["wrt"], 16)
                for e in range(NEXP):
                    src = bass.AP(w_dram, e * TS, [[0, 64], [1, TS]])
                    g.dma_start(wbsl(e), src).then_inc(s["wrt"], 16)

        if upto <= 4:
            return nc
        with nc.Block() as blk:
            @blk.tensor
            def _(t):
                t.wait_ge(s["wrt"], 16 * (1 + NEXP))
                t.wait_ge(s["wld2"], 16)
                for e in range(NEXP):
                    pt = P0 if e % 2 == 0 else P1
                    if e >= 2:
                        t.wait_ge(s["ev3"], e - 1)
                    for c in range(NDT):
                        lhs = lA_s[:, (e * NDT + c) * RK:(e * NDT + c + 1) * RK]
                        for hf in range(2):
                            mm = t.matmul(pt[0:64, hf * 512:(hf + 1) * 512], lhs,
                                          xnT[:, c * TS + hf * 512:c * TS + (hf + 1) * 512],
                                          start=(c == 0), stop=(c == NDT - 1))
                            if c == NDT - 1 and hf == 1:
                                mm.then_inc(s["mm3"], 1)

            @blk.vector
            def _(v):
                for e in range(NEXP):
                    v.wait_ge(s["mm3"], e + 1)
                    pt = P0 if e % 2 == 0 else P1
                    v.tensor_mul(t1sl(e), pt[0:64, :], wbsl(e)).then_inc(s["ev3"], 1)

        if upto <= 5:
            return nc
        if dbg:
            with nc.Block() as blk:
                @blk.sync
                def _(sync):
                    sync.dma_start(t1_o[:, :], t1w_s[:, :]).then_inc(s["stout"], 16)

        with nc.Block() as blk:
            @blk.tensor
            def _(t):
                t.wait_ge(s["wld2"], 32)
                for c in range(NDT):
                    pt = P0 if c % 2 == 0 else P1
                    if c >= 2:
                        t.wait_ge(s["ev4"], c - 1)
                    for p in range(NEXP // 2):
                        lhs = lB_s[:, p * D + c * 128:p * D + (c + 1) * 128]
                        rhs_p = t1w_s[:, p * TS:(p + 1) * TS]
                        for hf in range(2):
                            mm = t.matmul(pt[:, hf * 512:(hf + 1) * 512], lhs,
                                          rhs_p[:, hf * 512:(hf + 1) * 512],
                                          start=(p == 0), stop=(p == NEXP // 2 - 1))
                            if p == NEXP // 2 - 1 and hf == 1:
                                mm.then_inc(s["mm4"], 1)

            @blk.vector
            def _(v):
                for c in range(NDT):
                    v.wait_ge(s["mm4"], c + 1)
                    if c >= 2:
                        v.wait_ge(s["st4"], 16 * (c - 1))
                    pt = P0 if c % 2 == 0 else P1
                    v.scalar_tensor_tensor(md_t[c % 2][:, :],
                                           xnT[:, c * TS:(c + 1) * TS],
                                           dcol_s[:, c:c + 1], pt[:, :],
                                           ALU.mult, ALU.add).then_inc(s["ev4"], 1)

            @blk.sync
            def _(sync):
                for c in range(NDT):
                    sync.wait_ge(s["ev4"], c + 1)
                    sync.dma_start(md_o[c * 128:(c + 1) * 128, :], md_t[c % 2][:, :]
                                   ).then_inc(s["st4"], 16)

    if upto <= 6:
        return nc
    with contextlib.ExitStack() as stk5:
        zb_abs = [stk5.enter_context(nc.sbuf_tensor(f"zb_abs{i}", [128, TS], F32)) for i in range(2)]
        zb_rel = [stk5.enter_context(nc.sbuf_tensor(f"zb_rel{i}", [128, TS], F32)) for i in range(2)]
        e_t = stk5.enter_context(nc.sbuf_tensor("e_t", [128, TS], F32))
        ee_t = stk5.enter_context(nc.sbuf_tensor("ee_t", [128, TS], F32))
        l_t = [stk5.enter_context(nc.sbuf_tensor(f"l_t{i}", [128, TS], F32)) for i in range(2)]
        dt_t = [stk5.enter_context(nc.sbuf_tensor(f"dt_t{i}", [128, TS], F32)) for i in range(2)]
        u_t = [stk5.enter_context(nc.sbuf_tensor(f"u_t{i}", [128, TS], BF16)) for i in range(2)]

        with nc.Block() as blk:
            @blk.tensor
            def _(t):
                for c in range(NDT):
                    pt = P0 if c % 2 == 0 else P1
                    if c >= 2:
                        t.wait_ge(s["ev5a"], c - 1)
                    lhs = dtpw_s[:, c * 128:(c + 1) * 128]
                    for hf in range(2):
                        mm = t.matmul(pt[:, hf * 512:(hf + 1) * 512], lhs,
                                      dt_in_s[:, hf * 512:(hf + 1) * 512],
                                      start=True, stop=True)
                        if hf == 1:
                            mm.then_inc(s["mm5"], 1)

            @blk.vector
            def _(v):
                for c in range(NDT):
                    v.wait_ge(s["mm5"], c + 1)
                    if c >= 2:
                        v.wait_ge(s["st5"], 32 * (c - 1))
                    pt = P0 if c % 2 == 0 else P1
                    v.tensor_scalar(zb_abs[c % 2][:, :], pt[:, :],
                                    bcol_s[:, c:c + 1], None, ALU.add)
                    v.tensor_scalar_max(zb_rel[c % 2][:, :], zb_abs[c % 2][:, :], 0.0
                                        ).then_inc(s["ev5a"], 1)
                    v.wait_ge(s["act5"], c + 1)
                    v.tensor_add(dt_t[c % 2][:, :], zb_rel[c % 2][:, :],
                                 l_t[c % 2][:, :])
                    v.tensor_mul(u_t[c % 2][:, :], dt_t[c % 2][:, :],
                                 xnT[:, c * TS:(c + 1) * TS]).then_inc(s["ev5b"], 1)

            @blk.scalar
            def _(sc):
                for c in range(NDT):
                    sc.wait_ge(s["ev5a"], c + 1)
                    sc.activation(e_t[:, :], zb_abs[c % 2][:, :], AF.Abs)
                    sc.activation(ee_t[:, :], e_t[:, :], AF.Exp, scale=-1.0)
                    sc.activation(l_t[c % 2][:, :], ee_t[:, :], AF.Ln, bias=1.0
                                  ).then_inc(s["act5"], 1)

            @blk.sync
            def _(sync):
                for c in range(NDT):
                    sync.wait_ge(s["ev5b"], c + 1)
                    sync.dma_start(dt_o[c * 128:(c + 1) * 128, :], dt_t[c % 2][:, :]
                                   ).then_inc(s["st5"], 16)
                    sync.dma_start(u_o[c * 128:(c + 1) * 128, :], u_t[c % 2][:, :]
                                   ).then_inc(s["st5"], 16)


    with nc.Block() as blk:
        @blk.sync
        def _(sync):
            sync.wait_ge(s["stout"], 16)
            sync.wait_ge(s["st4"], 16 * NDT)
            sync.wait_ge(s["st5"], 32 * NDT)

    return nc


def host_L1_inputs(inp, core):
    b = core // 4
    ts = (core % 4) * TS
    f = np.ascontiguousarray
    A = inp["L0_ffn_lora_A"]
    Bm = inp["L0_ffn_lora_B"]
    return dict(
        x=f(inp["L0_attn_x"][b, ts:ts + TS, :]),
        xpw=f(inp["L0_attn_x_proj_w"].reshape(160, NDT, 128).transpose(2, 1, 0).reshape(128, -1)),
        dtpw=f(inp["L0_attn_dt_proj_w"].T),
        dtpb=f(inp["L0_attn_dt_proj_b"].reshape(NDT, 128)),
        dvec=f(inp["L0_attn_D"].reshape(NDT, 128)),
        gw=f(inp["L0_ffn_gate_w"].reshape(NEXP, NDT, 128).transpose(2, 1, 0).reshape(128, -1)),
        lA=f(A.reshape(NEXP, RK, NDT, 128).transpose(3, 0, 2, 1).reshape(128, -1)),
        lB=f(Bm.reshape(NEXP // 2, 2, D, RK).transpose(1, 3, 0, 2).reshape(128, -1)),
        idm=np.eye(128, dtype=np.float32),
    )


def _f(a):
    return np.ascontiguousarray(a)


def kernel(**inp):
    inp = {k: np.asarray(v) for k, v in inp.items()}
    f = _f
    ident = np.eye(128, dtype=np.float32)
    ones = np.ones((1, TS), np.float32)

    nc1 = build_L1()
    maps1 = [host_L1_inputs(inp, c) for c in range(NCORE)]
    r1 = run_bass_kernel_spmd(nc1, maps1, core_ids=list(range(NCORE))).results

    dt_g = [np.hstack([r1[4 * b + j]["dt_o"] for j in range(4)]) for b in range(2)]
    u_g = [np.hstack([r1[4 * b + j]["u_o"] for j in range(4)]) for b in range(2)]
    bc_g = [np.hstack([r1[4 * b + j]["bc_o"] for j in range(4)]) for b in range(2)]
    md_g = [np.hstack([r1[4 * b + j]["md_o"] for j in range(4)]) for b in range(2)]
    nc2 = build_L2()
    maps2 = []
    for c in range(NCORE):
        b, q = c // 4, c % 4
        maps2.append(dict(
            dt=f(dt_g[b][q * DQ:(q + 1) * DQ]),
            u=f(u_g[b][q * DQ:(q + 1) * DQ]),
            bc=f(bc_g[b]),
            alog=f(inp["L0_attn_A_log"][q * DQ:(q + 1) * DQ]),
        ))
    r2 = run_bass_kernel_spmd(nc2, maps2, core_ids=list(range(NCORE))).results
    y_g = [np.vstack([r2[4 * b + q]["y_o"] for q in range(4)]) for b in range(2)]

    Wkv = inp["L1_attn_Wkv_d"]
    Wvu = inp["L1_attn_Wv_u"]
    Wo = inp["L1_attn_Wo"]
    nc3 = build_L3()
    maps3 = []
    for c in range(NCORE):
        b, ts = c // 4, (c % 4) * TS
        sl = (c % 4)
        maps3.append(dict(
            x=f(inp["L0_attn_x"][b, ts:ts + TS]),
            yt=f(y_g[b][:, ts:ts + TS].T),
            mdt=f(md_g[b][:, ts:ts + TS].T),
            wkv=f(Wkv.reshape(NDT, 128, KVL).transpose(1, 0, 2).reshape(128, -1)),
            bkv=f(inp["L1_attn_bkv_d"][:, None]),
            ones=ones, idm=ident,
            wvu=f(Wvu.T.reshape(NDT, 128, KVL).transpose(1, 0, 2).reshape(128, -1)),
            wocol=f(Wo[:, sl * TS:(sl + 1) * TS]),
            bvu=f(inp["L1_attn_bv_u"].reshape(NDT, 128).T),
            bo=f(inp["L1_attn_bo"][None, sl * TS:(sl + 1) * TS]),
        ))
    r3 = run_bass_kernel_spmd(nc3, maps3, core_ids=list(range(NCORE))).results

    ckvT_g = [np.hstack([np.asarray(r3[4 * b + q]["ckv_o"]) for q in range(4)])
              for b in range(2)]
    w2_full = np.hstack([np.asarray(r3[q]["w2s_o"]) for q in range(4)])
    c0_full = np.hstack([np.asarray(r3[q]["c0_o"]) for q in range(4)])[0]
    bd8 = np.zeros((NEXP, 128), np.float32)
    for h in range(NEXP):
        bd8[h, h * 16:(h + 1) * 16] = 1.0
    bdq = np.zeros((NH, 128), np.float32)
    for h in range(NH):
        bdq[h, h * 4:(h + 1) * 4] = 1.0
    rep4 = np.zeros((KVL, 128), np.float32)
    for h in range(NH):
        for l in range(KVL):
            rep4[l, h * 4 + l] = 1.0
    bd32 = np.zeros((128, NH), np.float32)
    for h in range(NH):
        bd32[h * 4:(h + 1) * 4, h] = 1.0
    wqr = np.zeros((128, NDT, 40), np.float32)
    wqr[:, :, 0:QL] = inp["L1_attn_Wq_d"].reshape(NDT, 128, QL).transpose(1, 0, 2)
    wqr[:, :, 32:40] = inp["L1_ffn_Wr"].T.reshape(NDT, 128, NEXP).transpose(1, 0, 2)
    nc4 = build_L4()
    maps4 = []
    for c in range(NCORE):
        b = c // 4
        ckvT_b = ckvT_g[b]
        maps4.append(dict(
            x1=f(np.asarray(r3[c]["x1_o"])),
            xnT=f(np.asarray(r3[c]["xnT_o"])),
            ckvT=f(ckvT_b),
            ckvtm=f(ckvT_b.T.reshape(32, 128, 5).transpose(1, 0, 2).reshape(128, -1)),
            w2=f(w2_full),
            c0p=f(c0_full.reshape(NDT, 128).T),
            l1a=f(inp["L1_ffn_A"].reshape(128, D).T.reshape(NDT, 128, 128)
                  .transpose(1, 0, 2).reshape(128, -1)),
            l1b=f(inp["L1_ffn_B"].transpose(0, 2, 1).reshape(128, D)),
            wqr=f(wqr.reshape(128, -1)),
            wqka=f(np.concatenate([inp["L1_attn_W_qk"],
                                   inp["L1_attn_b_qk"][None, :]], 0)),
            bq=f(inp["L1_attn_bq_d"][:, None]),
            br=f(inp["L1_ffn_br"][:, None]),
            ones=ones, idm=ident,
            bd8=bd8, bdq=bdq, rep4=rep4, bd32=bd32,
        ))
    r4 = run_bass_kernel_spmd(nc4, maps4, core_ids=list(range(NCORE))).results
    out = np.stack([np.vstack([np.asarray(r4[4 * b + q]["out_o"], np.float32)
                               for q in range(4)]) for b in range(2)], 0)
    return out


def build_L2(dbg=False):
    nc = bass.Bass("TRN2", target_bir_lowering=False)
    dt_d = nc.dram_tensor("dt", [DQ, LTOT], F32, kind="ExternalInput")
    u_d = nc.dram_tensor("u", [DQ, LTOT], BF16, kind="ExternalInput")
    bc_d = nc.dram_tensor("bc", [32, LTOT], F32, kind="ExternalInput")
    alog_d = nc.dram_tensor("alog", [DQ, NST], F32, kind="ExternalInput")
    y_o = nc.dram_tensor("y_o", [DQ, LTOT], BF16, kind="ExternalOutput")
    if dbg:
        dbg_a = nc.dram_tensor("dbg_a", [128, LTOT], F32, kind="ExternalOutput")
        dbg_b = nc.dram_tensor("dbg_b", [128, LTOT], F32, kind="ExternalOutput")
        dbg_h = nc.dram_tensor("dbg_h", [128, LTOT], BF16, kind="ExternalOutput")
        dbg_B = nc.dram_tensor("dbg_B", [128, LTOT], BF16, kind="ExternalOutput")

    NJ = 2
    NB = (DQ // 128) // NJ

    A_s = nc.alloc_sbuf_tensor("A_s", [128, (DQ // 128) * NST], F32)
    dt_s = [nc.alloc_sbuf_tensor(f"dt_s{j}", [128, LTOT], F32) for j in range(NJ)]
    u_s = [nc.alloc_sbuf_tensor(f"u_s{j}", [128, LTOT], BF16) for j in range(NJ)]
    Bbc = [nc.alloc_sbuf_tensor(f"Bbc{i}", [128, LTOT], BF16) for i in range(2)]
    Cbc = [nc.alloc_sbuf_tensor(f"Cbc{i}", [128, LTOT], BF16) for i in range(2)]
    dA_s = [nc.alloc_sbuf_tensor(f"dA_s{i}", [128, LTOT], F32) for i in range(2)]
    dBx_s = [nc.alloc_sbuf_tensor(f"dBx_s{i}", [128, LTOT], F32) for i in range(2)]
    h_s = [nc.alloc_sbuf_tensor(f"h_s{i}", [128, LTOT], BF16) for i in range(2)]
    tmp_s = nc.alloc_sbuf_tensor("tmp_s", [128, LTOT], BF16)
    y_s = [nc.alloc_sbuf_tensor(f"y_s{j}", [128, LTOT], F32) for j in range(NJ)]

    s = {}
    for name in ["ald", "dld", "bld", "act", "pool", "scan", "y", "st", "dbgv", "dbgs", "fence"]:
        s[name] = nc.alloc_semaphore("s_" + name)

    with nc.Block() as blk:
        @blk.sync
        def _(sync):
            src = bass.AP(alog_d, 0, [[NST, 128], [128 * NST, DQ // 128], [1, NST]])
            dst = A_s[:, :].rearrange("p (c n) -> p c n", c=DQ // 128)
            sync.dma_start(dst, src).then_inc(s["ald"], 16)

        @blk.scalar
        def _(sc):
            sc.wait_ge(s["ald"], 16)
            sc.activation(A_s[:, :], A_s[:, :], AF.Exp).then_inc(s["act"], 1)

        @blk.vector
        def _(v):
            v.wait_ge(s["act"], 1)
            v.tensor_scalar(A_s[:, :], A_s[:, :], -1.0, None, ALU.mult)

    with nc.Block() as blk:
        @blk.sync
        def _(sync):
            for b in range(NB):
                if b >= 1:
                    sync.wait_ge(s["st"], 16 * NJ * b)
                for j in range(NJ):
                    r0 = (b * NJ + j) * 128
                    sync.dma_start(dt_s[j][:, :], dt_d[r0:r0 + 128, :]).then_inc(s["dld"], 16)
                    sync.dma_start(u_s[j][:, :], u_d[r0:r0 + 128, :]).then_inc(s["dld"], 16)

        @blk.gpsimd
        def _(g):
            it = 0
            for b in range(NB):
                for n in range(NST):
                    gi = b * NST + n
                    if gi >= 2:
                        g.wait_ge(s["scan"], (gi - 2) * NJ + NJ)
                    g.dma_start(Bbc[gi % 2][:, :],
                                bass.AP(bc_d, n * LTOT, [[0, 128], [1, LTOT]])
                                ).then_inc(s["bld"], 16)
                    g.dma_start(Cbc[gi % 2][:, :],
                                bass.AP(bc_d, (16 + n) * LTOT, [[0, 128], [1, LTOT]])
                                ).then_inc(s["bld"], 16)
                    for j in range(NJ):
                        g.wait_ge(s["dld"], 32 * (b * NJ + j) + 32)
                        g.wait_ge(s["bld"], 32 * gi + 16)
                        if it >= 2:
                            g.wait_ge(s["scan"], it - 1)
                        g.tensor_tensor(dBx_s[it % 2][:, :], u_s[j][:, :],
                                        Bbc[gi % 2][:, :], ALU.mult
                                        ).then_inc(s["pool"], 1)
                        it += 1
                for j in range(NJ):
                    g.wait_ge(s["y"], b * NJ + j + 1)
                    r0 = (b * NJ + j) * 128
                    g.dma_start(y_o[r0:r0 + 128, :], y_s[j][:, :]).then_inc(s["st"], 16)
            g.wait_ge(s["st"], 16 * NJ * NB)

        @blk.scalar
        def _(sc):
            it = 0
            for b in range(NB):
                for n in range(NST):
                    for j in range(NJ):
                        sc.wait_ge(s["dld"], 32 * (b * NJ + j) + 32)
                        if it >= 2:
                            sc.wait_ge(s["scan"], it - 1)
                        c_glob = b * NJ + j
                        sc.activation(dA_s[it % 2][:, :], dt_s[j][:, :], AF.Exp,
                                      scale=A_s[:, c_glob * NST + n:c_glob * NST + n + 1]
                                      ).then_inc(s["act"], 1)
                        if dbg and it == 1:
                            sc.wait_ge(s["dbgv"], 1)
                            sc.dma_start(dbg_a[:, :], dA_s[0][:, :]).then_inc(s["dbgs"], 16)
                            sc.dma_start(dbg_b[:, :], dBx_s[0][:, :]).then_inc(s["dbgs"], 16)
                            sc.dma_start(dbg_h[:, :], h_s[0][:, :]).then_inc(s["dbgs"], 16)
                            sc.dma_start(dbg_B[:, :], Bbc[0][:, :]).then_inc(s["dbgs"], 16)
                        it += 1

        @blk.vector
        def _(v):
            it = 0
            for b in range(NB):
                if b >= 1:
                    v.wait_ge(s["st"], 16 * NJ * b)
                for j in range(NJ):
                    v.memset(y_s[j][:, :], 0.0)
                for n in range(NST):
                    gi = b * NST + n
                    for j in range(NJ):
                        v.wait_ge(s["act"], 1 + it + 1)
                        v.wait_ge(s["pool"], it + 1)
                        v.tensor_tensor_scan(h_s[it % 2][:, :], dA_s[it % 2][:, :],
                                             dBx_s[it % 2][:, :], 0.0,
                                             ALU.mult, ALU.add).then_inc(s["fence"], 1)
                        v.wait_ge(s["fence"], 2 * it + 1)
                        v.wait_ge(s["bld"], 32 * gi + 32)
                        v.tensor_mul(tmp_s[:, :], h_s[it % 2][:, :], Cbc[gi % 2][:, :]
                                     ).then_inc(s["fence"], 1)
                        v.wait_ge(s["fence"], 2 * it + 2)
                        v.tensor_add(y_s[j][:, :], y_s[j][:, :], tmp_s[:, :]
                                     ).then_inc(s["scan"], 1)
                        if dbg and it == 0:
                            v.sem_inc(s["dbgv"], 1)
                            v.wait_ge(s["dbgs"], 64)
                        it += 1
                for j in range(NJ):
                    v.sem_inc(s["y"], 1)



    with nc.Block() as blk:
        pass

    return nc


def build_L3(upto=99):
    nc = bass.Bass("TRN2", target_bir_lowering=False)
    x_d = nc.dram_tensor("x", [TS, D], F32, kind="ExternalInput")
    yt_d = nc.dram_tensor("yt", [TS, D], BF16, kind="ExternalInput")
    mdt_d = nc.dram_tensor("mdt", [TS, D], BF16, kind="ExternalInput")
    wkv_d = nc.dram_tensor("wkv", [128, NDT * KVL], F32, kind="ExternalInput")
    bkv_d = nc.dram_tensor("bkv", [KVL, 1], F32, kind="ExternalInput")
    ones_d = nc.dram_tensor("ones", [1, TS], F32, kind="ExternalInput")
    idm_d = nc.dram_tensor("idm", [128, 128], F32, kind="ExternalInput")
    wvu_d = nc.dram_tensor("wvu", [128, NH * KVL], F32, kind="ExternalInput")
    wocol_d = nc.dram_tensor("wocol", [D, TS], F32, kind="ExternalInput")
    bvu_d = nc.dram_tensor("bvu", [128, NDT], F32, kind="ExternalInput")
    bo_d = nc.dram_tensor("bo", [1, TS], F32, kind="ExternalInput")

    x1_o = nc.dram_tensor("x1_o", [TS, D], F32, kind="ExternalOutput")
    xnT_o = nc.dram_tensor("xnT_o", [D, TS], BF16, kind="ExternalOutput")
    ckv_o = nc.dram_tensor("ckv_o", [KVL + 1, TS], F32, kind="ExternalOutput")
    w2s_o = nc.dram_tensor("w2s_o", [128, TS], F32, kind="ExternalOutput")
    c0_o = nc.dram_tensor("c0_o", [1, TS], F32, kind="ExternalOutput")

    xnT = nc.alloc_sbuf_tensor("xnT", [128, NDT * TS], BF16)
    wkv_s = nc.alloc_sbuf_tensor("wkv_s", [128, NDT * KVL], BF16)
    bkvc_s = nc.alloc_sbuf_tensor("bkvc_s", [KVL, 1], F32)
    ones_s = nc.alloc_sbuf_tensor("ones_s", [1, TS], BF16)
    id_b = nc.alloc_sbuf_tensor("id_b", [128, 128], BF16)
    id_f = nc.alloc_sbuf_tensor("id_f", [128, 128], F32)
    cols = nc.alloc_sbuf_tensor("cols", [128, 8], F32)
    x_t = [nc.alloc_sbuf_tensor(f"x_t{i}", [128, D], F32) for i in range(2)]
    yt_t = nc.alloc_sbuf_tensor("yt_t", [128, D], BF16)
    mdt_t = nc.alloc_sbuf_tensor("mdt_t", [128, D], BF16)
    xsq = nc.alloc_sbuf_tensor("xsq", [128, D], BF16)
    xn_t = [nc.alloc_sbuf_tensor(f"xn_t{i}", [128, D], BF16) for i in range(2)]
    ckv_sb = nc.alloc_sbuf_tensor("ckv_sb", [KVL + 1, TS], F32)
    wvu_s = nc.alloc_sbuf_tensor("wvu_s", [128, NH * KVL], BF16)
    wo_t = [nc.alloc_sbuf_tensor(f"wo_t{i}", [128, TS], BF16) for i in range(2)]
    bvu_s = nc.alloc_sbuf_tensor("bvu_s", [128, NDT], BF16)
    w2row = nc.alloc_sbuf_tensor("w2row", [KVL, TS], F32)
    c0row = nc.alloc_sbuf_tensor("c0row", [1, TS], F32)
    bo_s = nc.alloc_sbuf_tensor("bo_s", [1, TS], F32)

    P0 = nc.alloc_psum_tensor("P0", [128, 1024], F32)
    P1 = nc.alloc_psum_tensor("P1", [128, 1024], F32)
    P2 = nc.alloc_psum_tensor("P2", [128, 1024], F32)
    P3 = nc.alloc_psum_tensor("P3", [128, 1024], BF16)
    P4 = nc.alloc_psum_tensor("P4", [128, 1024], BF16)
    p_tr = [P3, P4]

    s = {}
    for name in ["wld", "xld", "c1", "c2", "xn", "gtr", "gtrd", "x1st",
                 "mmk", "evk", "wol", "mmw", "evw", "st"]:
        s[name] = nc.alloc_semaphore("s_" + name)

    with nc.Block() as blk:
        @blk.sync
        def _(sync):
            sync.dma_start(bo_s[:, :], bo_d[:, :]).then_inc(s["wld"], 16)

        @blk.gpsimd
        def _(g):
            g.dma_start(wkv_s[:, :], wkv_d[:, :]).then_inc(s["wld"], 16)
            g.dma_start(bkvc_s[:, :], bkv_d[:, :]).then_inc(s["wld"], 16)
            g.dma_start(ones_s[:, :], ones_d[:, :]).then_inc(s["wld"], 16)
            g.dma_start(id_b[:, :], idm_d[:, :]).then_inc(s["wld"], 16)
            g.dma_start(id_f[:, :], idm_d[:, :]).then_inc(s["wld"], 16)
            g.dma_start(wvu_s[:, :], wvu_d[:, :]).then_inc(s["wld"], 16)
            g.dma_start(bvu_s[:, :], bvu_d[:, :]).then_inc(s["wld"], 16)

    with nc.Block() as blk:
        @blk.sync
        def _(sync):
            for i in range(NTT):
                if i >= 2:
                    sync.wait_ge(s["xn"], i - 1)
                    sync.wait_ge(s["x1st"], 16 * (i - 1))
                sync.dma_start(x_t[i % 2][:, :], x_d[i * 128:(i + 1) * 128, :]
                               ).then_inc(s["xld"], 16)
                sync.wait_ge(s["c2"], 2 * i)
                sync.dma_start(yt_t[:, :], yt_d[i * 128:(i + 1) * 128, :]
                               ).then_inc(s["xld"], 16)
                sync.dma_start(mdt_t[:, :], mdt_d[i * 128:(i + 1) * 128, :]
                               ).then_inc(s["xld"], 16)

        @blk.gpsimd
        def _(gx):
            for i in range(NTT):
                gx.wait_ge(s["xn"], i + 1)
                gx.dma_start(x1_o[i * 128:(i + 1) * 128, :], x_t[i % 2][:, :]
                             ).then_inc(s["x1st"], 16)

        @blk.vector
        def _(v):
            for i in range(NTT):
                v.wait_ge(s["xld"], 48 * (i + 1))
                xt = x_t[i % 2]
                v.tensor_add(xt[:, :], xt[:, :], yt_t[:, :])
                v.tensor_add(xt[:, :], xt[:, :], mdt_t[:, :]).then_inc(s["c2"], 1)
                v.wait_ge(s["c1"], 2 * i + 1)
                v.tensor_scalar(cols[:, 1:2], cols[:, 0:1], 1.0 / D, None, ALU.mult)
                v.tensor_scalar(cols[:, 3:4], cols[:, 2:3], 1.0 / D, None, ALU.mult)
                v.drain()
                v.tensor_mul(cols[:, 4:5], cols[:, 1:2], cols[:, 1:2])
                v.drain()
                v.tensor_sub(cols[:, 5:6], cols[:, 3:4], cols[:, 4:5])
                v.drain()
                v.tensor_scalar(cols[:, 5:6], cols[:, 5:6], 1e-5, None, ALU.add
                                ).then_inc(s["c2"], 1)
                v.wait_ge(s["c1"], 2 * i + 2)
                if i >= 1:
                    v.wait_ge(s["gtr"], i * NDT)
                v.tensor_scalar(xn_t[i % 2][:, :], xt[:, :], cols[:, 1:2],
                                cols[:, 7:8], ALU.subtract, ALU.mult
                                ).then_inc(s["xn"], 1)
                for c in range(1, NDT, 2):
                    v.wait_ge(s["gtr"], i * NDT + c + 1)
                    v.tensor_copy(xnT[:, c * TS + i * 128: c * TS + (i + 1) * 128],
                                  p_tr[c % 2][:, 0:128]).then_inc(s["gtrd"], 1)

        @blk.scalar
        def _(sc):
            for i in range(NTT):
                sc.wait_ge(s["c2"], 2 * i + 1)
                sc.activation(xsq[:, :], x_t[i % 2][:, :], AF.Copy,
                              accum_out=cols[:, 0:1])
                sc.activation(xsq[:, :], x_t[i % 2][:, :], AF.Square,
                              accum_out=cols[:, 2:3]).then_inc(s["c1"], 1)
                sc.wait_ge(s["c2"], 2 * i + 2)
                sc.activation(cols[:, 6:7], cols[:, 5:6], AF.Ln)
                sc.drain()
                sc.activation(cols[:, 7:8], cols[:, 6:7], AF.Exp, scale=-0.5
                              ).then_inc(s["c1"], 1)
                for c in range(0, NDT, 2):
                    sc.wait_ge(s["gtr"], i * NDT + c + 1)
                    sc.copy(xnT[:, c * TS + i * 128: c * TS + (i + 1) * 128],
                            p_tr[c % 2][:, 0:128]).then_inc(s["gtrd"], 1)

        @blk.tensor
        def _(t):
            for i in range(NTT):
                t.wait_ge(s["xn"], i + 1)
                for c in range(NDT):
                    idx = i * NDT + c
                    if idx >= 2:
                        t.wait_ge(s["gtrd"], idx - 1)
                    t.transpose(p_tr[c % 2][:, 0:128],
                                xn_t[i % 2][:, c * 128:(c + 1) * 128],
                                id_b[:, :]).then_inc(s["gtr"], 1)

    if upto <= 1:
        return nc
    with nc.Block() as blk:
        @blk.sync
        def _(sync):
            dst = bass.AP(xnT_o, 0, [[TS, 128], [128 * TS, NDT], [1, TS]])
            src = xnT[:, :].rearrange("p (c t) -> p c t", c=NDT)
            sync.dma_start(dst, src).then_inc(s["st"], 16)

        @blk.tensor
        def _(t):
            for c in range(NDT):
                lhs = wkv_s[:, c * KVL:(c + 1) * KVL]
                for hf in range(2):
                    mm = t.matmul(P0[0:KVL, hf * 512:(hf + 1) * 512], lhs,
                                  xnT[:, c * TS + hf * 512:c * TS + (hf + 1) * 512],
                                  start=(c == 0), stop=(c == NDT - 1))
                    if c == NDT - 1 and hf == 1:
                        mm.then_inc(s["mmk"], 1)

        @blk.vector
        def _(v):
            v.wait_ge(s["mmk"], 1)
            v.tensor_scalar(ckv_sb[0:KVL, :], P0[0:KVL, :], bkvc_s[:, :], None,
                            ALU.add).then_inc(s["evk"], 1)

        @blk.gpsimd
        def _(g):
            g.wait_ge(s["evk"], 1)
            g.dma_start(ckv_sb[KVL:KVL + 1, :], ones_d[:, :]).then_inc(s["evk"], 16)


        @blk.sync
        def _(sync2):
            pass

    if upto <= 2:
        return nc
    with nc.Block() as blk:
        @blk.gpsimd
        def _(g):
            for c in range(NDT):
                if c >= 2:
                    g.wait_ge(s["mmw"], c - 1)
                g.dma_start(wo_t[c % 2][:, :], wocol_d[c * 128:(c + 1) * 128, :]
                            ).then_inc(s["wol"], 16)

        @blk.sync
        def _(sync):
            sync.wait_ge(s["evk"], 17)
            sync.dma_start(ckv_o[:, :], ckv_sb[:, :]).then_inc(s["st"], 16)

        @blk.tensor
        def _(t):
            for c in range(NDT):
                t.wait_ge(s["wol"], 16 * (c + 1))
                if c >= 2:
                    t.wait_ge(s["evw"], c - 1)
                PW = P1 if c % 2 == 0 else P2
                for hf in range(2):
                    t.matmul(PW[0:KVL, hf * 512:(hf + 1) * 512],
                             wvu_s[:, c * KVL:(c + 1) * KVL],
                             wo_t[c % 2][:, hf * 512:(hf + 1) * 512],
                             start=True, stop=True)
                for hf in range(2):
                    mm = t.matmul(P0[0:1, hf * 512:(hf + 1) * 512],
                                  bvu_s[:, c:c + 1],
                                  wo_t[c % 2][:, hf * 512:(hf + 1) * 512],
                                  start=(c == 0), stop=(c == NDT - 1))
                    if hf == 1:
                        mm.then_inc(s["mmw"], 1)

        @blk.vector
        def _(v):
            for c in range(NDT):
                v.wait_ge(s["mmw"], c + 1)
                v.tensor_copy(w2row[:, :], (P1 if c % 2 == 0 else P2)[0:KVL, :]
                              ).then_inc(s["evw"], 1)
                v.wait_ge(s["st"], 48 + 16 * c)
            v.tensor_add(c0row[:, :], P0[0:1, :], bo_s[:, :]).then_inc(s["evw"], 1)

        @blk.sync
        def _(sync2):
            for c in range(NDT):
                sync2.wait_ge(s["evw"], c + 1)
                sync2.dma_start(w2s_o[c * KVL:(c + 1) * KVL, :], w2row[:, :]
                                ).then_inc(s["st"], 16)
            sync2.wait_ge(s["evw"], NDT + 1)
            sync2.dma_start(c0_o[:, :], c0row[:, :]).then_inc(s["st"], 16)

    with nc.Block() as blk:
        @blk.sync
        def _(sync):
            sync.wait_ge(s["st"], 16 * (1 + 1 + NDT + 1))
            sync.wait_ge(s["x1st"], 16 * NTT)

    return nc


def build_L4(upto=99, dbg=False):
    nc = bass.Bass("TRN2", target_bir_lowering=False)
    x1_d = nc.dram_tensor("x1", [TS, D], F32, kind="ExternalInput")
    xnT_d = nc.dram_tensor("xnT", [D, TS], BF16, kind="ExternalInput")
    ckvT_d = nc.dram_tensor("ckvT", [KVL + 1, LTOT], F32, kind="ExternalInput")
    ckvtm_d = nc.dram_tensor("ckvtm", [128, 32 * (KVL + 1)], F32, kind="ExternalInput")
    w2_d = nc.dram_tensor("w2", [128, D], F32, kind="ExternalInput")
    c0p_d = nc.dram_tensor("c0p", [128, NDT], F32, kind="ExternalInput")
    l1a_d = nc.dram_tensor("l1a", [128, NDT * 128], F32, kind="ExternalInput")
    l1b_d = nc.dram_tensor("l1b", [128, D], F32, kind="ExternalInput")
    wqr_d = nc.dram_tensor("wqr", [128, NDT * 40], F32, kind="ExternalInput")
    wqka_d = nc.dram_tensor("wqka", [QL + 1, 128], F32, kind="ExternalInput")
    bq_d = nc.dram_tensor("bq", [QL, 1], F32, kind="ExternalInput")
    br_d = nc.dram_tensor("br", [NEXP, 1], F32, kind="ExternalInput")
    ones_d = nc.dram_tensor("ones", [1, TS], F32, kind="ExternalInput")
    idm_d = nc.dram_tensor("idm", [128, 128], F32, kind="ExternalInput")
    bd8_d = nc.dram_tensor("bd8", [NEXP, 128], F32, kind="ExternalInput")
    bdq_d = nc.dram_tensor("bdq", [NH, 128], F32, kind="ExternalInput")
    rep4_d = nc.dram_tensor("rep4", [KVL, 128], F32, kind="ExternalInput")
    bd32_d = nc.dram_tensor("bd32", [128, NH], F32, kind="ExternalInput")
    out_o = nc.dram_tensor("out_o", [TS, D], F32, kind="ExternalOutput")
    q_dram = nc.dram_tensor("q_scr", [128, TS], BF16, kind="Internal")
    m_dram = nc.dram_tensor("m_scr", [NH, TS], BF16, kind="Internal")
    if dbg:
        dbg_o = {}
        for nm, shp, dt in [("lowT", [128, TS], BF16), ("cqT", [QL + 1, TS], BF16),
                            ("routT", [NEXP, TS], F32), ("scoresT", [NEXP, TS], BF16),
                            ("lowsc", [128, TS], BF16), ("qT05", [128, TS], BF16),
                            ("mneg", [NH, TS], F32), ("urawT", [128, TS], BF16),
                            ("dnT", [NH, TS], F32), ("unT", [128, TS], BF16)]:
            dbg_o[nm] = nc.dram_tensor("dbg_" + nm, shp, dt, kind="ExternalOutput")
        dbg_o["big"] = nc.dram_tensor("dbg_big", [128, NDT * TS], BF16, kind="ExternalOutput")
        dbg_o["l1a"] = nc.dram_tensor("dbg_l1a", [128, NDT * 128], BF16, kind="ExternalOutput")
        dbg_o["wqka"] = nc.dram_tensor("dbg_wqka", [QL + 1, 128], BF16, kind="ExternalOutput")
        dbg_o["qph"] = nc.dram_tensor("dbg_qph", [5, TS], BF16, kind="ExternalOutput")
        dbg_o["eb"] = nc.dram_tensor("dbg_eb", [128, 2 * TS], BF16, kind="ExternalOutput")
        dbg_o["cktm"] = nc.dram_tensor("dbg_cktm", [128, 32 * 5], BF16, kind="ExternalOutput")
        dbg_o["wqr"] = nc.dram_tensor("dbg_wqr", [128, NDT * 40], BF16, kind="ExternalOutput")
    u_dram = nc.dram_tensor("u_scr", [128, TS], F32, kind="Internal")
    d_dram = nc.dram_tensor("d_scr", [NH, TS], F32, kind="Internal")

    bigbuf = nc.alloc_sbuf_tensor("bigbuf", [128, NDT * TS], BF16)
    ckvT_s = nc.alloc_sbuf_tensor("ckvT_s", [KVL + 1, LTOT], BF16)
    ckvtm_s = nc.alloc_sbuf_tensor("ckvtm_s", [128, 32 * (KVL + 1)], BF16)
    w2_s = nc.alloc_sbuf_tensor("w2_s", [128, D], BF16)
    c0p_s = nc.alloc_sbuf_tensor("c0p_s", [128, NDT], F32)
    l1a_s = nc.alloc_sbuf_tensor("l1a_s", [128, NDT * 128], BF16)
    l1b_s = nc.alloc_sbuf_tensor("l1b_s", [128, D], BF16)
    wqr_s = nc.alloc_sbuf_tensor("wqr_s", [128, NDT * 40], BF16)
    wqka_s = nc.alloc_sbuf_tensor("wqka_s", [QL + 1, 128], BF16)
    bq_s = nc.alloc_sbuf_tensor("bq_s", [QL, 1], F32)
    br_s = nc.alloc_sbuf_tensor("br_s", [NEXP, 1], F32)
    ones_s = nc.alloc_sbuf_tensor("ones_s", [1, TS], BF16)
    id_f = nc.alloc_sbuf_tensor("id_f", [128, 128], F32)
    id_b = nc.alloc_sbuf_tensor("id_b", [128, 128], BF16)
    bd8_s = nc.alloc_sbuf_tensor("bd8_s", [NEXP, 128], BF16)
    bdq_s = nc.alloc_sbuf_tensor("bdq_s", [NH, 128], BF16)
    rep4_s = nc.alloc_sbuf_tensor("rep4_s", [KVL, 128], BF16)
    bd32_s = nc.alloc_sbuf_tensor("bd32_s", [128, NH], BF16)
    lowT = nc.alloc_sbuf_tensor("lowT", [128, TS], BF16)
    lowsc = nc.alloc_sbuf_tensor("lowsc", [128, TS], BF16)
    cqT = nc.alloc_sbuf_tensor("cqT", [QL + 1, TS], BF16)
    routT = nc.alloc_sbuf_tensor("routT", [NEXP, TS], F32)
    g_tm = nc.alloc_sbuf_tensor("g_tm", [128, NEXP], F32)
    ge_tm = nc.alloc_sbuf_tensor("ge_tm", [128, NEXP], F32)
    gcol = nc.alloc_sbuf_tensor("gcol", [128, 4], F32)
    scoresT = nc.alloc_sbuf_tensor("scoresT", [NEXP, TS], BF16)
    kc8 = nc.alloc_sbuf_tensor("kc8", [KVL, 8], F32)
    kabs_f = nc.alloc_sbuf_tensor("kabs_f", [KVL, 1], F32)
    kabs_b = nc.alloc_sbuf_tensor("kabs_b", [KVL, 1], BF16)
    kabscol = nc.alloc_sbuf_tensor("kabscol", [128, 1], F32)
    kabsbd = nc.alloc_sbuf_tensor("kabsbd", [128, NH], BF16)
    qph = [nc.alloc_sbuf_tensor(f"qph{i}", [5, TS], BF16) for i in range(2)]
    expb = [nc.alloc_sbuf_tensor("expb0", [128, 2 * TS], BF16)]
    u5 = [nc.alloc_sbuf_tensor(f"u5_{i}", [5, TS], F32) for i in range(2)]
    urawT = nc.alloc_sbuf_tensor("urawT", [128, TS], BF16)
    dnT = nc.alloc_sbuf_tensor("dnT", [NH, TS], F32)
    rc32 = nc.alloc_sbuf_tensor("rc32", [NH, TS], BF16)
    mneg2 = nc.alloc_sbuf_tensor("mneg2", [NH, TS], F32)
    unT = nc.alloc_sbuf_tensor("unT", [128, TS], BF16)
    x1t = nc.alloc_sbuf_tensor("x1t", [128, D], BF16)
    outt = nc.alloc_sbuf_tensor("outt", [128, D // 2], F32)

    G0 = nc.alloc_psum_tensor("G0", [128, 1024], F32)
    G1 = nc.alloc_psum_tensor("G1", [128, 1024], F32)

    s = {}
    for name in ["wld", "mm1", "ev1", "tr", "g", "mm2", "ev2", "mm3", "ev3",
                 "qst", "qld", "smm", "sev", "sact", "su", "sue", "ust",
                 "uld", "mm6", "ev6", "x1ld", "otr", "otrd", "ost", "f", "dbgf",
                 "sac5"]:
        s[name] = nc.alloc_semaphore("s_" + name)

    with nc.Block() as blk:
        @blk.sync
        def _(sync):
            src = bass.AP(xnT_d, 0, [[TS, 128], [128 * TS, NDT], [1, TS]])
            dst = bigbuf[:, :].rearrange("p (c t) -> p c t", c=NDT)
            sync.dma_start(dst, src).then_inc(s["wld"], 16)
            sync.dma_start(c0p_s[:, :], c0p_d[:, :]).then_inc(s["wld"], 16)
            sync.dma_start(bq_s[:, :], bq_d[:, :]).then_inc(s["wld"], 16)
            sync.dma_start(br_s[:, :], br_d[:, :]).then_inc(s["wld"], 16)
            sync.dma_start(id_f[:, :], idm_d[:, :]).then_inc(s["wld"], 16)

        @blk.gpsimd
        def _(g):
            for dst, srcd in ((ckvT_s, ckvT_d), (ckvtm_s, ckvtm_d), (w2_s, w2_d),
                              (l1a_s, l1a_d), (l1b_s, l1b_d), (wqr_s, wqr_d),
                              (wqka_s, wqka_d), (ones_s, ones_d), (id_b, idm_d),
                              (bd8_s, bd8_d), (bdq_s, bdq_d), (rep4_s, rep4_d),
                              (bd32_s, bd32_d)):
                g.dma_start(dst[:, :], srcd[:, :]).then_inc(s["wld"], 16)

    if upto <= 0:
        return nc
    if dbg:
        with nc.Block() as blk:
            @blk.sync
            def _(sync):
                sync.wait_ge(s["wld"], 16 * 18)
                sync.dma_start(dbg_o["big"][:, :], bigbuf[:, :]).then_inc(s["dbgf"], 16)
                sync.dma_start(dbg_o["l1a"][:, :], l1a_s[:, :]).then_inc(s["dbgf"], 16)
                sync.dma_start(dbg_o["wqr"][:, :], wqr_s[:, :]).then_inc(s["dbgf"], 16)
                sync.wait_ge(s["dbgf"], 48)
    with nc.Block() as blk:
        @blk.tensor
        def _(t):
            t.wait_ge(s["wld"], 16 * 18)
            for c in range(NDT):
                st_, sp = (c == 0), (c == NDT - 1)
                rhs = bigbuf[:, c * TS:(c + 1) * TS]
                for hf in range(2):
                    r = rhs[:, hf * 512:(hf + 1) * 512]
                    t.matmul(G0[:, hf * 512:(hf + 1) * 512],
                             l1a_s[:, c * 128:(c + 1) * 128], r, start=st_, stop=sp)
                    mm = t.matmul(G1[0:40, hf * 512:(hf + 1) * 512],
                                  wqr_s[:, c * 40:(c + 1) * 40], r, start=st_, stop=sp)
                    if sp and hf == 1:
                        mm.then_inc(s["mm1"], 1)

        @blk.vector
        def _(v):
            v.wait_ge(s["mm1"], 1)
            v.tensor_copy(lowT[:, :], G0[:, :])
            v.tensor_scalar(cqT[0:QL, :], G1[0:QL, :], bq_s[:, :], None, ALU.add)
            v.tensor_scalar(routT[:, :], G1[32:40, :], br_s[:, :], None, ALU.add
                            ).then_inc(s["ev1"], 1)

        @blk.gpsimd
        def _(g):
            g.wait_ge(s["ev1"], 1)
            g.dma_start(cqT[QL:QL + 1, :], ones_d[:, :]).then_inc(s["ev1"], 16)

    if dbg:
        with nc.Block() as blk:
            @blk.sync
            def _(sync):
                sync.dma_start(dbg_o["lowT"][:, :], lowT[:, :]).then_inc(s["dbgf"], 16)
                sync.dma_start(dbg_o["cqT"][:, :], cqT[:, :]).then_inc(s["dbgf"], 16)
                sync.wait_ge(s["dbgf"], 80)
    if upto <= 1:
        return nc
    with nc.Block() as blk:
        @blk.tensor
        def _(t):
            for j in range(NTT):
                if j >= 2:
                    t.wait_ge(s["g"], 6 * (j - 1))
                bank = G0[:, 0:512] if j % 2 == 0 else G0[:, 512:1024]
                t.transpose(bank[:, 0:8], routT[:, j * 128:(j + 1) * 128],
                            id_f[0:NEXP, 0:NEXP]).then_inc(s["tr"], 1)
                t.wait_ge(s["g"], 6 * j + 5)
                t.transpose(bank[0:8, 256:384], g_tm[:, :], id_f[:, :]
                            ).then_inc(s["tr"], 1)
            t.wait_ge(s["g"], 6 * NTT)
            for hf in range(2):
                mm = t.matmul(G1[:, hf * 512:(hf + 1) * 512], bd8_s[:, :],
                              scoresT[:, hf * 512:(hf + 1) * 512],
                              start=True, stop=True)
                if hf == 1:
                    mm.then_inc(s["mm2"], 1)

        @blk.vector
        def _(v):
            for j in range(NTT):
                bank = G0[:, 0:512] if j % 2 == 0 else G0[:, 512:1024]
                v.wait_ge(s["tr"], 2 * j + 1)
                v.tensor_copy(g_tm[:, :], bank[:, 0:8])
                v.drain()
                v.reduce_max(gcol[:, 0:1], g_tm[:, :], axis=AX.X, negate=True
                             ).then_inc(s["g"], 1)
                v.wait_ge(s["g"], 6 * j + 2)
                v.reduce_sum(gcol[:, 1:2], ge_tm[:, :], axis=AX.X)
                v.drain()
                v.reciprocal(gcol[:, 2:3], gcol[:, 1:2])
                v.drain()
                v.tensor_scalar(gcol[:, 2:3], gcol[:, 2:3], 16.0, None, ALU.mult
                                ).then_inc(s["g"], 1)
                v.wait_ge(s["g"], 6 * j + 3)
                v.tensor_scalar(g_tm[:, :], ge_tm[:, :], gcol[:, 2:3], None,
                                ALU.mult).then_inc(s["g"], 3)
                v.wait_ge(s["tr"], 2 * j + 2)
                v.tensor_copy(scoresT[:, j * 128:(j + 1) * 128], bank[0:8, 256:384])
                v.drain()
                v.sem_inc(s["f"], 1)
            v.wait_ge(s["mm2"], 1)
            v.tensor_mul(lowsc[:, :], lowT[:, :], G1[:, :]).then_inc(s["ev2"], 1)

        @blk.scalar
        def _(sc):
            for j in range(NTT):
                sc.wait_ge(s["g"], 6 * j + 1)
                sc.activation(ge_tm[:, :], g_tm[:, :], AF.Exp, bias=gcol[:, 0:1]
                              ).then_inc(s["g"], 1)

    if dbg and int(dbg) >= 2:
        with nc.Block() as blk:
            @blk.sync
            def _(sync):
                sync.dma_start(dbg_o["routT"][:, :], routT[:, :]).then_inc(s["dbgf"], 16)
                sync.dma_start(dbg_o["scoresT"][:, :], scoresT[:, :]).then_inc(s["dbgf"], 16)
                sync.dma_start(dbg_o["lowsc"][:, :], lowsc[:, :]).then_inc(s["dbgf"], 16)
                sync.wait_ge(s["dbgf"], 128)
    if upto <= 2:
        return nc
    stk3 = contextlib.ExitStack()
    qT05 = stk3.enter_context(nc.sbuf_tensor("qT05", [128, TS], BF16))
    absq = stk3.enter_context(nc.sbuf_tensor("absq", [128, TS], BF16))
    mneg = stk3.enter_context(nc.sbuf_tensor("mneg", [NH, TS], F32))
    with nc.Block() as blk:
        @blk.tensor
        def _(t):
            for hf in range(2):
                mm = t.matmul(G0[:, hf * 512:(hf + 1) * 512], wqka_s[:, :],
                              cqT[:, hf * 512:(hf + 1) * 512], start=True, stop=True)
                if hf == 1:
                    mm.then_inc(s["mm3"], 1)
            t.wait_ge(s["ev3"], 2)
            t.matmul(G1[:, 0:1], rep4_s[:, :], kabs_b[:, :], start=True, stop=True
                     ).then_inc(s["mm3"], 1)
            t.wait_ge(s["ev3"], 3)
            t.wait_ge(s["sact"], 1)
            for hf in range(2):
                mm = t.matmul(G0[0:NH, hf * 512:(hf + 1) * 512], kabsbd[:, :],
                              absq[:, hf * 512:(hf + 1) * 512], start=True, stop=True)
                if hf == 1:
                    mm.then_inc(s["mm3"], 1)

        @blk.vector
        def _(v):
            v.wait_ge(s["mm3"], 1)
            v.tensor_scalar(qT05[:, :], G0[:, :], 0.5, None, ALU.mult
                            ).then_inc(s["ev3"], 1)
            for k in range(8):
                v.tensor_reduce(kc8[:, k:k + 1], ckvT_s[0:KVL, k * 512:(k + 1) * 512],
                                AX.X, ALU.max, apply_absolute_value=True)
            v.drain()
            v.reduce_max(kabs_f[:, :], kc8[:, :], axis=AX.X)
            v.drain()
            v.tensor_copy(kabs_b[:, :], kabs_f[:, :])
            v.drain()
            v.sem_inc(s["ev3"], 1)
            v.wait_ge(s["mm3"], 2)
            v.tensor_copy(kabscol[:, :], G1[:, 0:1])
            v.drain()
            v.tensor_scalar(kabsbd[:, :], bd32_s[:, :], kabscol[:, :], None, ALU.mult)
            v.drain()
            v.sem_inc(s["ev3"], 1)
            v.wait_ge(s["mm3"], 3)
            v.tensor_scalar(mneg[:, :], G0[0:NH, :], -1.0, None, ALU.mult
                            ).then_inc(s["ev3"], 1)

        @blk.scalar
        def _(sc):
            sc.wait_ge(s["mm3"], 1)
            sc.activation(absq[:, :], G0[:, :], AF.Abs, scale=0.5).then_inc(s["sact"], 1)

        @blk.gpsimd
        def _(g):
            g.wait_ge(s["ev3"], 1)
            if dbg:
                g.dma_start(dbg_o["qT05"][:, :], qT05[:, :]).then_inc(s["dbgf"], 16)
                g.dma_start(dbg_o["wqka"][:, :], wqka_s[:, :]).then_inc(s["dbgf"], 16)
                g.wait_ge(s["dbgf"], 128 + 32)
            g.dma_start(q_dram[:, :], qT05[:, :]).then_inc(s["qst"], 16)
            g.wait_ge(s["ev3"], 4)
            if dbg:
                g.dma_start(dbg_o["mneg"][:, :], mneg[:, :]).then_inc(s["dbgf"], 16)
            g.dma_start(m_dram[:, :], mneg[:, :]).then_inc(s["qst"], 16)
            g.wait_ge(s["qst"], 32)

    stk3.close()
    if upto <= 3:
        return nc
    with contextlib.ExitStack() as stk:
        G2 = stk.enter_context(nc.psum_tensor("G2", [128, 1024], F32))
        QU = stk.enter_context(nc.psum_tensor("QU", [5, 1024], F32))
        W3 = [G0, G1, G2]
        expb2 = expb[0]

        with nc.Block() as blk:
            @blk.gpsimd
            def _(g):
                for h in range(NH):
                    if h >= 2:
                        g.wait_ge(s["smm"], 64 * (h - 1))
                    g.dma_start(qph[h % 2][0:4, :], q_dram[4 * h:4 * h + 4, :]
                                ).then_inc(s["qld"], 16)
                    g.dma_start(qph[h % 2][4:5, :], m_dram[h:h + 1, :]
                                ).then_inc(s["qld"], 16)
                    if dbg and h == 0:
                        g.wait_ge(s["qld"], 32)
                        g.dma_start(dbg_o["qph"][:, :], qph[0][:, :]).then_inc(s["dbgf"], 16)
                    if dbg and h == 1:
                        g.dma_start(dbg_o["cktm"][:, :], ckvtm_s[:, :]).then_inc(s["dbgf"], 16)

            @blk.tensor
            def _(t):
                def umm(itp):
                    if itp % 32 == 0 and itp >= 32:
                        t.wait_ge(s["sue"], itp // 32)
                    t.wait_ge(s["sac5"], itp + 2)
                    sl = itp % 2
                    for hf in range(2):
                        mm = t.matmul(
                            QU[:, hf * 512:(hf + 1) * 512],
                            ckvtm_s[:, (itp % 32) * 5:((itp % 32) + 1) * 5],
                            expb2[:, sl * TS + hf * 512:sl * TS + (hf + 1) * 512],
                            start=(itp % 32 == 0), stop=(itp % 32 == 31))
                        if hf == 1:
                            mm.then_inc(s["su"], 1)

                for h in range(NH):
                    t.wait_ge(s["qld"], 32 * (h + 1))
                    for kt in range(NST * 2):
                        it = h * 32 + kt
                        if it >= 3:
                            t.wait_ge(s["sac5"], it - 1)
                        for hf in range(2):
                            mm = t.matmul(
                                W3[it % 3][:, hf * 512:(hf + 1) * 512],
                                ckvT_s[:, kt * 128:(kt + 1) * 128],
                                qph[h % 2][:, hf * 512:(hf + 1) * 512],
                                start=True, stop=True)
                            mm.then_inc(s["smm"], 1)
                        if it >= 2 and it % 2 == 0:
                            for itp in range(it - 2, it):
                                umm(itp)
                for itp in range(NH * 32 - 2, NH * 32):
                    umm(itp)

            @blk.scalar
            def _(sc):
                sc.sem_inc(s["sac5"], 1)
                for it in range(NH * 32):
                    sc.wait_ge(s["smm"], 2 * (it + 1))
                    if it >= 2:
                        sc.wait_ge(s["su"], it - 1)
                    sc.activation(expb2[:, (it % 2) * TS:(it % 2 + 1) * TS],
                                  W3[it % 3][:, :], AF.Exp)
                    sc.drain()
                    sc.sem_inc(s["sac5"], 1)
                    if dbg and it == 1:
                        sc.dma_start(dbg_o["eb"][:, :], expb2[:, :]).then_inc(s["dbgf"], 16)

            @blk.vector
            def _(v):
                for h in range(NH):
                    if h >= 2:
                        v.wait_ge(s["ust"], 32 * (h - 1))
                    v.wait_ge(s["su"], 32 * (h + 1))
                    v.tensor_copy(u5[h % 2][:, :], QU[:, :])
                    v.drain()
                    v.sem_inc(s["sue"], 1)

            @blk.sync
            def _(sync):
                for h in range(NH):
                    sync.wait_ge(s["sue"], h + 1)
                    sync.dma_start(u_dram[h * 4:(h + 1) * 4, :], u5[h % 2][0:4, :]
                                   ).then_inc(s["ust"], 16)
                    sync.dma_start(d_dram[h:h + 1, :], u5[h % 2][4:5, :]
                                   ).then_inc(s["ust"], 16)

    if upto <= 4:
        return nc
    GS = [G0, G1]
    with nc.Block() as blk:
        @blk.gpsimd
        def _(g):
            g.wait_ge(s["ust"], 32 * NH)
            g.dma_start(urawT[:, :], u_dram[:, :]).then_inc(s["uld"], 16)
            g.dma_start(dnT[:, :], d_dram[:, :]).then_inc(s["uld"], 16)

        @blk.vector
        def _(v):
            v.wait_ge(s["uld"], 32)
            v.reciprocal(mneg2[:, :], dnT[:, :])
            v.drain()
            v.tensor_copy(rc32[:, :], mneg2[:, :])
            v.drain()
            v.sem_inc(s["ev2"], 1)
            v.wait_ge(s["mm2"], 2)
            v.tensor_mul(unT[:, :], urawT[:, :], G0[:, :]).then_inc(s["ev2"], 1)

        @blk.tensor
        def _(t):
            t.wait_ge(s["ev2"], 2)
            for hf in range(2):
                mm = t.matmul(G0[:, hf * 512:(hf + 1) * 512], bdq_s[:, :],
                              rc32[:, hf * 512:(hf + 1) * 512], start=True, stop=True)
                if hf == 1:
                    mm.then_inc(s["mm2"], 1)

    if dbg and int(dbg) >= 3:
        with nc.Block() as blk:
            @blk.sync
            def _(sync):
                for nm, t_ in [("urawT", urawT), ("dnT", dnT), ("unT", unT)]:
                    sync.dma_start(dbg_o[nm][:, :], t_[:, :]).then_inc(s["dbgf"], 16)
                sync.wait_ge(s["dbgf"], 224 if int(dbg) >= 2 else 128)
    if upto <= 5:
        return nc
    with nc.Block() as blk:
        @blk.tensor
        def _(t):
            for c in range(NDT):
                if c >= 2:
                    t.wait_ge(s["ev6"], c - 1)
                for hf in range(2):
                    t.matmul(GS[c % 2][:, hf * 512:(hf + 1) * 512],
                             w2_s[:, c * 128:(c + 1) * 128],
                             unT[:, hf * 512:(hf + 1) * 512], start=True, stop=False)
                    mm = t.matmul(GS[c % 2][:, hf * 512:(hf + 1) * 512],
                                  l1b_s[:, c * 128:(c + 1) * 128],
                                  lowsc[:, hf * 512:(hf + 1) * 512],
                                  start=False, stop=True)
                    if hf == 1:
                        mm.then_inc(s["mm6"], 1)

        @blk.vector
        def _(v):
            for c in range(NDT):
                v.wait_ge(s["mm6"], c + 1)
                v.tensor_scalar(bigbuf[:, c * TS:(c + 1) * TS], GS[c % 2][:, :],
                                c0p_s[:, c:c + 1], None, ALU.add).then_inc(s["ev6"], 1)

    if upto <= 6:
        return nc
    if dbg and int(dbg) >= 3:
        with nc.Block() as blk:
            @blk.sync
            def _(sync):
                sync.dma_start(dbg_o["big"][:, :], bigbuf[:, :]).then_inc(s["dbgf"], 16)
                sync.wait_ge(s["dbgf"], 16 * 17)
    with contextlib.ExitStack() as stk:
        PT = [stk.enter_context(nc.psum_tensor(f"PT{i}", [128, 1024], BF16))
              for i in range(2)]

        with nc.Block() as blk:
            @blk.sync
            def _(sync):
                for i in range(NTT):
                    for half in range(2):
                        sync.wait_ge(s["otrd"], NDT * i + 16 * (half + 1))
                        sync.dma_start(
                            out_o[i * 128:(i + 1) * 128,
                                  half * 2048:(half + 1) * 2048],
                            outt[:, :]).then_inc(s["ost"], 16)

            @blk.gpsimd
            def _(g):
                for i in range(NTT):
                    if i >= 1:
                        g.wait_ge(s["otrd"], NDT * i)
                    g.dma_start(x1t[:, :], x1_d[i * 128:(i + 1) * 128, :]
                                ).then_inc(s["x1ld"], 16)

            @blk.tensor
            def _(t):
                for i in range(NTT):
                    for c in range(NDT):
                        idx = i * NDT + c
                        if idx >= 2:
                            t.wait_ge(s["otrd"], idx - 1)
                        t.transpose(PT[c % 2][:, 0:128],
                                    bigbuf[:, c * TS + i * 128:c * TS + (i + 1) * 128],
                                    id_b[:, :]).then_inc(s["otr"], 1)

            @blk.vector
            def _(v):
                for i in range(NTT):
                    v.wait_ge(s["x1ld"], 16 * (i + 1))
                    for c in range(NDT):
                        idx = i * NDT + c
                        if c % 16 == 0:
                            prev = 2 * i + c // 16 - 1
                            if prev >= 0:
                                v.wait_ge(s["ost"], 16 * (prev + 1))
                        v.wait_ge(s["otr"], idx + 1)
                        v.tensor_add(outt[:, (c % 16) * 128:(c % 16 + 1) * 128],
                                     x1t[:, c * 128:(c + 1) * 128],
                                     PT[c % 2][:, 0:128]).then_inc(s["otrd"], 1)

    with nc.Block() as blk:
        @blk.sync
        def _(sync):
            sync.wait_ge(s["ost"], 16 * NTT)

    return nc
